# version 06
# speedup vs baseline: 1.1101x; 1.0036x over previous
import numpy as np
import ml_dtypes
from contextlib import ExitStack
import concourse.bass as bass
import concourse.mybir as mybir

F32 = mybir.dt.float32
BF16 = mybir.dt.bfloat16
I32 = mybir.dt.int32
U8 = mybir.dt.uint8
FP8 = mybir.dt.float8e4
Alu = mybir.AluOpType
Act = mybir.ActivationFunctionType
AX = mybir.AxisListType
BF16NP = ml_dtypes.bfloat16

T = 2048
D = 1024
NCH = 8
NTT = 16

COMPUTE = ("tensor", "vector", "scalar", "gpsimd")
DMA_Q = {"sync": "q_sync", "gpsimd": "q_gps", "scalar": "q_act", "vector": "q_dve", "tensor": "q_pe"}


class Prog:
    def __init__(self, nc):
        self.nc = nc
        self.seq = {e: [] for e in ("tensor", "vector", "scalar", "gpsimd", "sync")}
        self.cnt = {e: 0 for e in COMPUTE}
        for q in DMA_Q.values():
            self.cnt[q] = 0
        self.state = {}
        self.dep_targets = set()
        self.final_waits = set()
        self.tok_tag = {}

    def _read_deps(self, keys):
        d = set()
        for k in keys:
            st = self.state.get(k)
            if st and st[0] is not None:
                d.add(st[0])
        return d

    def _write_deps(self, keys):
        d = set()
        for k in keys:
            st = self.state.get(k)
            if st:
                if st[0] is not None:
                    d.add(st[0])
                d.update(st[1])
        return d

    def op(self, eng, fn, r=(), w=(), dma=False, tag=""):
        deps = self._read_deps(r) | self._write_deps(w)
        if dma:
            q = DMA_Q[eng]
            token = (q, self.cnt[q])
            self.cnt[q] += 1
        else:
            token = (eng, self.cnt[eng])
            self.cnt[eng] += 1
        eff = set(deps)
        for t in eff:
            if t[0] in COMPUTE:
                self.dep_targets.add(t)
        for k in r:
            self.state.setdefault(k, [None, set()])[1].add(token)
        for k in w:
            self.state[k] = [token, set()]
        self.seq[eng].append((fn, eff, token, dma, tag))
        self.tok_tag[token] = f"{token[0]}:{token[1]}:{tag}"
        return token

    def finalize(self, last_engine="sync", listing=None):
        nc = self.nc
        lst = open(listing, "w") if listing else None
        for t in self.final_waits:
            if t[0] in COMPUTE:
                self.dep_targets.add(t)
        marked = {e: sorted(i for (ee, i) in self.dep_targets if ee == e) for e in COMPUTE}
        rank = {}
        for e in COMPUTE:
            for n, i in enumerate(marked[e]):
                rank[(e, i)] = n + 1
        es = ExitStack()
        sems = {}
        NQS = 16
        for e in COMPUTE:
            sems[e] = es.enter_context(nc.semaphore(f"s_{e}"))
        qsems = {}
        for q in sorted(set(DMA_Q.values())):
            qsems[q] = [es.enter_context(nc.semaphore(f"s_{q}_{i}")) for i in range(NQS)]

        def wait_val(tok):
            de, di = tok
            if de in COMPUTE:
                return sems[de], rank[tok]
            return qsems[de][di % NQS], 16 * (di // NQS + 1)

        def emit_stream(eng_obj, eng_name):
            waited = {}

            def wait(tok):
                sem, val = wait_val(tok)
                if waited.get(sem.name, 0) < val:
                    eng_obj.wait_ge(sem, val)
                    waited[sem.name] = val

            for (fn, deps, token, dma, tag) in self.seq[eng_name]:
                wl = []
                for tok in sorted(deps):
                    sem, val = wait_val(tok)
                    if waited.get(sem.name, 0) < val:
                        wl.append((sem.name, val))
                for tok in sorted(deps):
                    wait(tok)
                if dma and token[1] >= NQS:
                    wait((token[0], token[1] - NQS))
                inst = fn(eng_obj)
                if dma:
                    inst.then_inc(qsems[token[0]][token[1] % NQS], 16)
                elif token in rank:
                    inst.then_inc(sems[token[0]], 1)
                if lst:
                    inc = "DMA+16" if dma else ("+1" if token in rank else "")
                    dt_ = [self.tok_tag.get(t, t) for t in sorted(deps)]
                    lst.write(f"{eng_name} {token[1]} {tag} waits={wl} deps={dt_} {inc}\n")
            if eng_name == last_engine:
                for tok in sorted(self.final_waits):
                    wait(tok)

        with es, nc.Block() as block:
            block.sync(lambda e: emit_stream(e, "sync"))
            block.vector(lambda e: emit_stream(e, "vector"))
            block.scalar(lambda e: emit_stream(e, "scalar"))
            block.tensor(lambda e: emit_stream(e, "tensor"))
            block.gpsimd(lambda e: emit_stream(e, "gpsimd"))
        if lst:
            lst.close()
        return sum(len(s) for s in self.seq.values())


class Region:
    GR = 2048

    def __init__(self, arena, name, off, nbytes):
        self.arena = arena
        self.name = name
        self.off = off
        self.nbytes = nbytes

    def K(self, byteoff, nbytes):
        g0 = byteoff // self.GR
        g1 = (byteoff + nbytes - 1) // self.GR
        return tuple(f"{self.name}g/{g}" for g in range(g0, g1 + 1))

    def ap(self, dtype, cols, part=128, byteoff=0, p0=0):
        sz = {F32: 4, BF16: 2, I32: 4, U8: 1, FP8: 1}[dtype]
        assert byteoff >= 0 and byteoff + cols * sz <= self.nbytes, \
            (self.name, byteoff, cols, sz, self.nbytes)
        v = self.arena[p0:p0 + part, self.off + byteoff:self.off + byteoff + cols * sz]
        return v.bitcast(dtype)


class MK:
    def __init__(self):
        nc = bass.Bass("TRN2", target_bir_lowering=False, dynamic_dma_scratch_size=2048)
        self.nc = nc
        self.P = Prog(nc)
        self._declare_io()
        self._declare_consts()
        self._declare_arena()
        self._psum()
        self._load_consts()

    def _declare_io(self):
        nc = self.nc
        self.inp = {}

        def di(name, shape, dt=F32):
            self.inp[name] = nc.dram_tensor(name, shape, dt, kind="ExternalInput")

        di("x", [T, D])
        di("l0_wq", [D, D]); di("l0_wk", [D, D]); di("l0_wv", [D, D]); di("l0_wo", [D, D])
        di("l0_wa", [D, 16]); di("l0_ba", [1, 16])
        di("l0f_wg", [3072, D], I32); di("l0f_wu", [3072, D], I32); di("l0f_wd", [D, 3072], I32)
        di("l1_wkqv", [4, D, 3072]); di("l1_wm1", [4, D, 2048]); di("l1_bm1", [4, 2048])
        di("l1_wm2", [4, D, D]); di("l1_bm2", [4, D])
        di("l1_gng", [4, D]); di("l1_gnb", [4, D])
        di("l1_wf1", [4, D, 2048]); di("l1_bf1", [4, 2048])
        di("l1_wf2", [4, 2048, D]); di("l1_bf2", [4, D])
        di("l1_lng", [4, D]); di("l1_lnb", [4, D])
        di("f1_gw", [4096, D]); di("f1_uw", [4096, D]); di("f1_dw", [D, 4096])
        di("f1_sin", [1, 1]); di("f1_sind", [1, 1])
        self.y_h = nc.dram_tensor("y", [T, D], F32, kind="ExternalOutput")
        self.xspill_h = nc.dram_tensor("xspill", [NCH, 128, T], F32, kind="Internal")
        self.gwT_h = nc.dram_tensor("gwT", [D, 4096], BF16, kind="Internal")
        self.uwT_h = nc.dram_tensor("uwT", [D, 4096], BF16, kind="Internal")
        self.dwT_h = nc.dram_tensor("dwT", [4096, D], BF16, kind="Internal")

    def _declare_consts(self):
        nc = self.nc
        self.const_h = {}

        def ci(name, arr):
            self.const_h[name] = nc.inline_tensor(np.ascontiguousarray(arr), name)

        ci("id128f", np.eye(128, dtype=np.float32))
        ci("id128b", np.eye(128, dtype=np.float32).astype(BF16NP))
        ci("id16f", np.eye(16, dtype=np.float32))
        ci("id16b", np.eye(16, dtype=np.float32).astype(BF16NP))
        ci("ones_1x128f", np.ones((1, 128), np.float32))
        ci("onescol_b", np.ones((128, 1), np.float32).astype(BF16NP))
        ci("maskbig", np.where(np.arange(128)[:, None] <= np.arange(128)[None, :],
                               0.0, -1e30).astype(np.float32))
        ci("rn", (1.0 + np.sqrt(np.arange(T, dtype=np.float64))).astype(np.float32)[None, :])
        for dk in range(2):
            m = (128 * dk + np.arange(128)[:, None] <= np.arange(256)[None, :])
            ci(f"trimask{dk}", m.astype(np.float32).astype(BF16NP))

    def _declare_arena(self):
        nc = self.nc
        avail = (nc.sbuf_bytes_remaining - 512) // 64 * 64
        self.arena = nc.alloc_sbuf_tensor("arena", [128, avail], U8)
        off = [0]
        self.reg = {}

        def R(name, nbytes):
            r = Region(self.arena, name, off[0], nbytes)
            off[0] += nbytes
            assert off[0] <= avail, (name, off[0], avail)
            self.reg[name] = r
            return r

        self.S0 = R("S0", 65536)
        self.S1 = R("S1", 65536)
        self.PA = R("PA", 32768)
        self.PB = R("PB", 32768)
        self.WP = R("WP", 8192)
        self.SC = R("SC", 8192)
        self.SC.GR = 256
        self.SM = R("SM", avail - off[0])
        self._smoff = {}
        so = [0]

        def A(name, nbytes, align=4):
            o = (so[0] + align - 1) // align * align
            self._smoff[name] = o
            so[0] = o + nbytes

        A("id128f", 512); A("id128b", 256); A("id16f", 64); A("id16b", 32)
        A("ones_1x128f", 512); A("onescol_b", 2)
        A("maskbig", 512)
        A("trimask0", 512); A("trimask1", 512)
        A("rnrow", 1024)
        A("biasT", 512)
        A("bias_nat", 512)
        A("sz", 2080)
        A("szbf", 1040)
        A("szscr", 260)
        A("la_tok", 1024)
        A("wa", 512)
        A("wab", 256)
        A("ba", 64)
        A("ba2", 64)
        A("ones16", 512)
        A("ecol", 512)
        A("lrow2", 1024)
        A("srows", 256)
        A("ccols", 256)
        assert so[0] <= self.SM.nbytes, (so[0], self.SM.nbytes)
        self._sm_used = so[0]
        self._sm_alloc = A
        self._ccols = {}

    def cc(self, value):
        if value not in self._ccols:
            i = len(self._ccols)
            assert i < 64
            ap = self.smv("ccols", F32, 1, byteoff=4 * i)
            self.P.op("vector", lambda e, ap=ap, value=value: e.memset(ap, float(value)),
                      w=(f"cc/{value}",), tag=f"cc_{value}")
            self._ccols[value] = (ap, f"cc/{value}")
        return self._ccols[value]

    def smv(self, name, dtype, cols, part=128, p0=0, byteoff=0):
        return self.SM.ap(dtype, cols, part=part, p0=p0,
                          byteoff=self._smoff[name] + byteoff)

    def wp(self, dtype, cols, granule, part=128, p0=0):
        return self.WP.ap(dtype, cols, part=part, p0=p0, byteoff=granule * 1024)

    def _psum(self):
        nc = self.nc
        self.psb = [nc.alloc_psum_tensor(f"psb{i}", [128, 512], F32) for i in range(4)]
        self.ps_ab = [nc.alloc_psum_tensor(f"ps_ab{i}", [128, 512], F32) for i in range(2)]
        self.ps_st = nc.alloc_psum_tensor("ps_st", [128, 512], F32)
        self.ps_x = nc.alloc_psum_tensor("ps_x", [128, 512], F32)
        assert nc.psum_banks_remaining == 0, nc.psum_banks_remaining
        self.trbanks = [(self.ps_x, "ps_x"), (self.psb[0], "psb/0"),
                        (self.psb[1], "psb/1"), (self.psb[2], "psb/2")]

    def _load_consts(self):
        P = self.P
        for nm, cols, part, dt_ in [
            ("id128f", 128, 128, F32), ("id128b", 128, 128, BF16), ("id16f", 16, 16, F32),
            ("ones_1x128f", 128, 1, F32), ("onescol_b", 1, 128, BF16),
            ("maskbig", 128, 128, F32),
            ("trimask0", 256, 128, BF16), ("trimask1", 256, 128, BF16),
        ]:
            h = self.const_h[nm]
            P.op("sync", lambda e, nm=nm, cols=cols, part=part, dt_=dt_, h=h:
                 e.dma_start(out=self.smv(nm, dt_, cols, part=part), in_=h.ap()),
                 w=(nm,), dma=True, tag=f"ld_{nm}")

    def stage_load_x(self):
        P = self.P
        for tt in range(NTT):
            P.op("sync", lambda e, tt=tt: e.dma_start(
                out=self.S1.ap(F32, D, byteoff=tt * 4096),
                in_=self.inp["x"].ap()[tt * 128:(tt + 1) * 128, :]),
                w=self.S1.K(tt * 4096, 4096), dma=True, tag=f"ld_x{tt}")
        for tt in range(NTT):
            for c in range(NCH):
                i = tt * NCH + c
                pb, pk = self.psb[i % 2], f"psb/{i % 2}"
                P.op("tensor", lambda e, tt=tt, c=c, pb=pb: e.transpose(
                    pb[:, 0:128],
                    self.S1.ap(F32, 128, byteoff=tt * 4096 + c * 512),
                    self.smv("id128f", F32, 128)),
                    r=self.S1.K(tt * 4096 + c * 512, 512) + ("id128f",), w=(pk,), tag=f"trx{tt}_{c}")
                P.op("vector", lambda e, tt=tt, c=c, pb=pb: e.tensor_copy(
                    out=self.S0.ap(F32, 128, byteoff=c * 8192 + tt * 512), in_=pb[:, 0:128]),
                    r=(pk,), w=self.S0.K(c * 8192 + tt * 512, 512), tag=f"evx{tt}_{c}")

    def ln_fm(self, name, src_reg, src_off, dst_reg, dst_off, dst_dtype, scratch,
              gamma=None, beta=None, gb_keys=()):
        P = self.P
        sreg, boff = scratch
        xb = lambda b: sreg.ap(BF16, 512, byteoff=boff + 1024 * (b % 2))
        xs = lambda b: sreg.ap(BF16, 512, byteoff=boff + 2048 + 1024 * (b % 2))
        tmp = sreg.ap(F32, 512, byteoff=boff + 4096)
        kxb = lambda b: sreg.K(boff + 1024 * (b % 2), 1024)
        kxs = lambda b: sreg.K(boff + 2048 + 1024 * (b % 2), 1024)
        ktmp = sreg.K(boff + 4096, 2048)
        dsz = {F32: 4, BF16: 2}[dst_dtype]

        def srow(r):
            return self.SC.ap(F32, 512, part=1, p0=0, byteoff=2048 * r)

        def src_ap(c, s):
            return src_reg.ap(F32, 512, byteoff=src_off(c) + s * 2048)

        def src_k(c, s):
            return src_reg.K(src_off(c) + s * 2048, 2048)

        for s in range(4):
            for c in range(NCH):
                P.op("vector", lambda e, c=c, s=s: e.tensor_copy(
                    out=xb(c), in_=src_ap(c, s)),
                    r=src_k(c, s), w=kxb(c), tag=f"{name}_xb")
                P.op("scalar", lambda e, c=c, s=s: e.activation(
                    out=xs(c), in_=src_ap(c, s), func=Act.Square),
                    r=src_k(c, s), w=kxs(c), tag=f"{name}_xs")
                P.op("tensor", lambda e, c=c, s=s: e.matmul(
                    self.ps_st[0:1, :], self.smv("onescol_b", BF16, 1), xb(c),
                    start=(c == 0), stop=(c == NCH - 1)),
                    r=kxb(c) + ("onescol_b",), w=("ps_st",), tag=f"{name}_ms")
                P.op("tensor", lambda e, c=c, s=s: e.matmul(
                    self.ps_x[0:1, 0:512], self.smv("onescol_b", BF16, 1), xs(c),
                    start=(c == 0), stop=(c == NCH - 1)),
                    r=kxs(c) + ("onescol_b",), w=("ps_x",), tag=f"{name}_mq")
            P.op("vector", lambda e, s=s: e.tensor_scalar(
                out=srow(0), in0=self.ps_st[0:1, :], scalar1=1.0 / D, scalar2=None,
                op0=Alu.mult),
                r=("ps_st", "ps_x"), w=self.SC.K(0, 2048), tag=f"{name}_mu")
            P.op("vector", lambda e, s=s: e.tensor_tensor(
                out=srow(1), in0=srow(0), in1=srow(0), op=Alu.mult),
                r=self.SC.K(0, 2048), w=self.SC.K(2048, 2048), tag=f"{name}_musq")
            P.op("vector", lambda e, s=s: e.scalar_tensor_tensor(
                out=srow(1), in0=self.ps_x[0:1, 0:512], scalar=1.0 / D, in1=srow(1),
                op0=Alu.mult, op1=Alu.subtract),
                r=("ps_x",) + self.SC.K(2048, 2048), w=self.SC.K(2048, 2048), tag=f"{name}_var")
            epsap, epsk = self.cc(1e-5)
            P.op("scalar", lambda e, s=s, epsap=epsap: e.activation(
                out=srow(2), in_=srow(1), func=Act.Ln, bias=epsap[0:1, 0:1], scale=1.0),
                r=self.SC.K(2048, 2048) + (epsk,), w=self.SC.K(4096, 2048), tag=f"{name}_lnv")
            P.op("scalar", lambda e, s=s: e.activation(
                out=srow(2), in_=srow(2), func=Act.Exp, scale=-0.5),
                r=self.SC.K(4096, 2048), w=self.SC.K(4096, 2048), tag=f"{name}_rstd")
            P.op("vector", lambda e, s=s: e.scalar_tensor_tensor(
                out=srow(3), in0=srow(0), scalar=-1.0, in1=srow(2),
                op0=Alu.mult, op1=Alu.mult),
                r=self.SC.K(0, 2048) + self.SC.K(4096, 2048), w=self.SC.K(6144, 2048), tag=f"{name}_nmr")
            P.op("tensor", lambda e, s=s: e.matmul(
                self.ps_ab[0][:, :], self.smv("ones_1x128f", F32, 128, part=1),
                srow(2), start=True, stop=True),
                r=self.SC.K(4096, 2048) + ("ones_1x128f",), w=("ps_ab0",), tag=f"{name}_A")
            P.op("tensor", lambda e, s=s: e.matmul(
                self.ps_ab[1][:, :], self.smv("ones_1x128f", F32, 128, part=1),
                srow(3), start=True, stop=True),
                r=self.SC.K(6144, 2048) + ("ones_1x128f",), w=("ps_ab1",), tag=f"{name}_B")
            for c in range(NCH):
                dst = dst_reg.ap(dst_dtype, 512, byteoff=dst_off(c) + s * 512 * dsz)
                dk = dst_reg.K(dst_off(c) + s * 512 * dsz, 512 * dsz)
                P.op("vector", lambda e, c=c, s=s: e.tensor_tensor(
                    out=tmp, in0=src_ap(c, s), in1=self.ps_ab[0][:, :], op=Alu.mult),
                    r=src_k(c, s) + ("ps_ab0",), w=ktmp, tag=f"{name}_ap1")
                if gamma is None:
                    P.op("vector", lambda e, c=c, s=s, dst=dst: e.tensor_tensor(
                        out=dst, in0=tmp, in1=self.ps_ab[1][:, :], op=Alu.add),
                        r=ktmp + ("ps_ab1",), w=dk, tag=f"{name}_ap2")
                else:
                    P.op("vector", lambda e, c=c, s=s: e.tensor_tensor(
                        out=tmp, in0=tmp, in1=self.ps_ab[1][:, :], op=Alu.add),
                        r=ktmp + ("ps_ab1",), w=ktmp, tag=f"{name}_ap2")
                    P.op("vector", lambda e, c=c, s=s, dst=dst: e.tensor_scalar(
                        out=dst, in0=tmp, scalar1=gamma(c), scalar2=beta(c),
                        op0=Alu.mult, op1=Alu.add),
                        r=ktmp + tuple(gb_keys), w=dk, tag=f"{name}_ap3")

    def stream_w_cols(self, hbm, oc, n_in=D, tag="w"):
        P = self.P
        nin_ch = n_in // 128
        wslot = oc % 2
        wf32 = self.WP.ap(F32, nin_ch * 128, byteoff=0)
        wbf = self.WP.ap(BF16, nin_ch * 128, byteoff=4096 + 2048 * wslot)
        kf = self.WP.K(0, nin_ch * 512)
        kb = self.WP.K(4096 + 2048 * wslot, nin_ch * 256)
        src = hbm.ap()[:, oc * 128:(oc + 1) * 128]
        src = src.rearrange("(kc p) m -> p kc m", p=128)
        P.op("sync", lambda e, src=src, wf32=wf32, nin_ch=nin_ch: e.dma_start(
            out=wf32.rearrange("p (kc m) -> p kc m", kc=nin_ch), in_=src),
            w=kf, dma=True, tag=f"{tag}_dma{oc}")
        P.op("vector", lambda e, wf32=wf32, wbf=wbf: e.tensor_copy(out=wbf, in_=wf32),
             r=kf, w=kb, tag=f"{tag}_cvt{oc}")
        return (lambda kc: self.WP.ap(BF16, 128, byteoff=4096 + 2048 * wslot + kc * 256)), kb

    def stage_l0_attn(self, upto='all'):
        P = self.P
        ln0 = lambda kc, c0, cols: self.S1.ap(BF16, cols, byteoff=kc * 4096 + c0 * 2)
        ln0k = lambda kc, c0, cols: self.S1.K(kc * 4096 + c0 * 2, cols * 2)

        for kc in range(NCH):
            wf32 = self.WP.ap(F32, D, byteoff=0)
            kf = self.WP.K(0, 4096)
            P.op("sync", lambda e, kc=kc, wf32=wf32: e.dma_start(
                out=wf32, in_=self.inp["l0_wv"].ap()[kc * 128:(kc + 1) * 128, :]),
                w=kf, dma=True, tag=f"wv_dma{kc}")
            P.op("vector", lambda e, kc=kc, wf32=wf32: e.tensor_copy(
                out=self.PB.ap(BF16, D, byteoff=kc * 2048), in_=wf32),
                r=kf, w=self.PB.K(kc * 2048, 2048), tag=f"wv_cvt{kc}")
        P.op("sync", lambda e: e.dma_start(
            out=self.smv("wa", F32, 128).rearrange("p (kc m) -> p kc m", kc=8),
            in_=self.inp["l0_wa"].ap().rearrange("(kc p) m -> p kc m", p=128)),
            w=("wa",), dma=True, tag="wa_dma")
        P.op("vector", lambda e: e.tensor_copy(
            out=self.smv("wab", BF16, 128), in_=self.smv("wa", F32, 128)),
            r=("wa",), w=("wab",), tag="wa_cvt")
        P.op("sync", lambda e: e.dma_start(
            out=self.smv("ba", F32, 16, part=1), in_=self.inp["l0_ba"].ap()),
            w=("ba",), dma=True, tag="ba_dma")
        P.op("tensor", lambda e: e.matmul(
            self.ps_x[:, 0:16], self.smv("ones_1x128f", F32, 128, part=1),
            self.smv("ba", F32, 16, part=1), start=True, stop=True),
            r=("ba", "ones_1x128f"), w=("ps_x",), tag="ba_bc")
        P.op("vector", lambda e: e.tensor_copy(
            out=self.smv("ba2", F32, 16), in_=self.ps_x[:, 0:16]),
            r=("ps_x",), w=("ba2",), tag="ba2_ev")
        P.op("vector", lambda e: e.memset(self.smv("ones16", F32, 128, part=16), 1.0),
             w=("ones16",), tag="ones16")
        P.op("vector", lambda e: e.memset(self.smv("sz", F32, 520), 0.0),
             w=("sz",), tag="sz0")
        P.op("vector", lambda e: e.memset(self.smv("szbf", BF16, 520), 0.0),
             w=("szbf",), tag="szbf0")

        eps_ap, eps_k = self.cc(1e-6)
        ntt_run = 1 if upto == "ya0" else NTT
        for tt in range(ntt_run):
            b0, b1 = self.ps_ab[0], self.ps_ab[1]
            for kc in range(NCH) if upto != "ya0" else []:
                for half in range(2):
                    bank = (b0, b1)[half]
                    P.op("tensor", lambda e, tt=tt, kc=kc, half=half, bank=bank: e.matmul(
                        bank[:, 0:512], ln0(kc, tt * 128, 128),
                        self.PB.ap(BF16, 512, byteoff=kc * 2048 + half * 1024),
                        start=(kc == 0), stop=(kc == NCH - 1)),
                        r=ln0k(kc, tt * 128, 128) + self.PB.K(kc * 2048 + half * 1024, 1024),
                        w=(f"ps_ab{half}",), tag=f"v_mm{tt}_{kc}_{half}")
            for kc in range(NCH):
                P.op("tensor", lambda e, tt=tt, kc=kc: e.matmul(
                    self.ps_x[:, 0:16], ln0(kc, tt * 128, 128),
                    self.smv("wab", BF16, 16, byteoff=kc * 32),
                    start=(kc == 0), stop=(kc == NCH - 1)),
                    r=ln0k(kc, tt * 128, 128) + ("wab",), w=("ps_x",), tag=f"ya_mm{tt}_{kc}")
            for half in range(2) if upto != "ya0" else []:
                bank = (b0, b1)[half]
                P.op("vector", lambda e, tt=tt, half=half, bank=bank: e.tensor_copy(
                    out=self.S1.ap(BF16, 512, byteoff=32768 + tt * 2048 + half * 1024),
                    in_=bank[:, 0:512]),
                    r=(f"ps_ab{half}",), w=self.S1.K(32768 + tt * 2048 + half * 1024, 1024),
                    tag=f"v_ev{tt}_{half}")
            if upto in ("proj", "ya0") and tt == 0:
                P.op("vector", lambda e: e.tensor_copy(
                    out=self.S0.ap(F32, 16, byteoff=58368), in_=self.ps_x[:, 0:16]),
                    r=("ps_x",), w=self.S0.K(58368, 64), tag="ya_ps_dbg")
            if upto == "ya0":
                continue
            u = self.SC.ap(F32, 16, byteoff=4864)
            yv = self.SC.ap(F32, 16, byteoff=4928)
            mv = self.SC.ap(F32, 16, byteoff=4992)
            av = self.SC.ap(F32, 16, byteoff=5056)
            ku = self.SC.K(4864, 256)
            la = self.smv("la_tok", F32, 16, byteoff=tt * 64)
            one_ap, one_k = self.cc(1.0)
            P.op("vector", lambda e, tt=tt, u=u: e.tensor_tensor(
                out=u, in0=self.ps_x[:, 0:16], in1=self.smv("ba2", F32, 16), op=Alu.add),
                r=("ps_x", "ba2"), w=ku, tag=f"la_u{tt}")
            P.op("vector", lambda e, tt=tt, yv=yv, u=u: e.tensor_scalar(
                out=yv, in0=u, scalar1=2.0, scalar2=None, op0=Alu.add),
                r=ku, w=ku, tag=f"la_y{tt}")
            P.op("vector", lambda e, tt=tt, mv=mv, yv=yv: e.tensor_scalar(
                out=mv, in0=yv, scalar1=0.0, scalar2=None, op0=Alu.min),
                r=ku, w=ku, tag=f"la_m{tt}")
            P.op("scalar", lambda e, tt=tt, av=av, yv=yv: e.activation(
                out=av, in_=yv, func=Act.Abs),
                r=ku, w=ku, tag=f"la_a{tt}")
            P.op("scalar", lambda e, tt=tt, av=av: e.activation(
                out=av, in_=av, func=Act.Exp, scale=-1.0),
                r=ku, w=ku, tag=f"la_e{tt}")
            P.op("scalar", lambda e, tt=tt, av=av, one_ap=one_ap: e.activation(
                out=av, in_=av, func=Act.Ln, bias=one_ap[:, 0:1], scale=1.0),
                r=ku + (one_k,), w=ku, tag=f"la_l{tt}")
            P.op("vector", lambda e, tt=tt, la=la, mv=mv, av=av: e.tensor_tensor(
                out=la, in0=mv, in1=av, op=Alu.subtract),
                r=ku, w=("la_tok",), tag=f"la_f{tt}")
            if upto in ("proj", "ya0"):
                P.op("vector", lambda e, tt=tt, yv=yv: e.tensor_copy(
                    out=self.S0.ap(F32, 16, byteoff=57344 + tt * 64), in_=yv),
                    r=ku, w=self.S0.K(57344 + tt * 64, 64), tag=f"la_dbg{tt}")

        if upto == "ya0":
            return
        def proj_elu(dst_reg, w_name, tag):
            for oc in range(NCH):
                wbf, wkeys = self.stream_w_cols(self.inp[w_name], oc, tag=tag)
                for s in range(4):
                    bank = self.psb[(oc * 4 + s) % 4]
                    bk = f"psb/{(oc * 4 + s) % 4}"
                    for kc in range(NCH):
                        P.op("tensor", lambda e, oc=oc, s=s, kc=kc, bank=bank, wbf=wbf: e.matmul(
                            bank[:, :], wbf(kc), ln0(kc, s * 512, 512),
                            start=(kc == 0), stop=(kc == NCH - 1)),
                            r=wkeys + ln0k(kc, s * 512, 512), w=(bk,), tag=f"{tag}mm{oc}_{s}")
                    m = self.SC.ap(F32, 512, byteoff=0)
                    ev = self.SC.ap(F32, 512, byteoff=2048)
                    P.op("vector", lambda e, bank=bank, m=m: e.tensor_scalar(
                        out=m, in0=bank[:, :], scalar1=0.0, scalar2=None, op0=Alu.min),
                        r=(bk,), w=self.SC.K(0, 2048), tag=f"{tag}min{oc}_{s}")
                    P.op("scalar", lambda e, m=m, ev=ev: e.activation(
                        out=ev, in_=m, func=Act.Exp),
                        r=self.SC.K(0, 2048), w=self.SC.K(2048, 2048), tag=f"{tag}exp{oc}_{s}")
                    P.op("vector", lambda e, oc=oc, s=s, bank=bank, ev=ev, dst_reg=dst_reg:
                         e.scalar_tensor_tensor(
                             out=dst_reg.ap(BF16, 512, byteoff=oc * 4096 + s * 1024),
                             in0=bank[:, :], scalar=0.0, in1=ev,
                             op0=Alu.max, op1=Alu.add),
                         r=(bk,) + self.SC.K(2048, 2048),
                         w=dst_reg.K(oc * 4096 + s * 1024, 1024), tag=f"{tag}out{oc}_{s}")

        proj_elu(self.PB, "l0_wk", "k")
        proj_elu(self.PA, "l0_wq", "q")
        if upto == "proj":
            return

        kfm = lambda p, half, c0, cols: self.PB.ap(BF16, cols, byteoff=p * 4096 + c0 * 2, part=64, p0=half * 64)
        kfmk = lambda p, c0, cols: self.PB.K(p * 4096 + c0 * 2, cols * 2)
        qfm = lambda p, half, c0, cols: self.PA.ap(BF16, cols, byteoff=p * 4096 + c0 * 2, part=64, p0=half * 64)
        qfmk = lambda p, c0, cols: self.PA.K(p * 4096 + c0 * 2, cols * 2)
        LT = lambda c: self.WP.ap(F32, 128, part=16, byteoff=c * 512)
        LTk = lambda c: self.WP.K(c * 512, 512)
        lasl = self.SC.ap(F32, 128, part=16, byteoff=5120)
        lasl_k = self.SC.K(5120, 512)
        negLT = self.SC.ap(F32, 128, part=16, byteoff=5632)
        negLT_k = self.SC.K(5632, 512)
        Etb = self.SC.ap(BF16, 128, part=16, byteoff=6144)
        Etb_k = self.SC.K(6144, 256)
        ECh = self.SC.ap(F32, 128, part=16, byteoff=6400)
        ECh_k = self.SC.K(6400, 512)
        ECt = self.SC.ap(F32, 16, byteoff=6912)
        ECt_k = self.SC.K(6912, 64)
        ltok = self.SC.ap(F32, 16, byteoff=6976)
        ltok_k = self.SC.K(6976, 64)
        lrow = lambda sl: self.smv("lrow2", F32, 128, part=1, byteoff=512 * (sl % 2))
        lrow_k = lambda sl: (f"lrow2/{sl % 2}",)
        e2sb = self.SC.ap(BF16, 128, byteoff=7168)
        e2sb_k = self.SC.K(7168, 256)
        ktok = lambda cols, off: self.SC.ap(BF16, cols, byteoff=0 + off)
        kttok = lambda cols, off: self.SC.ap(BF16, cols, byteoff=2048 + off)
        dmat = self.SC.ap(F32, 128, byteoff=4096)
        dmat_k = self.SC.K(4096, 512)
        sd = self.SC.ap(BF16, 128, byteoff=4608)
        sd_k = self.SC.K(4608, 256)
        qts = self.SC.ap(BF16, 128, byteoff=7424)
        qts_k = self.SC.K(7424, 256)
        lam = self.smv("ecol", F32, 1)
        lam_k = ("ecol",)
        dcol = self.smv("srows", F32, 1)
        rcol = self.smv("srows", F32, 1, byteoff=4)
        dcol_k = ("srows",)

        gla = lambda tt, c0, cols: self.S1.ap(BF16, cols, byteoff=tt * 2048 + c0 * 2)
        glak = lambda tt, c0, cols: self.S1.K(tt * 2048 + c0 * 2, cols * 2)

        for c in range(NTT):
            P.op("tensor", lambda e, c=c: e.transpose(
                self.ps_x[0:16, 0:128],
                self.smv("la_tok", F32, 16, byteoff=c * 64),
                self.smv("id128f", F32, 128)),
                r=("la_tok", "id128f"), w=("ps_x",), tag=f"sc_latr{c}")
            P.op("vector", lambda e, c=c: e.tensor_copy(out=lasl, in_=self.ps_x[0:16, 0:128]),
                 r=("ps_x",), w=lasl_k, tag=f"sc_laev{c}")
            P.op("vector", lambda e, c=c: e.tensor_tensor_scan(
                out=LT(c), data0=self.smv("ones16", F32, 128, part=16), data1=lasl,
                initial=0.0, op0=Alu.mult, op1=Alu.add),
                r=lasl_k + ("ones16",), w=LTk(c), tag=f"sc_scan{c}")
            P.op("vector", lambda e, c=c: e.tensor_scalar(
                out=negLT, in0=LT(c), scalar1=-1.0, scalar2=None, op0=Alu.mult),
                r=LTk(c), w=negLT_k, tag=f"sc_neg{c}")
            P.op("scalar", lambda e, c=c: e.activation(
                out=Etb, in_=LT(c), func=Act.Exp),
                r=LTk(c), w=Etb_k, tag=f"sc_et{c}")
            P.op("vector", lambda e, c=c: e.tensor_scalar(
                out=ECh, in0=LT(c), scalar1=LT(c)[:, 127:128], scalar2=None,
                op0=Alu.subtract),
                r=LTk(c), w=ECh_k, tag=f"sc_ech{c}")
            P.op("scalar", lambda e: e.activation(
                out=ECh, in_=ECh, func=Act.Exp, scale=-1.0),
                r=ECh_k, w=ECh_k, tag=f"sc_eche{c}")
            P.op("tensor", lambda e: e.transpose(
                self.ps_x[0:128, 0:16], ECh, self.smv("id16f", F32, 16, part=16)),
                r=ECh_k + ("id16f",), w=("ps_x",), tag=f"sc_ect{c}")
            P.op("vector", lambda e: e.tensor_copy(out=ECt, in_=self.ps_x[0:128, 0:16]),
                 r=("ps_x",), w=ECt_k, tag=f"sc_ectev{c}")
            P.op("tensor", lambda e, c=c: e.transpose(
                self.ps_x[0:128, 16:32], LT(c), self.smv("id16f", F32, 16, part=16)),
                r=LTk(c) + ("id16f",), w=("ps_x",), tag=f"sc_lttr{c}")
            P.op("vector", lambda e: e.tensor_copy(out=ltok, in_=self.ps_x[0:128, 16:32]),
                 r=("ps_x",), w=ltok_k, tag=f"sc_ltev{c}")
            for p in range(NCH):
                P.op("tensor", lambda e, c=c, p=p: e.transpose(
                    self.ps_x[:, 0:512].bitcast(BF16)[:, 0:128],
                    self.PB.ap(BF16, 128, byteoff=p * 4096 + c * 256),
                    self.smv("id128b", BF16, 128)),
                    r=kfmk(p, c * 128, 128) + ("id128b",), w=("ps_x",), tag=f"sc_ktr{c}_{p}")
                P.op("vector", lambda e, p=p: e.tensor_copy(
                    out=ktok(128, p * 256), in_=self.ps_x[:, 0:512].bitcast(BF16)[:, 0:128]),
                    r=("ps_x",), w=self.SC.K(p * 256, 256), tag=f"sc_ktev{c}_{p}")
            for h in range(16):
                P.op("vector", lambda e, h=h: e.tensor_scalar(
                    out=kttok(64, h * 128), in0=ktok(64, h * 128),
                    scalar1=ECt[:, h:h + 1], scalar2=None, op0=Alu.mult),
                    r=self.SC.K(h * 128, 128) + ECt_k,
                    w=self.SC.K(2048 + h * 128, 128), tag=f"sc_kt{c}_{h}")
            for p in range(NCH):
                e2bank = self.ps_ab[p % 2]
                e2k = f"ps_ab{p % 2}"
                for half in range(2):
                    h = 2 * p + half
                    P.op("tensor", lambda e, c=c, h=h, half=half, e2bank=e2bank: e.matmul(
                        e2bank[0:1, 0:128], self.smv("id16f", F32, 1, part=16, byteoff=h * 4),
                        LT(c), start=True, stop=True),
                        r=LTk(c) + ("id16f",), w=(e2k,), tag=f"sc_lr{c}_{h}")
                    P.op("vector", lambda e, half=half, e2bank=e2bank: e.tensor_copy(
                        out=lrow(half), in_=e2bank[0:1, 0:128]),
                        r=(e2k,), w=lrow_k(half), tag=f"sc_lrev{c}_{h}")
                P.op("tensor", lambda e, e2bank=e2bank: e.matmul(
                    e2bank[0:64, 0:128], self.smv("ones_1x128f", F32, 64, part=1),
                    lrow(0), start=True, stop=True),
                    r=lrow_k(0) + ("ones_1x128f",), w=(e2k,), tag=f"sc_b2a{c}_{p}")
                P.op("tensor", lambda e, e2bank=e2bank: e.matmul(
                    e2bank[64:128, 0:128], self.smv("ones_1x128f", F32, 64, part=1),
                    lrow(1), start=True, stop=True),
                    r=lrow_k(1) + ("ones_1x128f",), w=(e2k,), tag=f"sc_b2b{c}_{p}")
                P.op("scalar", lambda e, e2bank=e2bank: e.activation(
                    out=lam, in_=e2bank[:, 127:128], func=Act.Exp),
                    r=(e2k,), w=lam_k, tag=f"sc_lam{c}_{p}")
                P.op("scalar", lambda e, e2bank=e2bank: e.activation(
                    out=e2sb, in_=e2bank[:, 0:128], func=Act.Exp),
                    r=(e2k,), w=e2sb_k, tag=f"sc_e2{c}_{p}")
                P.op("vector", lambda e, c=c, p=p: e.tensor_tensor(
                    out=qts, in0=self.PA.ap(BF16, 128, byteoff=p * 4096 + c * 256),
                    in1=e2sb, op=Alu.mult),
                    r=qfmk(p, c * 128, 128) + e2sb_k, w=qts_k, tag=f"sc_qt{c}_{p}")
                P.op("vector", lambda e, p=p: e.tensor_scalar(
                    out=self.smv("szscr", F32, 65), in0=self.smv("sz", F32, 65, byteoff=p * 260),
                    scalar1=lam[:, 0:1], scalar2=None, op0=Alu.mult),
                    r=("sz",) + lam_k, w=("szscr",), tag=f"sc_su1{c}_{p}")
                for half in range(2):
                    h = 2 * p + half
                    sb = self.psb[h % 2]
                    sbk = f"psb/{h % 2}"
                    db = self.psb[2 + h % 2]
                    dbk = f"psb/{2 + h % 2}"
                    P.op("tensor", lambda e, c=c, p=p, half=half, sb=sb: e.matmul(
                        sb[:, 0:128], kfm(p, half, c * 128, 128), qfm(p, half, c * 128, 128),
                        start=True, stop=True),
                        r=kfmk(p, c * 128, 128) + qfmk(p, c * 128, 128), w=(sbk,),
                        tag=f"sc_sT{c}_{h}")
                    P.op("tensor", lambda e, half=half, db=db: e.matmul(
                        db[:, 0:128], self.smv("ones_1x128f", F32, 128, part=1),
                        lrow(half), start=True, stop=True),
                        r=lrow_k(half) + ("ones_1x128f",), w=(dbk,), tag=f"sc_d1{c}_{h}")
                    P.op("vector", lambda e, h=h, db=db: e.scalar_tensor_tensor(
                        out=dmat, in0=db[:, 0:128], scalar=ltok[:, h:h + 1],
                        in1=self.smv("maskbig", F32, 128), op0=Alu.subtract, op1=Alu.min),
                        r=(dbk, "maskbig") + ltok_k, w=dmat_k, tag=f"sc_dmin{c}_{h}")
                    P.op("scalar", lambda e: e.activation(out=dmat, in_=dmat, func=Act.Exp),
                         r=dmat_k, w=dmat_k, tag=f"sc_dexp{c}_{h}")
                    P.op("vector", lambda e, sb=sb: e.tensor_tensor(
                        out=sd, in0=sb[:, 0:128], in1=dmat, op=Alu.mult),
                        r=(sbk,) + dmat_k, w=sd_k, tag=f"sc_sd{c}_{h}")
                    P.op("tensor", lambda e, c=c, h=h: e.matmul(
                        self.ps_st[:, 0:64], sd,
                        self.S1.ap(BF16, 64, byteoff=32768 + c * 2048 + h * 128),
                        start=True, stop=False),
                        r=sd_k + self.S1.K(32768 + c * 2048 + h * 128, 128),
                        w=("ps_st",), tag=f"sc_nv{c}_{h}")
                    P.op("tensor", lambda e, c=c, p=p, half=half: e.matmul(
                        self.ps_st[:, 0:64], qts[half * 64:half * 64 + 64, :],
                        self.smv("szbf", BF16, 64, byteoff=p * 130, part=64, p0=half * 64),
                        start=False, stop=True),
                        r=qts_k + ("szbf",), w=("ps_st",), tag=f"sc_ni{c}_{h}")
                    P.op("tensor", lambda e, c=c, h=h: e.matmul(
                        self.ps_st[:, 128:129], sd, self.smv("onescol_b", BF16, 1),
                        start=True, stop=False),
                        r=sd_k + ("onescol_b",), w=("ps_st",), tag=f"sc_nd{c}_{h}")
                    P.op("tensor", lambda e, c=c, p=p, half=half: e.matmul(
                        self.ps_st[:, 128:129], qts[half * 64:half * 64 + 64, :],
                        self.smv("szbf", BF16, 1, byteoff=p * 130 + 128, part=64, p0=half * 64),
                        start=False, stop=True),
                        r=qts_k + ("szbf",), w=("ps_st",), tag=f"sc_nid{c}_{h}")
                    P.op("vector", lambda e: e.tensor_scalar(
                        out=dcol, in0=self.ps_st[:, 128:129], scalar1=1e-6, scalar2=None,
                        op0=Alu.add),
                        r=("ps_st",), w=dcol_k, tag=f"sc_de{c}_{h}")
                    P.op("vector", lambda e: e.reciprocal(out=rcol, in_=dcol),
                         r=dcol_k, w=dcol_k, tag=f"sc_rc{c}_{h}")
                    P.op("scalar", lambda e, c=c, h=h: e.mul(
                        gla(c, h * 64, 64), self.ps_st[:, 0:64], rcol[:, 0:1]),
                        r=("ps_st",) + dcol_k, w=glak(c, h * 64, 64), tag=f"sc_gv{c}_{h}")
                    pr = half * 64
                    pcol = half * 128
                    P.op("tensor", lambda e, c=c, h=h, pr=pr, pcol=pcol: e.matmul(
                        self.ps_x[pr:pr + 64, pcol:pcol + 64], kttok(64, h * 128),
                        self.S1.ap(BF16, 64, byteoff=32768 + c * 2048 + h * 128),
                        start=True, stop=True),
                        r=self.SC.K(2048 + h * 128, 128) + self.S1.K(32768 + c * 2048 + h * 128, 128),
                        w=("ps_x",), tag=f"sc_sv{c}_{h}")
                    P.op("tensor", lambda e, c=c, h=h, pr=pr, pcol=pcol: e.matmul(
                        self.ps_x[pr:pr + 64, pcol + 64:pcol + 65], kttok(64, h * 128),
                        self.smv("onescol_b", BF16, 1),
                        start=True, stop=True),
                        r=self.SC.K(2048 + h * 128, 128) + ("onescol_b",),
                        w=("ps_x",), tag=f"sc_sz{c}_{h}")
                for half in range(2):
                    pr = half * 64
                    pcol = half * 128
                    P.op("vector", lambda e, p=p, pr=pr, pcol=pcol: e.tensor_tensor(
                        out=self.smv("sz", F32, 65, byteoff=p * 260, part=64, p0=pr),
                        in0=self.smv("szscr", F32, 65, part=64, p0=pr),
                        in1=self.ps_x[pr:pr + 64, pcol:pcol + 65], op=Alu.add),
                        r=("szscr", "ps_x"), w=("sz",), tag=f"sc_su2{c}_{p}_{half}")
                P.op("gpsimd", lambda e, p=p: e.tensor_copy(
                    out=self.smv("szbf", BF16, 65, byteoff=p * 130),
                    in_=self.smv("sz", F32, 65, byteoff=p * 260)),
                    r=("sz",), w=("szbf",), tag=f"sc_su3{c}_{p}")

        if upto == "scan":
            return
        for p in range(NCH):
            for tt in range(NTT):
                P.op("tensor", lambda e, p=p, tt=tt: e.transpose(
                    self.ps_x[:, 0:512].bitcast(BF16)[:, 0:128],
                    gla(tt, p * 128, 128), self.smv("id128b", BF16, 128)),
                    r=glak(tt, p * 128, 128) + ("id128b",), w=("ps_x",), tag=f"gf_tr{p}_{tt}")
                P.op("vector", lambda e, p=p, tt=tt: e.tensor_copy(
                    out=self.PA.ap(BF16, 128, byteoff=p * 4096 + tt * 256),
                    in_=self.ps_x[:, 0:512].bitcast(BF16)[:, 0:128]),
                    r=("ps_x",), w=self.PA.K(p * 4096 + tt * 256, 256), tag=f"gf_ev{p}_{tt}")

        for oc in range(NCH):
            wbf, wkeys = self.stream_w_cols(self.inp["l0_wo"], oc, tag="wo")
            for s in range(4):
                bank = self.psb[(oc * 4 + s) % 4]
                bk = f"psb/{(oc * 4 + s) % 4}"
                for kc in range(NCH):
                    P.op("tensor", lambda e, oc=oc, s=s, kc=kc, bank=bank, wbf=wbf: e.matmul(
                        bank[:, :], wbf(kc),
                        self.PA.ap(BF16, 512, byteoff=kc * 4096 + s * 1024),
                        start=(kc == 0), stop=(kc == NCH - 1)),
                        r=wkeys + self.PA.K(kc * 4096 + s * 1024, 1024), w=(bk,),
                        tag=f"wo_mm{oc}_{s}_{kc}")
                P.op("vector", lambda e, oc=oc, s=s, bank=bank: e.tensor_tensor(
                    out=self.S0.ap(F32, 512, byteoff=oc * 8192 + s * 2048),
                    in0=bank[:, :], in1=self.S0.ap(F32, 512, byteoff=oc * 8192 + s * 2048),
                    op=Alu.add),
                    r=(bk,) + self.S0.K(oc * 8192 + s * 2048, 2048),
                    w=self.S0.K(oc * 8192 + s * 2048, 2048), tag=f"wo_ev{oc}_{s}")

    def stage_l0_ffn(self):
        P = self.P
        ln1 = lambda kc, c0, cols: self.S1.ap(BF16, cols, byteoff=kc * 4096 + c0 * 2)
        ln1k = lambda kc, c0, cols: self.S1.K(kc * 4096 + c0 * 2, cols * 2)
        psx_bf = lambda: self.ps_x[:, 0:512].bitcast(BF16)

        def hid_ap(oc, c0, cols):
            reg = self.PA if oc < 16 else self.PB
            off = (oc if oc < 16 else oc - 16) * 2048
            return reg.ap(BF16, cols, byteoff=off + c0 * 2)

        def hid_k(oc, c0, cols):
            reg = self.PA if oc < 16 else self.PB
            off = (oc if oc < 16 else oc - 16) * 2048
            return reg.K(off + c0 * 2, cols * 2)

        def stage_wT(hbm, oc, slot, tag):
            wi = self.WP.ap(I32, D, byteoff=0)
            ki = self.WP.K(0, 4096)
            wb = self.WP.ap(BF16, D, byteoff=4096 + 2048 * (slot % 2))
            kb = self.WP.K(4096 + 2048 * (slot % 2), 2048)
            P.op("sync", lambda e, oc=oc, wi=wi: e.dma_start(
                out=wi, in_=hbm.ap()[oc * 128:(oc + 1) * 128, :]),
                w=ki, dma=True, tag=f"{tag}dma{oc}")
            P.op("vector", lambda e, wi=wi, wb=wb: e.tensor_copy(out=wb, in_=wi),
                 r=ki, w=kb, tag=f"{tag}cvt{oc}")
            st_off = (slot % 4) * 2048
            for kc in range(NCH):
                P.op("tensor", lambda e, kc=kc, wb=wb: e.transpose(
                    psx_bf()[:, 0:128], wb[:, kc * 128:(kc + 1) * 128],
                    self.smv("id128b", BF16, 128)),
                    r=kb + ("id128b",), w=("ps_x",), tag=f"{tag}tr{oc}_{kc}")
                P.op("vector", lambda e, kc=kc, st_off=st_off: e.tensor_copy(
                    out=self.SC.ap(BF16, 128, byteoff=st_off + kc * 256),
                    in_=psx_bf()[:, 0:128]),
                    r=("ps_x",), w=self.SC.K(st_off + kc * 256, 256), tag=f"{tag}trev{oc}_{kc}")
            keys = self.SC.K(st_off, 2048)
            return (lambda kc, st_off=st_off: self.SC.ap(BF16, 128, byteoff=st_off + kc * 256)), keys

        for half in range(2):
            t0 = half * 1024
            for oc in range(24):
                gT, gkeys = stage_wT(self.inp["l0f_wg"], oc, 2 * oc, "wg")
                uT, ukeys = stage_wT(self.inp["l0f_wu"], oc, 2 * oc + 1, "wu")
                for s in range(2):
                    c0 = t0 + s * 512
                    bg = self.psb[(oc * 2 + s) % 2]
                    bgk = f"psb/{(oc * 2 + s) % 2}"
                    bu = self.psb[2 + (oc * 2 + s) % 2]
                    buk = f"psb/{2 + (oc * 2 + s) % 2}"
                    for kc in range(NCH):
                        P.op("tensor", lambda e, kc=kc, c0=c0, bg=bg, gT=gT: e.matmul(
                            bg[:, :], gT(kc), ln1(kc, c0, 512),
                            start=(kc == 0), stop=(kc == NCH - 1)),
                            r=gkeys + ln1k(kc, c0, 512), w=(bgk,), tag=f"g_mm{half}_{oc}_{s}")
                        P.op("tensor", lambda e, kc=kc, c0=c0, bu=bu, uT=uT: e.matmul(
                            bu[:, :], uT(kc), ln1(kc, c0, 512),
                            start=(kc == 0), stop=(kc == NCH - 1)),
                            r=ukeys + ln1k(kc, c0, 512), w=(buk,), tag=f"u_mm{half}_{oc}_{s}")
                    sg = self.S1.ap(F32, 512, byteoff=32768 + (s % 2) * 2048)
                    sgk = self.S1.K(32768 + (s % 2) * 2048, 2048)
                    P.op("scalar", lambda e, bg=bg, sg=sg: e.activation(
                        out=sg, in_=bg[:, :], func=Act.Silu, scale=1.0 / 127.0),
                        r=(bgk,), w=sgk, tag=f"sil{half}_{oc}_{s}")
                    P.op("vector", lambda e, oc=oc, s=s, bu=bu, sg=sg: e.tensor_tensor(
                        out=hid_ap(oc, s * 512, 512), in0=bu[:, :], in1=sg, op=Alu.mult),
                        r=(buk,) + sgk, w=hid_k(oc, s * 512, 512), tag=f"hid{half}_{oc}_{s}")
            for dc in range(NCH):
                lhsT = {}
                for q in range(4):
                    wi = self.WP.ap(I32, 768, byteoff=0)
                    ki = self.WP.K(0, 3072)
                    wb = self.WP.ap(BF16, 768, byteoff=4096 + 2048 * (q % 2))
                    kb = self.WP.K(4096 + 2048 * (q % 2), 1536)
                    P.op("sync", lambda e, dc=dc, q=q, wi=wi: e.dma_start(
                        out=wi, in_=self.inp["l0f_wd"].ap()[dc * 128:(dc + 1) * 128,
                                                            q * 768:(q + 1) * 768]),
                        w=ki, dma=True, tag=f"wd_dma{dc}_{q}")
                    P.op("vector", lambda e, wi=wi, wb=wb: e.tensor_copy(out=wb, in_=wi),
                         r=ki, w=kb, tag=f"wd_cvt{dc}_{q}")
                    for j in range(6):
                        hc = q * 6 + j
                        st_off = (hc % 24) * 256
                        P.op("tensor", lambda e, j=j, wb=wb: e.transpose(
                            psx_bf()[:, 0:128], wb[:, j * 128:(j + 1) * 128],
                            self.smv("id128b", BF16, 128)),
                            r=kb + ("id128b",), w=("ps_x",), tag=f"wd_tr{dc}_{hc}")
                        P.op("vector", lambda e, st_off=st_off: e.tensor_copy(
                            out=self.SC.ap(BF16, 128, byteoff=st_off),
                            in_=psx_bf()[:, 0:128]),
                            r=("ps_x",), w=self.SC.K(st_off, 256), tag=f"wd_trev{dc}_{hc}")
                        lhsT[hc] = (self.SC.ap(BF16, 128, byteoff=st_off),
                                    self.SC.K(st_off, 256))
                for s in range(2):
                    c0 = t0 + s * 512
                    bank = self.psb[(dc * 2 + s) % 4]
                    bk = f"psb/{(dc * 2 + s) % 4}"
                    for hc in range(24):
                        ap_, kk = lhsT[hc]
                        P.op("tensor", lambda e, hc=hc, c0=c0, t0=t0, bank=bank, ap_=ap_: e.matmul(
                            bank[:, :], ap_, hid_ap(hc, c0 - t0, 512),
                            start=(hc == 0), stop=(hc == 23)),
                            r=kk + hid_k(hc, c0 - t0, 512), w=(bk,), tag=f"d_mm{half}_{dc}_{hc}")
                    P.op("vector", lambda e, dc=dc, c0=c0, bank=bank: e.scalar_tensor_tensor(
                        out=self.S0.ap(F32, 512, byteoff=dc * 8192 + c0 * 4),
                        in0=bank[:, :], scalar=1.0 / 16129.0,
                        in1=self.S0.ap(F32, 512, byteoff=dc * 8192 + c0 * 4),
                        op0=Alu.mult, op1=Alu.add),
                        r=(bk,) + self.S0.K(dc * 8192 + c0 * 4, 2048),
                        w=self.S0.K(dc * 8192 + c0 * 4, 2048), tag=f"d_ev{half}_{dc}_{s}")

    def stage_l1_attn(self, nlayers=4, upto='all'):
        P = self.P

        for c in range(NCH):
            P.op("sync", lambda e, c=c: e.dma_start(
                out=self.xspill_h.ap()[c], in_=self.S0.ap(F32, T, byteoff=c * 8192)),
                r=self.S0.K(c * 8192, 8192), dma=True, tag=f"xsp{c}")

        h_ap = lambda kc, c0, cols: self.S1.ap(F32, cols, byteoff=kc * 8192 + c0 * 4)
        h_k = lambda kc, c0, cols: self.S1.K(kc * 8192 + c0 * 4, cols * 4)
        hbf = lambda kc, c0, cols: self.S0.ap(BF16, cols, byteoff=kc * 4096 + c0 * 2)
        hbf_k = lambda kc, c0, cols: self.S0.K(kc * 4096 + c0 * 2, cols * 2)
        r_ap = hbf
        r_k = hbf_k
        kfm = lambda kc, c0, cols: self.PA.ap(BF16, cols, byteoff=kc * 4096 + c0 * 2)
        kfm_k = lambda kc, c0, cols: self.PA.K(kc * 4096 + c0 * 2, cols * 2)
        vtok = lambda tt, c0, cols: self.PB.ap(BF16, cols, byteoff=tt * 2048 + c0 * 2)
        vtok_k = lambda tt, c0, cols: self.PB.K(tt * 2048 + c0 * 2, cols * 2)
        wvst = lambda kc, c0, cols: self.S0.ap(BF16, cols, byteoff=32768 + kc * 2048 + c0 * 2)
        wvst_k = lambda kc, c0, cols: self.S0.K(32768 + kc * 2048 + c0 * 2, cols * 2)
        qsl = lambda b, kc, c0, cols: self.S0.ap(BF16, cols, byteoff=49152 + b * 4096 + kc * 512 + c0 * 2)
        qsl_k = lambda b, kc, c0, cols: self.S0.K(49152 + b * 4096 + kc * 512 + c0 * 2, cols * 2)
        scB = self.S0.ap(F32, 256, byteoff=57344)
        scB_k = self.S0.K(57344, 1024)
        psx_bf = lambda: self.ps_x[:, 0:512].bitcast(BF16)

        def stream_cols(src_ap2d, oc, slot, tag, n_in=D):
            nch = n_in // 128
            wf32 = self.WP.ap(F32, nch * 128, byteoff=0)
            kf = self.WP.K(0, nch * 512)
            wb_off = 4096 + 2048 * (slot % 2)
            wbf = self.WP.ap(BF16, nch * 128, byteoff=wb_off)
            kb = self.WP.K(wb_off, nch * 256)
            src = src_ap2d[:, oc * 128:(oc + 1) * 128].rearrange("(kc p) m -> p kc m", p=128)
            P.op("sync", lambda e, src=src, wf32=wf32, nch=nch: e.dma_start(
                out=wf32.rearrange("p (kc m) -> p kc m", kc=nch), in_=src),
                w=kf, dma=True, tag=f"{tag}dma{oc}")
            P.op("vector", lambda e, wf32=wf32, wbf=wbf: e.tensor_copy(out=wbf, in_=wf32),
                 r=kf, w=kb, tag=f"{tag}cvt{oc}")
            return (lambda kc, wb_off=wb_off: self.WP.ap(BF16, 128, byteoff=wb_off + kc * 256)), kb

        def proj_fm(src_ap2d, oc_list, dst_ap, dst_k, tag, mov_ap, mov_k, nstr=4,
                    evac="copy", bias_col=None, bias_keys=(), extra=None, scale=None):
            for i, oc in enumerate(oc_list):
                wbf, wkeys = stream_cols(src_ap2d, oc, i, tag)
                for s in range(nstr):
                    bank = self.psb[(i * nstr + s) % 4]
                    bk = f"psb/{(i * nstr + s) % 4}"
                    for kc in range(NCH):
                        P.op("tensor", lambda e, oc=oc, s=s, kc=kc, bank=bank, wbf=wbf: e.matmul(
                            bank[:, :], wbf(kc), mov_ap(kc, s * 512, 512),
                            start=(kc == 0), stop=(kc == NCH - 1)),
                            r=wkeys + mov_k(kc, s * 512, 512), w=(bk,), tag=f"{tag}mm{oc}_{s}")
                    if evac == "copy":
                        P.op("vector", lambda e, oc=oc, s=s, bank=bank: e.tensor_copy(
                            out=dst_ap(oc, s * 512, 512), in_=bank[:, :]),
                            r=(bk,), w=dst_k(oc, s * 512, 512), tag=f"{tag}ev{oc}_{s}")
                    elif evac == "scale":
                        P.op("vector", lambda e, oc=oc, s=s, bank=bank: e.tensor_scalar(
                            out=dst_ap(oc, s * 512, 512), in0=bank[:, :],
                            scalar1=scale, scalar2=None, op0=Alu.mult),
                            r=(bk,), w=dst_k(oc, s * 512, 512), tag=f"{tag}ev{oc}_{s}")
                    elif evac == "custom":
                        extra(oc, s, bank, bk)

        for l in range(nlayers):
            wkqv2d = self.inp["l1_wkqv"].ap()[l]
            wm1_2d = self.inp["l1_wm1"].ap()[l]
            wm2_2d = self.inp["l1_wm2"].ap()[l]
            wf1_2d = self.inp["l1_wf1"].ap()[l]
            wf2_2d = self.inp["l1_wf2"].ap()[l]

            def load_bias(hbm_row, n, col0, tag):
                nch = n // 128
                P.op("sync", lambda e, hbm_row=hbm_row, nch=nch: e.dma_start(
                    out=self.smv("bias_nat", F32, 128, part=nch),
                    in_=hbm_row.rearrange("(c p) -> c p", p=128)),
                    w=("bias_nat",), dma=True, tag=f"b_{tag}dma")
                P.op("tensor", lambda e, nch=nch: e.transpose(
                    self.ps_x[0:128, 0:nch], self.smv("bias_nat", F32, 128, part=nch),
                    self.smv("id16f", F32, nch, part=nch)),
                    r=("bias_nat", "id16f"), w=("ps_x",), tag=f"b_{tag}tr")
                P.op("vector", lambda e, nch=nch, col0=col0: e.tensor_copy(
                    out=self.smv("biasT", F32, nch, byteoff=col0 * 4),
                    in_=self.ps_x[0:128, 0:nch]),
                    r=("ps_x",), w=("biasT",), tag=f"b_{tag}ev")

            load_bias(self.inp["l1_bm1"].ap()[l], 2048, 0, f"bm1_{l}")
            load_bias(self.inp["l1_bm2"].ap()[l], D, 16, f"bm2_{l}")
            load_bias(self.inp["l1_gng"].ap()[l], D, 24, f"gng_{l}")
            load_bias(self.inp["l1_gnb"].ap()[l], D, 32, f"gnb_{l}")
            load_bias(self.inp["l1_bf1"].ap()[l], 2048, 40, f"bf1_{l}")
            load_bias(self.inp["l1_bf2"].ap()[l], D, 56, f"bf2_{l}")
            load_bias(self.inp["l1_lng"].ap()[l], D, 64, f"lng_{l}")
            load_bias(self.inp["l1_lnb"].ap()[l], D, 72, f"lnb_{l}")
            bcol = lambda c: self.smv("biasT", F32, 1, byteoff=c * 4)

            for kc in range(NCH):
                for s in range(2):
                    P.op("vector", lambda e, kc=kc, s=s: e.tensor_copy(
                        out=hbf(kc, s * 1024, 1024), in_=h_ap(kc, s * 1024, 1024)),
                        r=h_k(kc, s * 1024, 1024), w=hbf_k(kc, s * 1024, 1024),
                        tag=f"hbf{l}_{kc}_{s}")

            for kc in range(NCH):
                wf32 = self.WP.ap(F32, D, byteoff=0)
                kf = self.WP.K(0, 4096)
                P.op("sync", lambda e, kc=kc, wf32=wf32, wkqv2d=wkqv2d: e.dma_start(
                    out=wf32, in_=wkqv2d[kc * 128:(kc + 1) * 128, 2048:3072]),
                    w=kf, dma=True, tag=f"wv{l}dma{kc}")
                P.op("vector", lambda e, kc=kc, wf32=wf32: e.tensor_copy(
                    out=wvst(kc, 0, D), in_=wf32),
                    r=kf, w=wvst_k(kc, 0, D), tag=f"wv{l}cvt{kc}")
            for tt in range(NTT):
                for half in range(2):
                    bank = self.ps_ab[half]
                    bkk = f"ps_ab{half}"
                    for kc in range(NCH):
                        P.op("tensor", lambda e, tt=tt, half=half, kc=kc, bank=bank: e.matmul(
                            bank[:, :], hbf(kc, tt * 128, 128), wvst(kc, half * 512, 512),
                            start=(kc == 0), stop=(kc == NCH - 1)),
                            r=hbf_k(kc, tt * 128, 128) + wvst_k(kc, half * 512, 512),
                            w=(bkk,), tag=f"v{l}mm{tt}_{half}")
                    P.op("vector", lambda e, tt=tt, half=half, bank=bank: e.tensor_copy(
                        out=vtok(tt, half * 512, 512), in_=bank[:, :]),
                        r=(bkk,), w=vtok_k(tt, half * 512, 512), tag=f"v{l}ev{tt}_{half}")
            qwst_k = lambda oc: self.S0.K(32768 + oc * 2048, 2048)
            for oc in range(NCH):
                qstg = self.S0.ap(F32, 1024, byteoff=49152 + (oc % 2) * 4096)
                qsk = self.S0.K(49152 + (oc % 2) * 4096, 4096)
                src = wkqv2d[:, 1024 + oc * 128:1024 + (oc + 1) * 128].rearrange(
                    "(kc p) m -> p kc m", p=128)
                P.op("sync", lambda e, src=src, qstg=qstg: e.dma_start(
                    out=qstg.rearrange("p (kc m) -> p kc m", kc=NCH), in_=src),
                    w=qsk, dma=True, tag=f"qw{l}dma{oc}")
                P.op("vector", lambda e, oc=oc, qstg=qstg: e.tensor_copy(
                    out=self.S0.ap(BF16, 1024, byteoff=32768 + oc * 2048), in_=qstg),
                    r=qsk, w=qwst_k(oc), tag=f"qw{l}cvt{oc}")
            proj_fm(wkqv2d, list(range(8)), kfm, kfm_k, f"k{l}_",
                    hbf, hbf_k, evac="copy")

            for g in range(8):
                qb = g % 2
                for oc in range(NCH):
                    bank = self.psb[oc % 2]
                    bk = f"psb/{oc % 2}"
                    for kc in range(NCH):
                        P.op("tensor", lambda e, g=g, oc=oc, kc=kc, bank=bank: e.matmul(
                            bank[:, 0:256],
                            self.S0.ap(BF16, 128, byteoff=32768 + oc * 2048 + kc * 256),
                            hbf(kc, g * 256, 256),
                            start=(kc == 0), stop=(kc == NCH - 1)),
                            r=qwst_k(oc) + hbf_k(kc, g * 256, 256), w=(bk,), tag=f"q{l}mm{g}_{oc}")
                    P.op("vector", lambda e, g=g, oc=oc, bank=bank, qb=qb: e.tensor_scalar(
                        out=qsl(qb, oc, 0, 256), in0=bank[:, 0:256],
                        scalar1=1.0 / 32.0, scalar2=None, op0=Alu.mult),
                        r=(bk,), w=qsl_k(qb, oc, 0, 256), tag=f"q{l}ev{g}_{oc}")
                nkt = 2 * g + 2
                for kt in range(nkt):
                    bank = self.psb[2 + kt % 2]
                    bk = f"psb/{2 + kt % 2}"
                    for kc in range(NCH):
                        P.op("tensor", lambda e, g=g, kt=kt, kc=kc, bank=bank, qb=qb: e.matmul(
                            bank[:, 0:256], kfm(kc, kt * 128, 128), qsl(qb, kc, 0, 256),
                            start=(kc == 0), stop=(kc == NCH - 1)),
                            r=kfm_k(kc, kt * 128, 128) + qsl_k(qb, kc, 0, 256),
                            w=(bk,), tag=f"s{l}mm{g}_{kt}")
                    if kt >= 2 * g:
                        dk = kt - 2 * g
                        P.op("vector", lambda e, kt=kt, dk=dk, bank=bank: e.tensor_tensor(
                            out=self.SC.ap(BF16, 256, byteoff=kt * 512),
                            in0=bank[:, 0:256], in1=self.smv(f"trimask{dk}", BF16, 256),
                            op=Alu.mult),
                            r=(bk, f"trimask{dk}"), w=self.SC.K(kt * 512, 512),
                            tag=f"s{l}ev{g}_{kt}")
                    else:
                        P.op("vector", lambda e, kt=kt, bank=bank: e.tensor_copy(
                            out=self.SC.ap(BF16, 256, byteoff=kt * 512), in_=bank[:, 0:256]),
                            r=(bk,), w=self.SC.K(kt * 512, 512), tag=f"s{l}ev{g}_{kt}")
                for kt in range(nkt):
                    P.op("tensor", lambda e, kt=kt, nkt=nkt: e.matmul(
                        self.ps_st[0:1, 0:256], self.smv("onescol_b", BF16, 1),
                        self.SC.ap(BF16, 256, byteoff=kt * 512),
                        start=(kt == 0), stop=(kt == nkt - 1)),
                        r=self.SC.K(kt * 512, 512) + ("onescol_b",), w=("ps_st",),
                        tag=f"d{l}mm{g}_{kt}")
                P.op("sync", lambda e, g=g: e.dma_start(
                    out=self.smv("rnrow", F32, 256, part=1),
                    in_=self.const_h["rn"].ap()[0:1, g * 256:(g + 1) * 256]),
                    w=("rnrow",), dma=True, tag=f"rn{l}{g}")
                lr = lambda: self.smv("lrow2", F32, 256, part=1)
                P.op("scalar", lambda e: e.activation(
                    out=lr(), in_=self.ps_st[0:1, 0:256], func=Act.Abs),
                    r=("ps_st",), w=("lrow2/0",), tag=f"sc{l}abs{g}")
                P.op("vector", lambda e: e.tensor_tensor(
                    out=lr(), in0=lr(), in1=self.smv("rnrow", F32, 256, part=1),
                    op=Alu.max),
                    r=("lrow2/0", "rnrow"), w=("lrow2/0",), tag=f"sc{l}max{g}")
                P.op("vector", lambda e: e.reciprocal(out=lr(), in_=lr()),
                     r=("lrow2/0",), w=("lrow2/0",), tag=f"sc{l}rc{g}")
                P.op("tensor", lambda e: e.matmul(
                    self.ps_ab[0][:, 0:256], self.smv("ones_1x128f", F32, 128, part=1),
                    lr(), start=True, stop=True),
                    r=("lrow2/0", "ones_1x128f"), w=("ps_ab0",), tag=f"sc{l}bc{g}")
                P.op("vector", lambda e: e.tensor_copy(out=scB, in_=self.ps_ab[0][:, 0:256]),
                     r=("ps_ab0",), w=scB_k, tag=f"sc{l}ev{g}")
                for vc in range(NCH):
                    bank = self.psb[vc % 2]
                    bk = f"psb/{vc % 2}"
                    for kt in range(nkt):
                        P.op("tensor", lambda e, g=g, vc=vc, kt=kt, nkt=nkt, bank=bank: e.matmul(
                            bank[:, 0:256], vtok(kt, vc * 128, 128),
                            self.SC.ap(BF16, 256, byteoff=kt * 512),
                            start=(kt == 0), stop=(kt == nkt - 1)),
                            r=vtok_k(kt, vc * 128, 128) + self.SC.K(kt * 512, 512),
                            w=(bk,), tag=f"r{l}mm{g}_{vc}_{kt}")
                    P.op("vector", lambda e, g=g, vc=vc, bank=bank: e.tensor_tensor(
                        out=r_ap(vc, g * 256, 256), in0=bank[:, 0:256], in1=scB,
                        op=Alu.mult),
                        r=(bk,) + scB_k, w=r_k(vc, g * 256, 256), tag=f"r{l}ev{g}_{vc}")

            for oc in range(NCH):
                aT, akeys = stream_cols(wm1_2d, oc, 0, f"a{l}_")
                gT, gkeys = stream_cols(wm1_2d, oc + 8, 1, f"g{l}_")
                for s in range(4):
                    ba_ = self.psb[s % 2]
                    bak = f"psb/{s % 2}"
                    bg_ = self.psb[2 + s % 2]
                    bgk = f"psb/{2 + s % 2}"
                    for kc in range(NCH):
                        P.op("tensor", lambda e, oc=oc, s=s, kc=kc, ba_=ba_, aT=aT: e.matmul(
                            ba_[:, :], aT(kc), r_ap(kc, s * 512, 512),
                            start=(kc == 0), stop=(kc == NCH - 1)),
                            r=akeys + r_k(kc, s * 512, 512), w=(bak,), tag=f"a{l}mm{oc}_{s}")
                        P.op("tensor", lambda e, oc=oc, s=s, kc=kc, bg_=bg_, gT=gT: e.matmul(
                            bg_[:, :], gT(kc), r_ap(kc, s * 512, 512),
                            start=(kc == 0), stop=(kc == NCH - 1)),
                            r=gkeys + r_k(kc, s * 512, 512), w=(bgk,), tag=f"g{l}mm{oc}_{s}")
                    sg = self.S0.ap(F32, 512, byteoff=57344 + 2048)
                    sgk = self.S0.K(57344 + 2048, 2048)
                    P.op("scalar", lambda e, oc=oc, bg_=bg_, sg=sg: e.activation(
                        out=sg, in_=bg_[:, :], func=Act.Silu, bias=bcol(8 + oc), scale=1.0),
                        r=(bgk, "biasT"), w=sgk, tag=f"sil{l}_{oc}_{s}")
                    P.op("vector", lambda e, oc=oc, s=s, ba_=ba_, sg=sg: e.scalar_tensor_tensor(
                        out=self.PA.ap(BF16, 512, byteoff=oc * 4096 + s * 1024),
                        in0=ba_[:, :], scalar=bcol(oc), in1=sg,
                        op0=Alu.add, op1=Alu.mult),
                        r=(bak, "biasT") + sgk,
                        w=self.PA.K(oc * 4096 + s * 1024, 1024), tag=f"pr{l}_{oc}_{s}")

            prod_ap = lambda kc, c0, cols: self.PA.ap(BF16, cols, byteoff=kc * 4096 + c0 * 2)
            prod_k = lambda kc, c0, cols: self.PA.K(kc * 4096 + c0 * 2, cols * 2)
            m2_ap = lambda kc, c0, cols: self.PB.ap(BF16, cols, byteoff=kc * 4096 + c0 * 2)
            m2_k = lambda kc, c0, cols: self.PB.K(kc * 4096 + c0 * 2, cols * 2)
            nmm = [0]
            for oc in range(NCH):
                wT, wkeys = stream_cols(wm2_2d, oc, oc, f"m2{l}_")
                for s in range(4):
                    bank = self.psb[(oc * 4 + s) % 4]
                    bk = f"psb/{(oc * 4 + s) % 4}"
                    for kc in range(NCH):
                        P.op("tensor", lambda e, oc=oc, s=s, kc=kc, bank=bank, wT=wT: e.matmul(
                            bank[:, :], wT(kc), prod_ap(kc, s * 512, 512),
                            start=(kc == 0), stop=(kc == NCH - 1)),
                            r=wkeys + prod_k(kc, s * 512, 512), w=(bk,), tag=f"m2{l}mm{oc}_{s}")
                    P.op("vector", lambda e, oc=oc, s=s, bank=bank: e.tensor_scalar(
                        out=m2_ap(oc, s * 512, 512), in0=bank[:, :],
                        scalar1=bcol(16 + oc), scalar2=None, op0=Alu.add),
                        r=(bk, "biasT"), w=m2_k(oc, s * 512, 512), tag=f"m2{l}ev{oc}_{s}")
            for oc in range(NCH):
                for s in range(4):
                    i = oc * 4 + s
                    P.op("scalar", lambda e, oc=oc, s=s: e.activation(
                        out=self.SC.ap(BF16, 512, byteoff=0), in_=m2_ap(oc, s * 512, 512),
                        func=Act.Square),
                        r=m2_k(oc, s * 512, 512), w=self.SC.K(0, 1024), tag=f"gnsq{l}_{oc}_{s}")
                    P.op("tensor", lambda e, i=i, oc=oc, s=s: e.matmul(
                        self.ps_st[0:1, 0:512], self.smv("onescol_b", BF16, 1),
                        m2_ap(oc, s * 512, 512), start=(i == 0), stop=(i == 31)),
                        r=m2_k(oc, s * 512, 512) + ("onescol_b",), w=("ps_st",),
                        tag=f"gns{l}_{oc}_{s}")
                    P.op("tensor", lambda e, i=i: e.matmul(
                        self.ps_x[0:1, 0:512], self.smv("onescol_b", BF16, 1),
                        self.SC.ap(BF16, 512, byteoff=0), start=(i == 0), stop=(i == 31)),
                        r=self.SC.K(0, 1024) + ("onescol_b",), w=("ps_x",),
                        tag=f"gnq{l}_{oc}_{s}")
            gs = lambda i: self.smv("srows", F32, 1, part=1, byteoff=i * 4)
            P.op("vector", lambda e: e.reduce_sum(
                out=gs(0), in_=self.ps_st[0:1, 0:512], axis=AX.X, op=Alu.add)
                if False else e.tensor_reduce(
                out=gs(0), in_=self.ps_st[0:1, 0:512], axis=AX.X, op=Alu.add),
                r=("ps_st",), w=("srows",), tag=f"gnr1{l}")
            P.op("vector", lambda e: e.tensor_reduce(
                out=gs(1), in_=self.ps_x[0:1, 0:512], axis=AX.X, op=Alu.add),
                r=("ps_x",), w=("srows",), tag=f"gnr2{l}")
            NT2 = float(T * D)
            P.op("vector", lambda e: e.tensor_scalar(
                out=gs(0), in0=gs(0), scalar1=1.0 / NT2, scalar2=None, op0=Alu.mult),
                r=("srows",), w=("srows",), tag=f"gnmu{l}")
            P.op("vector", lambda e: e.tensor_tensor(
                out=gs(2), in0=gs(0), in1=gs(0), op=Alu.mult),
                r=("srows",), w=("srows",), tag=f"gnmsq{l}")
            P.op("vector", lambda e: e.scalar_tensor_tensor(
                out=gs(1), in0=gs(1), scalar=1.0 / NT2, in1=gs(2),
                op0=Alu.mult, op1=Alu.subtract),
                r=("srows",), w=("srows",), tag=f"gnvar{l}")
            epsap, epsk = self.cc(1e-5)
            P.op("scalar", lambda e, epsap=epsap: e.activation(
                out=gs(1), in_=gs(1), func=Act.Ln, bias=epsap[0:1, 0:1], scale=1.0),
                r=("srows", epsk), w=("srows",), tag=f"gnln{l}")
            P.op("scalar", lambda e: e.activation(
                out=gs(1), in_=gs(1), func=Act.Exp, scale=-0.5),
                r=("srows",), w=("srows",), tag=f"gnrs{l}")
            P.op("tensor", lambda e: e.matmul(
                self.ps_x[:, 0:2], self.smv("ones_1x128f", F32, 128, part=1),
                self.smv("srows", F32, 2, part=1), start=True, stop=True),
                r=("srows", "ones_1x128f"), w=("ps_x",), tag=f"gnbc{l}")
            P.op("vector", lambda e: e.tensor_copy(
                out=self.smv("ecol", F32, 2), in_=self.ps_x[:, 0:2]),
                r=("ps_x",), w=("ecol",), tag=f"gnbe{l}")
            mu_c = lambda: self.smv("ecol", F32, 1)
            rs_c = lambda: self.smv("ecol", F32, 1, byteoff=4)
            P.op("vector", lambda e: e.tensor_scalar(
                out=self.smv("ecol", F32, 8, byteoff=8),
                in0=self.smv("biasT", F32, 8, byteoff=24 * 4),
                scalar1=rs_c(), scalar2=None, op0=Alu.mult),
                r=("biasT", "ecol"), w=("ecol",), tag=f"gnA{l}")
            P.op("vector", lambda e: e.scalar_tensor_tensor(
                out=self.smv("ecol", F32, 8, byteoff=40),
                in0=self.smv("ecol", F32, 8, byteoff=8), scalar=mu_c(),
                in1=self.smv("biasT", F32, 8, byteoff=32 * 4),
                op0=Alu.mult, op1=Alu.subtract),
                r=("ecol", "biasT"), w=("ecol",), tag=f"gnB{l}")
            Acol = lambda c: self.smv("ecol", F32, 1, byteoff=8 + c * 4)
            Bcol = lambda c: self.smv("ecol", F32, 1, byteoff=40 + c * 4)
            y_ap = lambda kc, c0, cols: self.S0.ap(F32, cols, byteoff=kc * 8192 + c0 * 4)
            y_k = lambda kc, c0, cols: self.S0.K(kc * 8192 + c0 * 4, cols * 4)
            for kc in range(NCH):
                for s in range(4):
                    P.op("vector", lambda e, kc=kc, s=s: e.tensor_scalar(
                        out=self.SC.ap(F32, 512, byteoff=2048),
                        in0=m2_ap(kc, s * 512, 512), scalar1=Acol(kc), scalar2=Bcol(kc),
                        op0=Alu.mult, op1=Alu.subtract),
                        r=m2_k(kc, s * 512, 512) + ("ecol",), w=self.SC.K(2048, 2048),
                        tag=f"y{l}a{kc}_{s}")
                    P.op("vector", lambda e, kc=kc, s=s: e.tensor_tensor(
                        out=y_ap(kc, s * 512, 512), in0=self.SC.ap(F32, 512, byteoff=2048),
                        in1=h_ap(kc, s * 512, 512), op=Alu.add),
                        r=self.SC.K(2048, 2048) + h_k(kc, s * 512, 512),
                        w=y_k(kc, s * 512, 512), tag=f"y{l}b{kc}_{s}")

            ybf = lambda kc, c0, cols: self.PB.ap(BF16, cols, byteoff=kc * 4096 + c0 * 2)
            ybf_k = lambda kc, c0, cols: self.PB.K(kc * 4096 + c0 * 2, cols * 2)
            for sp in range(2):
                for sl in range(2):
                    s = sp * 2 + sl
                    for kc in range(NCH):
                        P.op("vector", lambda e, kc=kc, s=s: e.tensor_copy(
                            out=ybf(kc, s * 512, 512), in_=y_ap(kc, s * 512, 512)),
                            r=y_k(kc, s * 512, 512), w=ybf_k(kc, s * 512, 512),
                            tag=f"ybf{l}_{kc}_{s}")
                for fo in range(16):
                    wT, wkeys = stream_cols(wf1_2d, fo, fo, f"f1{l}p{sp}_")
                    for sl in range(2):
                        s = sp * 2 + sl
                        bank = self.psb[2 * (fo % 2) + sl]
                        bk = f"psb/{2 * (fo % 2) + sl}"
                        for kc in range(NCH):
                            P.op("tensor", lambda e, fo=fo, s=s, kc=kc, bank=bank, wT=wT: e.matmul(
                                bank[:, :], wT(kc), ybf(kc, s * 512, 512),
                                start=(kc == 0), stop=(kc == NCH - 1)),
                                r=wkeys + ybf_k(kc, s * 512, 512), w=(bk,), tag=f"f1{l}mm{s}_{fo}")
                        P.op("scalar", lambda e, fo=fo, sl=sl, bank=bank: e.activation(
                            out=self.PA.ap(BF16, 512, byteoff=fo * 2048 + sl * 1024),
                            in_=bank[:, :],
                            func=Act.Gelu, bias=bcol(40 + fo), scale=1.0),
                            r=(bk, "biasT"), w=self.PA.K(fo * 2048 + sl * 1024, 1024),
                            tag=f"f1{l}ev{s}_{fo}")
                for oc in range(NCH):
                    banks = (self.ps_ab[0], self.ps_ab[1])
                    bks = ("ps_ab0", "ps_ab1")
                    for hhalf in range(2):
                        wT, wkeys = stream_cols(
                            wf2_2d[hhalf * 1024:(hhalf + 1) * 1024, :], oc,
                            2 * oc + hhalf, f"f2{l}p{sp}h{hhalf}_")
                        for kc in range(NCH):
                            kci = hhalf * 8 + kc
                            for sl in range(2):
                                P.op("tensor", lambda e, kci=kci, kc=kc, sl=sl, bank=banks[sl],
                                     wT=wT: e.matmul(
                                    bank[:, :], wT(kc),
                                    self.PA.ap(BF16, 512, byteoff=kci * 2048 + sl * 1024),
                                    start=(kci == 0), stop=(kci == 15)),
                                    r=wkeys + self.PA.K(kci * 2048 + sl * 1024, 1024),
                                    w=(bks[sl],), tag=f"f2{l}mm{sp}_{oc}_{kci}_{sl}")
                    for sl in range(2):
                        s = sp * 2 + sl
                        P.op("vector", lambda e, oc=oc, s=s, bank=banks[sl]: e.scalar_tensor_tensor(
                            out=y_ap(oc, s * 512, 512), in0=bank[:, :], scalar=bcol(56 + oc),
                            in1=y_ap(oc, s * 512, 512), op0=Alu.add, op1=Alu.add),
                            r=(bks[sl], "biasT") + y_k(oc, s * 512, 512),
                            w=y_k(oc, s * 512, 512), tag=f"f2{l}ev{s}_{oc}")
            self.ln_fm(f"iln{l}", self.S0, lambda c: c * 8192,
                       self.S1, lambda c: c * 8192, F32,
                       scratch=(self.SC, 0),
                       gamma=lambda c: self.smv("biasT", F32, 1, byteoff=(64 + c) * 4),
                       beta=lambda c: self.smv("biasT", F32, 1, byteoff=(72 + c) * 4),
                       gb_keys=("biasT",))

        if upto == "h":
            return
        for c in range(NCH):
            P.op("sync", lambda e, c=c: e.dma_start(
                out=self.S0.ap(F32, T, byteoff=c * 8192), in_=self.xspill_h.ap()[c]),
                w=self.S0.K(c * 8192, 8192), dma=True, tag=f"xrs{c}")
            for s in range(4):
                P.op("vector", lambda e, c=c, s=s: e.tensor_tensor(
                    out=self.S0.ap(F32, 512, byteoff=c * 8192 + s * 2048),
                    in0=self.S0.ap(F32, 512, byteoff=c * 8192 + s * 2048),
                    in1=self.S1.ap(F32, 512, byteoff=c * 8192 + s * 2048), op=Alu.add),
                    r=self.S0.K(c * 8192 + s * 2048, 2048) + self.S1.K(c * 8192 + s * 2048, 2048),
                    w=self.S0.K(c * 8192 + s * 2048, 2048), tag=f"xadd{c}_{s}")

    def quant_chain(self, x_ap, x_keys, rdq2_col, dqh_col, col_keys,
                    out_ap, out_keys, scr, scr_off, tag, ncols=1024):
        P = self.P
        nb = ncols // 16
        CM = float(3 * 2 ** 22)
        buf = lambda i: scr.ap(F32, ncols, byteoff=scr_off + i * 4096)
        bk = lambda i: scr.K(scr_off + i * 4096, 4096)
        kbuf = lambda i: scr.ap(BF16, ncols, byteoff=scr_off + 16384 + i * 2048)
        kk = lambda i: scr.K(scr_off + 16384 + i * 2048, 2048)
        g3 = lambda ap_: ap_.rearrange("p (g s) -> p g s", s=16)

        def bc3(col_ap):
            import concourse.bass as _b
            pd = list(col_ap.ap[0])
            return _b.AP(col_ap.tensor, col_ap.offset, [pd, [1, nb], [0, 16]])

        P.op("vector", lambda e: e.tensor_tensor(
            out=g3(buf(0)), in0=g3(x_ap), in1=bc3(rdq2_col), op=Alu.mult),
            r=x_keys + col_keys, w=bk(0), tag=f"{tag}u")
        P.op("scalar", lambda e: e.activation(out=buf(1), in_=buf(0), func=Act.Abs),
             r=bk(0), w=bk(1), tag=f"{tag}abs")
        P.op("vector", lambda e: e.tensor_scalar(
            out=buf(2), in0=buf(1), scalar1=4.0, scalar2=0.0, op0=Alu.subtract, op1=Alu.max),
            r=bk(1), w=bk(2), tag=f"{tag}r1")
        P.op("vector", lambda e: e.tensor_scalar(
            out=buf(3), in0=buf(1), scalar1=8.0, scalar2=0.0, op0=Alu.subtract, op1=Alu.max),
            r=bk(1), w=bk(3), tag=f"{tag}r2")
        P.op("vector", lambda e: e.scalar_tensor_tensor(
            out=buf(2), in0=buf(3), scalar=0.5, in1=buf(2), op0=Alu.mult, op1=Alu.add),
            r=bk(2) + bk(3), w=bk(2), tag=f"{tag}w2")
        P.op("vector", lambda e: e.scalar_tensor_tensor(
            out=buf(0), in0=buf(2), scalar=-0.5, in1=buf(1), op0=Alu.mult, op1=Alu.add),
            r=bk(1) + bk(2), w=bk(0), tag=f"{tag}g")
        P.op("vector", lambda e: e.tensor_scalar(
            out=buf(1), in0=buf(0), scalar1=CM, scalar2=CM, op0=Alu.add, op1=Alu.subtract),
            r=bk(0), w=bk(1), tag=f"{tag}k")
        P.op("vector", lambda e: e.tensor_scalar(
            out=kbuf(0), in0=buf(1), scalar1=7.0, scalar2=0.0, op0=Alu.min, op1=Alu.max),
            r=bk(1), w=kk(0), tag=f"{tag}kc")
        P.op("vector", lambda e: e.tensor_scalar(
            out=kbuf(1), in0=kbuf(0), scalar1=4.0, scalar2=0.0, op0=Alu.subtract, op1=Alu.max),
            r=kk(0), w=kk(1), tag=f"{tag}s1")
        P.op("vector", lambda e: e.tensor_scalar(
            out=kbuf(2), in0=kbuf(0), scalar1=6.0, scalar2=0.0, op0=Alu.subtract, op1=Alu.max),
            r=kk(0), w=kk(2), tag=f"{tag}s2")
        P.op("vector", lambda e: e.scalar_tensor_tensor(
            out=kbuf(1), in0=kbuf(2), scalar=2.0, in1=kbuf(1), op0=Alu.mult, op1=Alu.add),
            r=kk(1) + kk(2), w=kk(1), tag=f"{tag}t")
        P.op("gpsimd", lambda e: e.tensor_tensor(
            out=kbuf(0), in0=kbuf(1), in1=kbuf(0), op=Alu.add),
            r=kk(0) + kk(1), w=kk(0), tag=f"{tag}q2")
        P.op("scalar", lambda e: e.activation(out=kbuf(2), in_=x_ap, func=Act.Sign),
             r=x_keys, w=kk(2), tag=f"{tag}sg")
        P.op("vector", lambda e: e.tensor_tensor(
            out=g3(buf(2)), in0=g3(kbuf(0)), in1=bc3(dqh_col), op=Alu.mult),
            r=kk(0) + col_keys, w=bk(2), tag=f"{tag}vd")
        P.op("vector", lambda e: e.tensor_tensor(
            out=out_ap, in0=buf(2), in1=kbuf(2), op=Alu.mult),
            r=bk(2) + kk(2), w=out_keys, tag=f"{tag}out")

    def amax16(self, src_ap, src_keys, am_ap, am_keys, tag, ncols=1024):
        g3 = lambda ap_: ap_.rearrange("p (g s) -> p g s", s=16)
        self.P.op("vector", lambda e: e.tensor_reduce(
            out=am_ap.unsqueeze(2), in_=g3(src_ap), axis=AX.X, op=Alu.max,
            apply_absolute_value=True),
            r=src_keys, w=am_keys, tag=f"{tag}am")

    def scales_from_amax(self, am_ap, am_keys, gs_off, rdq2_col, dqh_col, col_keys,
                         scr, scr_off, tag, nb=64):
        P = self.P
        CM = float(3 * 2 ** 22)
        sc = scr.ap(F32, nb, byteoff=scr_off)
        sck = scr.K(scr_off, nb * 4)
        Sb = scr.ap(F32, nb, byteoff=scr_off + 256)
        Sbk = scr.K(scr_off + 256, nb * 4)
        mkb = scr.ap(U8, nb, byteoff=scr_off + 512)
        mkk = scr.K(scr_off + 512, nb)
        hb = scr.ap(F32, nb, byteoff=scr_off + 576)
        hbk = scr.K(scr_off + 576, nb * 4)
        f8 = scr.ap(FP8, nb, byteoff=scr_off + 832)
        f8k = scr.K(scr_off + 832, nb)
        s8 = scr.ap(F32, nb, byteoff=scr_off + 896)
        s8k = scr.K(scr_off + 896, nb * 4)
        gcol = lambda i: self.smv("ecol", F32, 1, byteoff=gs_off + i * 4)
        P.op("vector", lambda e: e.tensor_scalar(
            out=sc, in0=am_ap, scalar1=gcol(0), scalar2=448.0, op0=Alu.mult, op1=Alu.min),
            r=am_keys + ("ecol",), w=sck, tag=f"{tag}sc")
        P.op("vector", lambda e: e.tensor_single_scalar(
            out=mkb, in_=sc, scalar=0.03125, op=Alu.is_ge),
            r=sck, w=mkk, tag=f"{tag}mk")
        P.op("vector", lambda e: e.tensor_scalar(
            out=Sb, in0=sc, scalar1=512.0, scalar2=CM, op0=Alu.mult, op1=Alu.add),
            r=sck, w=Sbk, tag=f"{tag}S1")
        P.op("vector", lambda e: e.tensor_scalar(
            out=Sb, in0=Sb, scalar1=CM, scalar2=1.0 / 512.0, op0=Alu.subtract, op1=Alu.mult),
            r=Sbk, w=Sbk, tag=f"{tag}S2")
        P.op("vector", lambda e: e.tensor_scalar(
            out=hb, in0=sc, scalar1=0.5, scalar2=None, op0=Alu.mult),
            r=sck, w=hbk, tag=f"{tag}h")
        P.op("vector", lambda e: e.tensor_copy(out=f8, in_=hb), r=hbk, w=f8k, tag=f"{tag}f8")
        P.op("vector", lambda e: e.tensor_copy(out=s8, in_=f8), r=f8k, w=s8k, tag=f"{tag}f8b")
        P.op("vector", lambda e: e.tensor_scalar(
            out=s8, in0=s8, scalar1=2.0, scalar2=None, op0=Alu.mult),
            r=s8k, w=s8k, tag=f"{tag}N")
        P.op("vector", lambda e: e.select(out=hb, mask=mkb, on_true=s8, on_false=Sb, add_drain=True),
             r=s8k + mkk + Sbk, w=hbk, tag=f"{tag}sel")
        P.op("vector", lambda e: e.tensor_scalar(
            out=sc, in0=hb, scalar1=1e-30, scalar2=None, op0=Alu.max),
            r=hbk, w=sck, tag=f"{tag}mx")
        P.op("vector", lambda e: e.reciprocal(out=sc, in_=sc), r=sck, w=sck, tag=f"{tag}rc")
        P.op("vector", lambda e, rdq2_col=rdq2_col: e.tensor_scalar(
            out=rdq2_col, in0=sc, scalar1=gcol(1), scalar2=None, op0=Alu.mult),
            r=sck + ("ecol",), w=col_keys, tag=f"{tag}rdq")
        P.op("vector", lambda e, dqh_col=dqh_col: e.tensor_scalar(
            out=dqh_col, in0=hb, scalar1=gcol(2), scalar2=None, op0=Alu.mult),
            r=hbk + ("ecol",), w=col_keys, tag=f"{tag}dqh")

    def make_gs_cols(self, gs_src_ap, gs_src_keys, gs_off, tag):
        P = self.P
        t = lambda i: self.smv("srows", F32, 1, part=1, byteoff=32 + i * 4)
        P.op("vector", lambda e: e.tensor_scalar(
            out=t(0), in0=gs_src_ap, scalar1=1.0 / 6.0, scalar2=None, op0=Alu.mult),
            r=gs_src_keys, w=("srows",), tag=f"{tag}a")
        P.op("vector", lambda e: e.tensor_scalar(
            out=t(1), in0=gs_src_ap, scalar1=2.0, scalar2=None, op0=Alu.mult),
            r=gs_src_keys, w=("srows",), tag=f"{tag}b")
        P.op("vector", lambda e: e.reciprocal(out=t(2), in_=t(1)),
             r=("srows",), w=("srows",), tag=f"{tag}c")
        P.op("tensor", lambda e: e.matmul(
            self.ps_x[:, 0:3], self.smv("ones_1x128f", F32, 128, part=1),
            self.smv("srows", F32, 3, part=1, byteoff=32), start=True, stop=True),
            r=("srows", "ones_1x128f"), w=("ps_x",), tag=f"{tag}d")
        P.op("vector", lambda e: e.tensor_copy(
            out=self.smv("ecol", F32, 3, byteoff=gs_off), in_=self.ps_x[:, 0:3]),
            r=("ps_x",), w=("ecol",), tag=f"{tag}e")

    def stage_l1_ffn(self, upto='all'):
        P = self.P
        for nm, off in (("f1_sin", 64), ("f1_sind", 76)):
            P.op("sync", lambda e, nm=nm: e.dma_start(
                out=self.smv("srows", F32, 1, part=1, byteoff=28), in_=self.inp[nm].ap()),
                w=("srows",), dma=True, tag=f"ld{nm}")
            self.make_gs_cols(self.smv("srows", F32, 1, part=1, byteoff=28),
                              ("srows",), off, f"gs{nm}")

        def quant_weight(hbm, n_o, n_i, outT_h, wtag):
            nsl = n_o // 128
            nsub = n_i // 1024
            gmax = self.smv("ecol", F32, 1, byteoff=108)
            smax = self.smv("ecol", F32, 1, byteoff=104)
            for sl in range(nsl):
                wf = self.WP.ap(F32, n_i, byteoff=0) if n_i <= 1024 else None
                if n_i <= 1024:
                    kf = self.WP.K(0, n_i * 4)
                    P.op("sync", lambda e, sl=sl, wf=wf, hbm=hbm: e.dma_start(
                        out=wf, in_=hbm.ap()[sl * 128:(sl + 1) * 128, :]),
                        w=kf, dma=True, tag=f"{wtag}Adma{sl}")
                    P.op("vector", lambda e, wf=wf: e.tensor_reduce(
                        out=smax, in_=wf, axis=AX.X, op=Alu.max,
                        apply_absolute_value=True),
                        r=kf, w=("ecol",), tag=f"{wtag}Ared{sl}")
                    ops = [0]
                else:
                    for sub in range(nsub):
                        wfs = self.WP.ap(F32, 1024, byteoff=(sub % 2) * 4096)
                        kfs = self.WP.K((sub % 2) * 4096, 4096)
                        P.op("sync", lambda e, sl=sl, sub=sub, wfs=wfs, hbm=hbm: e.dma_start(
                            out=wfs, in_=hbm.ap()[sl * 128:(sl + 1) * 128,
                                                  sub * 1024:(sub + 1) * 1024]),
                            w=kfs, dma=True, tag=f"{wtag}Adma{sl}_{sub}")
                        if sub == 0:
                            P.op("vector", lambda e, wfs=wfs: e.tensor_reduce(
                                out=smax, in_=wfs, axis=AX.X, op=Alu.max,
                                apply_absolute_value=True),
                                r=kfs, w=("ecol",), tag=f"{wtag}Ared{sl}_{sub}")
                        else:
                            P.op("vector", lambda e, wfs=wfs: e.tensor_reduce(
                                out=self.smv("ecol", F32, 1, byteoff=100), in_=wfs,
                                axis=AX.X, op=Alu.max, apply_absolute_value=True),
                                r=kfs, w=("ecol",), tag=f"{wtag}Ared{sl}_{sub}")
                            P.op("vector", lambda e: e.tensor_tensor(
                                out=smax, in0=smax,
                                in1=self.smv("ecol", F32, 1, byteoff=100), op=Alu.max),
                                r=("ecol",), w=("ecol",), tag=f"{wtag}Am{sl}_{sub}")
                if sl == 0:
                    P.op("vector", lambda e: e.tensor_copy(out=gmax, in_=smax),
                         r=("ecol",), w=("ecol",), tag=f"{wtag}Ainit")
                else:
                    P.op("vector", lambda e: e.tensor_tensor(
                        out=gmax, in0=gmax, in1=smax, op=Alu.max),
                        r=("ecol",), w=("ecol",), tag=f"{wtag}Aacc{sl}")
            P.op("tensor", lambda e: e.transpose(
                self.ps_x[0:1, 0:128], gmax, self.smv("id128f", F32, 128)),
                r=("ecol", "id128f"), w=("ps_x",), tag=f"{wtag}Atr")
            gsr = self.smv("srows", F32, 1, part=1, byteoff=28)
            P.op("vector", lambda e, gsr=gsr: e.tensor_reduce(
                out=gsr, in_=self.ps_x[0:1, 0:128], axis=AX.X, op=Alu.max),
                r=("ps_x",), w=("srows",), tag=f"{wtag}Ag")
            P.op("vector", lambda e, gsr=gsr: e.reciprocal(out=gsr, in_=gsr),
                 r=("srows",), w=("srows",), tag=f"{wtag}Arc")
            P.op("vector", lambda e, gsr=gsr: e.tensor_scalar(
                out=gsr, in0=gsr, scalar1=2688.0, scalar2=None, op0=Alu.mult),
                r=("srows",), w=("srows",), tag=f"{wtag}Ascale")
            self.make_gs_cols(gsr, ("srows",), 88, f"{wtag}gsc")
            for sl in range(nsl):
                for sub in range(nsub):
                    wfs = self.WP.ap(F32, 1024, byteoff=(sub % 2) * 4096)
                    kfs = self.WP.K((sub % 2) * 4096, 4096)
                    P.op("sync", lambda e, sl=sl, sub=sub, wfs=wfs, hbm=hbm: e.dma_start(
                        out=wfs, in_=hbm.ap()[sl * 128:(sl + 1) * 128,
                                              sub * 1024:(sub + 1) * 1024]),
                        w=kfs, dma=True, tag=f"{wtag}Bdma{sl}_{sub}")
                    am = self.S1.ap(F32, 64, byteoff=28672)
                    amk = self.S1.K(28672, 256)
                    self.amax16(wfs, kfs, am, amk, f"{wtag}B{sl}_{sub}")
                    rq = self.S1.ap(F32, 64, byteoff=29696)
                    dq = self.S1.ap(F32, 64, byteoff=29952)
                    ck = self.S1.K(29696, 512)
                    self.scales_from_amax(am, amk, 88, rq, dq, ck, self.S1, 24576,
                                          f"{wtag}S{sl}_{sub}")
                    qout = self.S1.ap(BF16, 1024, byteoff=45056 + sub * 2048)
                    qok = self.S1.K(45056 + sub * 2048, 2048)
                    self.quant_chain(wfs, kfs, rq, dq, ck, qout, qok,
                                     self.S1, 0, f"{wtag}Q{sl}_{sub}")
                for ic in range(n_i // 128):
                    trb, trk = self.trbanks[ic % 4]
                    P.op("tensor", lambda e, ic=ic, trb=trb: e.transpose(
                        trb[:, 0:512].bitcast(BF16)[:, 0:128],
                        self.S1.ap(BF16, 128, byteoff=45056 + ic * 256),
                        self.smv("id128b", BF16, 128)),
                        r=self.S1.K(45056 + ic * 256, 256) + ("id128b",),
                        w=(trk,), tag=f"{wtag}tr{sl}_{ic}")
                    P.op("vector", lambda e, ic=ic, trb=trb: e.tensor_copy(
                        out=self.PA.ap(BF16, 128, byteoff=ic * 256),
                        in_=trb[:, 0:512].bitcast(BF16)[:, 0:128]),
                        r=(trk,), w=self.PA.K(ic * 256, 256), tag=f"{wtag}te{sl}_{ic}")
                P.op("sync", lambda e, sl=sl, outT_h=outT_h, n_i=n_i: e.dma_start(
                    out=outT_h.ap()[:, sl * 128:(sl + 1) * 128].rearrange(
                        "(ic p) m -> p ic m", p=128),
                    in_=self.PA.ap(BF16, n_i, byteoff=0).rearrange(
                        "p (ic m) -> p ic m", m=128)),
                    r=self.PA.K(0, n_i * 2), dma=True, tag=f"{wtag}st{sl}")

        if upto != "nowq":
            quant_weight(self.inp["f1_gw"], 4096, D, self.gwT_h, "qg")
            quant_weight(self.inp["f1_uw"], 4096, D, self.uwT_h, "qu")
            quant_weight(self.inp["f1_dw"], D, 4096, self.dwT_h, "qd")
        if upto == "wq":
            return

        self.ln_fm("ln3", self.S0, lambda c: c * 8192, self.S1, lambda c: c * 8192,
                   F32, scratch=(self.PB, 0))
        l3t = lambda tt, c0, cols: (self.PA if tt < 8 else self.PB).ap(
            F32, cols, byteoff=(tt % 8) * 4096 + c0 * 4)
        l3t_k = lambda tt, c0, cols: (self.PA if tt < 8 else self.PB).K(
            (tt % 8) * 4096 + c0 * 4, cols * 4)
        for tt in range(NTT):
            for c in range(NCH):
                trb, trk = self.trbanks[(tt * NCH + c) % 4]
                P.op("tensor", lambda e, tt=tt, c=c, trb=trb: e.transpose(
                    trb[:, 0:128], self.S1.ap(F32, 128, byteoff=c * 8192 + tt * 512),
                    self.smv("id128f", F32, 128)),
                    r=self.S1.K(c * 8192 + tt * 512, 512) + ("id128f",),
                    w=(trk,), tag=f"l3t{tt}_{c}")
                P.op("vector", lambda e, tt=tt, c=c, trb=trb: e.tensor_copy(
                    out=l3t(tt, c * 128, 128), in_=trb[:, 0:128]),
                    r=(trk,), w=l3t_k(tt, c * 128, 128), tag=f"l3e{tt}_{c}")
        xqfm = lambda kc, c0, cols: self.S1.ap(BF16, cols, byteoff=kc * 4096 + c0 * 2)
        xqfm_k = lambda kc, c0, cols: self.S1.K(kc * 4096 + c0 * 2, cols * 2)
        for tt in range(NTT):
            am = self.S1.ap(F32, 64, byteoff=55296)
            amk = self.S1.K(55296, 256)
            self.amax16(l3t(tt, 0, 1024), l3t_k(tt, 0, 1024), am, amk, f"xq{tt}")
            rq = self.S1.ap(F32, 64, byteoff=63488)
            dq = self.S1.ap(F32, 64, byteoff=63744)
            ck = self.S1.K(63488, 512)
            self.scales_from_amax(am, amk, 64, rq, dq, ck, self.S1, 56320, f"xqs{tt}")
            xqt = self.S1.ap(BF16, 1024, byteoff=59392 + (tt % 2) * 2048)
            xqt_k = self.S1.K(59392 + (tt % 2) * 2048, 2048)
            self.quant_chain(l3t(tt, 0, 1024), l3t_k(tt, 0, 1024), rq, dq, ck,
                             xqt, xqt_k, self.S1, 32768, f"xqc{tt}")
            for c in range(NCH):
                trb, trk = self.trbanks[c % 4]
                P.op("tensor", lambda e, tt=tt, c=c, xqt=xqt, trb=trb: e.transpose(
                    trb[:, 0:512].bitcast(BF16)[:, 0:128],
                    xqt[:, c * 128:(c + 1) * 128], self.smv("id128b", BF16, 128)),
                    r=xqt_k + ("id128b",), w=(trk,), tag=f"xqt{tt}_{c}")
                P.op("vector", lambda e, tt=tt, c=c, trb=trb: e.tensor_copy(
                    out=xqfm(c, tt * 128, 128), in_=trb[:, 0:512].bitcast(BF16)[:, 0:128]),
                    r=(trk,), w=xqfm_k(c, tt * 128, 128), tag=f"xqe{tt}_{c}")

        if upto in ("xq", "nowq_xq"):
            return
        hid_bf = lambda tl, c0, cols: self.PB.ap(BF16, cols, byteoff=tl * 8192 + c0 * 2)
        hid_bf_k = lambda tl, c0, cols: self.PB.K(tl * 8192 + c0 * 2, cols * 2)
        hqfm = lambda hc, tl, c0, cols: self.S1.ap(BF16, cols, byteoff=32768 + hc * 1024 + tl * 256 + c0 * 2)
        hqfm_k = lambda hc, tl, c0, cols: self.S1.K(32768 + hc * 1024 + tl * 256 + c0 * 2, cols * 2)
        for st in range(1 if upto == "strip0" else 4):
            for hs in range(8):
                for w_i, (wh, wnm) in enumerate(((self.gwT_h, "g"), (self.uwT_h, "u"))):
                    wbuf = self.WP if (hs * 2 + w_i) % 2 == 0 else self.SC
                    for hf in range(2):
                        P.op("sync", lambda e, hs=hs, hf=hf, wh=wh, wbuf=wbuf: e.dma_start(
                            out=wbuf.ap(BF16, 2048, byteoff=hf * 4096).rearrange(
                                "p (ic m) -> p ic m", ic=4),
                            in_=wh.ap()[hf * 512:(hf + 1) * 512,
                                        hs * 512:(hs + 1) * 512].rearrange(
                                "(ic p) m -> p ic m", p=128)),
                            w=wbuf.K(hf * 4096, 4096), dma=True,
                            tag=f"w{wnm}{st}_{hs}_{hf}")
                    for tl in range(4):
                        tt = st * 4 + tl
                        bank = self.psb[tl]
                        bk = f"psb/{tl}"
                        for ic in range(NCH):
                            hf = ic // 4
                            icl = ic % 4
                            P.op("tensor", lambda e, tt=tt, hs=hs, ic=ic, hf=hf, icl=icl,
                                 bank=bank, wbuf=wbuf: e.matmul(
                                 bank[:, :], xqfm(ic, tt * 128, 128),
                                 wbuf.ap(BF16, 512, byteoff=hf * 4096 + icl * 1024),
                                 start=(ic == 0), stop=(ic == NCH - 1)),
                                 r=xqfm_k(ic, tt * 128, 128) + wbuf.K(hf * 4096 + icl * 1024, 1024),
                                 w=(bk,), tag=f"{wnm}mm{st}_{hs}_{tl}")
                        if w_i == 0:
                            P.op("scalar", lambda e, tl=tl, bank=bank: e.activation(
                                out=self.PA.ap(F32, 512, byteoff=tl * 2048),
                                in_=bank[:, :], func=Act.Silu),
                                r=(bk,), w=self.PA.K(tl * 2048, 2048),
                                tag=f"sil{st}_{hs}_{tl}")
                        else:
                            P.op("vector", lambda e, tl=tl, bank=bank: e.tensor_tensor(
                                out=self.PA.ap(F32, 512, byteoff=8192),
                                in0=bank[:, :], in1=self.PA.ap(F32, 512, byteoff=tl * 2048),
                                op=Alu.mult),
                                r=(bk,) + self.PA.K(tl * 2048, 2048),
                                w=self.PA.K(8192, 2048), tag=f"hid{st}_{hs}_{tl}")
                            P.op("vector", lambda e, tl=tl, hs=hs: e.tensor_reduce(
                                out=self.PA.ap(F32, 32, byteoff=24576 + tl * 1024 + hs * 128).unsqueeze(2),
                                in_=self.PA.ap(F32, 512, byteoff=8192).rearrange(
                                    "p (g s) -> p g s", s=16),
                                axis=AX.X, op=Alu.max, apply_absolute_value=True),
                                r=self.PA.K(8192, 2048),
                                w=self.PA.K(24576 + tl * 1024 + hs * 128, 128),
                                tag=f"ham{st}_{hs}_{tl}")
                            P.op("vector", lambda e, tl=tl, hs=hs: e.tensor_copy(
                                out=hid_bf(tl, hs * 512, 512),
                                in_=self.PA.ap(F32, 512, byteoff=8192)),
                                r=self.PA.K(8192, 2048), w=hid_bf_k(tl, hs * 512, 512),
                                tag=f"hbf{st}_{hs}_{tl}")
            for tl in range(4):
                for sub in range(4):
                    am = self.PA.ap(F32, 64, byteoff=24576 + tl * 1024 + sub * 256)
                    amk = self.PA.K(24576 + tl * 1024 + sub * 256, 256)
                    rq = self.PA.ap(F32, 64, byteoff=28928)
                    dq = self.PA.ap(F32, 64, byteoff=29184)
                    ck = self.PA.K(28928, 512)
                    self.scales_from_amax(am, amk, 76, rq, dq, ck, self.PA, 22528,
                                          f"hs{st}_{tl}_{sub}")
                    self.quant_chain(hid_bf(tl, sub * 1024, 1024),
                                     hid_bf_k(tl, sub * 1024, 1024), rq, dq, ck,
                                     hid_bf(tl, sub * 1024, 1024),
                                     hid_bf_k(tl, sub * 1024, 1024),
                                     self.PA, 0, f"hq{st}_{tl}_{sub}")
                for hc in range(32):
                    trb, trk = self.trbanks[hc % 4]
                    P.op("tensor", lambda e, tl=tl, hc=hc, trb=trb: e.transpose(
                        trb[:, 0:512].bitcast(BF16)[:, 0:128],
                        hid_bf(tl, hc * 128, 128), self.smv("id128b", BF16, 128)),
                        r=hid_bf_k(tl, hc * 128, 128) + ("id128b",), w=(trk,),
                        tag=f"hqt{st}_{tl}_{hc}")
                    P.op("vector", lambda e, tl=tl, hc=hc, trb=trb: e.tensor_copy(
                        out=hqfm(hc, tl, 0, 128),
                        in_=trb[:, 0:512].bitcast(BF16)[:, 0:128]),
                        r=(trk,), w=hqfm_k(hc, tl, 0, 128), tag=f"hqe{st}_{tl}_{hc}")
            if upto == "strip0":
                return
            for dc in range(NCH):
                wbuf = self.SC if dc % 2 == 0 else self.WP
                for hf in range(2):
                    P.op("sync", lambda e, dc=dc, hf=hf, wbuf=wbuf: e.dma_start(
                        out=wbuf.ap(BF16, 2048, byteoff=hf * 4096).rearrange(
                            "p (ic m) -> p ic m", ic=16),
                        in_=self.dwT_h.ap()[hf * 2048:(hf + 1) * 2048,
                                            dc * 128:(dc + 1) * 128].rearrange(
                            "(ic p) m -> p ic m", p=128)),
                        w=wbuf.K(hf * 4096, 4096), dma=True, tag=f"dw{st}_{dc}_{hf}")
                bank = self.ps_ab[dc % 2]
                bk = f"ps_ab{dc % 2}"
                for hc in range(32):
                    hf = hc // 16
                    hcl = hc % 16
                    P.op("tensor", lambda e, st=st, dc=dc, hc=hc, hf=hf, hcl=hcl,
                         bank=bank, wbuf=wbuf: e.matmul(
                         bank[:, :], wbuf.ap(BF16, 128, byteoff=hf * 4096 + hcl * 256),
                         hqfm(hc, 0, 0, 512), start=(hc == 0), stop=(hc == 31)),
                         r=wbuf.K(hf * 4096 + hcl * 256, 256) + self.S1.K(32768 + hc * 1024, 1024),
                         w=(bk,), tag=f"dmm{st}_{dc}_{hc}")
                P.op("vector", lambda e, st=st, dc=dc, bank=bank: e.tensor_tensor(
                    out=self.S0.ap(F32, 512, byteoff=dc * 8192 + st * 2048),
                    in0=bank[:, :], in1=self.S0.ap(F32, 512, byteoff=dc * 8192 + st * 2048),
                    op=Alu.add),
                    r=(bk,) + self.S0.K(dc * 8192 + st * 2048, 2048),
                    w=self.S0.K(dc * 8192 + st * 2048, 2048), tag=f"dev{st}_{dc}")

    def debug_out_fm(self, region, chunk_off, dtype=F32, scale=1.0):
        P = self.P
        dsz = {F32: 4, BF16: 2}[dtype]
        for tt in range(NTT):
            for c in range(NCH):
                rk = region.K(chunk_off(c) + tt * 128 * dsz, 128 * dsz)
                if dtype == F32:
                    P.op("tensor", lambda e, tt=tt, c=c: e.transpose(
                        self.ps_x[:, 0:128],
                        region.ap(F32, 128, byteoff=chunk_off(c) + tt * 512),
                        self.smv("id128f", F32, 128)),
                        r=rk + ("id128f",), w=("ps_x",), tag=f"dbgt{tt}_{c}")
                    src2 = self.ps_x[:, 0:128]
                else:
                    P.op("tensor", lambda e, tt=tt, c=c: e.transpose(
                        self.ps_x[:, 0:512].bitcast(BF16)[:, 0:128],
                        region.ap(BF16, 128, byteoff=chunk_off(c) + tt * 256),
                        self.smv("id128b", BF16, 128)),
                        r=rk + ("id128b",), w=("ps_x",), tag=f"dbgt{tt}_{c}")
                    src2 = self.ps_x[:, 0:512].bitcast(BF16)[:, 0:128]
                P.op("vector", lambda e, tt=tt, c=c, src2=src2: e.tensor_scalar(
                    out=self.WP.ap(F32, 128, byteoff=c * 512), in0=src2,
                    scalar1=float(scale), scalar2=None, op0=Alu.mult),
                    r=("ps_x",), w=self.WP.K(c * 512, 512), tag=f"dbge{tt}_{c}")
            tok = P.op("sync", lambda e, tt=tt: e.dma_start(
                out=self.y_h.ap()[tt * 128:(tt + 1) * 128, :],
                in_=self.WP.ap(F32, D, byteoff=0)),
                r=self.WP.K(0, 4096), dma=True, tag=f"dbgs{tt}")
            P.final_waits.add(tok)

    def debug_out_tok(self, region, tile_off, dtype=BF16):
        P = self.P
        dsz = {F32: 4, BF16: 2}[dtype]
        for tt in range(NTT):
            P.op("vector", lambda e, tt=tt: e.tensor_copy(
                out=self.WP.ap(F32, D, byteoff=0),
                in_=region.ap(dtype, D, byteoff=tile_off(tt))),
                r=region.K(tile_off(tt), D * dsz), w=self.WP.K(0, 4096), tag=f"dtk{tt}")
            tok = P.op("sync", lambda e, tt=tt: e.dma_start(
                out=self.y_h.ap()[tt * 128:(tt + 1) * 128, :],
                in_=self.WP.ap(F32, D, byteoff=0)),
                r=self.WP.K(0, 4096), dma=True, tag=f"dtks{tt}")
            P.final_waits.add(tok)

    def finalize(self, listing=None):
        return self.P.finalize(listing=listing)


def in_map_for_core(ins, core):
    return {
        "x": np.ascontiguousarray(ins["L0_attn_x"][core]),
        "l0_wq": ins["L0_attn_Wq"], "l0_wk": ins["L0_attn_Wk"],
        "l0_wv": ins["L0_attn_Wv"], "l0_wo": ins["L0_attn_Wo"],
        "l0_wa": ins["L0_attn_Wa"], "l0_ba": np.ascontiguousarray(ins["L0_attn_ba"].reshape(1, 16)),
        "l0f_wg": ins["L0_ffn_w_gate"], "l0f_wu": ins["L0_ffn_w_up"],
        "l0f_wd": ins["L0_ffn_w_down"],
        "l1_wkqv": ins["L1_attn_Wkqv"], "l1_wm1": ins["L1_attn_Wm1"],
        "l1_bm1": ins["L1_attn_bm1"], "l1_wm2": ins["L1_attn_Wm2"],
        "l1_bm2": ins["L1_attn_bm2"], "l1_gng": ins["L1_attn_gn_g"],
        "l1_gnb": ins["L1_attn_gn_b"], "l1_wf1": ins["L1_attn_Wf1"],
        "l1_bf1": ins["L1_attn_bf1"], "l1_wf2": ins["L1_attn_Wf2"],
        "l1_bf2": ins["L1_attn_bf2"], "l1_lng": ins["L1_attn_ln_g"],
        "l1_lnb": ins["L1_attn_ln_b"],
        "f1_gw": ins["L1_ffn_gate_w"], "f1_uw": ins["L1_ffn_up_w"],
        "f1_dw": ins["L1_ffn_down_w"],
        "f1_sin": np.ascontiguousarray(ins["L1_ffn_s_in"].reshape(1, 1)),
        "f1_sind": np.ascontiguousarray(ins["L1_ffn_s_in_down"].reshape(1, 1)),
    }


def _build():
    m = MK()
    m.stage_load_x()
    m.ln_fm("ln0", m.S0, lambda c: c * 8192, m.S1, lambda c: c * 4096, BF16,
            scratch=(m.S1, 32768 + 8192))
    m.stage_l0_attn()
    m.ln_fm("ln1", m.S0, lambda c: c * 8192, m.S1, lambda c: c * 4096, BF16,
            scratch=(m.S1, 32768 + 8192))
    m.stage_l0_ffn()
    m.ln_fm("ln2", m.S0, lambda c: c * 8192, m.S1, lambda c: c * 8192, F32,
            scratch=(m.PB, 0))
    m.stage_l1_attn()
    m.stage_l1_ffn()
    m.debug_out_fm(m.S0, lambda c: c * 8192, dtype=F32)
    m.finalize()
    return m


def kernel(**inputs):
    from concourse.bass_utils import run_bass_kernel_spmd
    m = _build()
    in_maps = [in_map_for_core(inputs, core) for core in range(8)]
    res = run_bass_kernel_spmd(m.nc, in_maps, core_ids=list(range(8)))
    out = np.stack([np.asarray(res.results[c]["y"]) for c in range(8)], axis=0)
    return out.astype(np.float32)


# version 07 (final)
# speedup vs baseline: 1.1150x; 1.0044x over previous
import numpy as np
import ml_dtypes
from contextlib import ExitStack
import concourse.bass as bass
import concourse.mybir as mybir

F32 = mybir.dt.float32
BF16 = mybir.dt.bfloat16
I32 = mybir.dt.int32
U8 = mybir.dt.uint8
FP8 = mybir.dt.float8e4
Alu = mybir.AluOpType
Act = mybir.ActivationFunctionType
AX = mybir.AxisListType
BF16NP = ml_dtypes.bfloat16

T = 2048
D = 1024
NCH = 8
NTT = 16

COMPUTE = ("tensor", "vector", "scalar", "gpsimd")
DMA_Q = {"sync": "q_sync", "gpsimd": "q_gps", "scalar": "q_act", "vector": "q_dve", "tensor": "q_pe"}


class Prog:
    def __init__(self, nc):
        self.nc = nc
        self.seq = {e: [] for e in ("tensor", "vector", "scalar", "gpsimd", "sync")}
        self.cnt = {e: 0 for e in COMPUTE}
        for q in DMA_Q.values():
            self.cnt[q] = 0
        self.state = {}
        self.dep_targets = set()
        self.final_waits = set()
        self.tok_tag = {}

    def _read_deps(self, keys):
        d = set()
        for k in keys:
            st = self.state.get(k)
            if st and st[0] is not None:
                d.add(st[0])
        return d

    def _write_deps(self, keys):
        d = set()
        for k in keys:
            st = self.state.get(k)
            if st:
                if st[0] is not None:
                    d.add(st[0])
                d.update(st[1])
        return d

    def op(self, eng, fn, r=(), w=(), dma=False, tag=""):
        deps = self._read_deps(r) | self._write_deps(w)
        if dma:
            q = DMA_Q[eng]
            token = (q, self.cnt[q])
            self.cnt[q] += 1
        else:
            token = (eng, self.cnt[eng])
            self.cnt[eng] += 1
        eff = set(deps)
        for t in eff:
            if t[0] in COMPUTE:
                self.dep_targets.add(t)
        for k in r:
            self.state.setdefault(k, [None, set()])[1].add(token)
        for k in w:
            self.state[k] = [token, set()]
        self.seq[eng].append((fn, eff, token, dma, tag))
        self.tok_tag[token] = f"{token[0]}:{token[1]}:{tag}"
        return token

    def finalize(self, last_engine="sync", listing=None):
        nc = self.nc
        lst = open(listing, "w") if listing else None
        for t in self.final_waits:
            if t[0] in COMPUTE:
                self.dep_targets.add(t)
        marked = {e: sorted(i for (ee, i) in self.dep_targets if ee == e) for e in COMPUTE}
        rank = {}
        for e in COMPUTE:
            for n, i in enumerate(marked[e]):
                rank[(e, i)] = n + 1
        es = ExitStack()
        sems = {}
        NQS = 16
        for e in COMPUTE:
            sems[e] = es.enter_context(nc.semaphore(f"s_{e}"))
        qsems = {}
        for q in sorted(set(DMA_Q.values())):
            qsems[q] = [es.enter_context(nc.semaphore(f"s_{q}_{i}")) for i in range(NQS)]

        def wait_val(tok):
            de, di = tok
            if de in COMPUTE:
                return sems[de], rank[tok]
            return qsems[de][di % NQS], 16 * (di // NQS + 1)

        def emit_stream(eng_obj, eng_name):
            waited = {}

            def wait(tok):
                sem, val = wait_val(tok)
                if waited.get(sem.name, 0) < val:
                    eng_obj.wait_ge(sem, val)
                    waited[sem.name] = val

            for (fn, deps, token, dma, tag) in self.seq[eng_name]:
                wl = []
                for tok in sorted(deps):
                    sem, val = wait_val(tok)
                    if waited.get(sem.name, 0) < val:
                        wl.append((sem.name, val))
                for tok in sorted(deps):
                    wait(tok)
                if dma and token[1] >= NQS:
                    wait((token[0], token[1] - NQS))
                inst = fn(eng_obj)
                if dma:
                    inst.then_inc(qsems[token[0]][token[1] % NQS], 16)
                elif token in rank:
                    inst.then_inc(sems[token[0]], 1)
                if lst:
                    inc = "DMA+16" if dma else ("+1" if token in rank else "")
                    dt_ = [self.tok_tag.get(t, t) for t in sorted(deps)]
                    lst.write(f"{eng_name} {token[1]} {tag} waits={wl} deps={dt_} {inc}\n")
            if eng_name == last_engine:
                for tok in sorted(self.final_waits):
                    wait(tok)

        with es, nc.Block() as block:
            block.sync(lambda e: emit_stream(e, "sync"))
            block.vector(lambda e: emit_stream(e, "vector"))
            block.scalar(lambda e: emit_stream(e, "scalar"))
            block.tensor(lambda e: emit_stream(e, "tensor"))
            block.gpsimd(lambda e: emit_stream(e, "gpsimd"))
        if lst:
            lst.close()
        return sum(len(s) for s in self.seq.values())


class Region:
    GR = 2048

    def __init__(self, arena, name, off, nbytes):
        self.arena = arena
        self.name = name
        self.off = off
        self.nbytes = nbytes

    def K(self, byteoff, nbytes):
        g0 = byteoff // self.GR
        g1 = (byteoff + nbytes - 1) // self.GR
        return tuple(f"{self.name}g/{g}" for g in range(g0, g1 + 1))

    def ap(self, dtype, cols, part=128, byteoff=0, p0=0):
        sz = {F32: 4, BF16: 2, I32: 4, U8: 1, FP8: 1}[dtype]
        assert byteoff >= 0 and byteoff + cols * sz <= self.nbytes, \
            (self.name, byteoff, cols, sz, self.nbytes)
        v = self.arena[p0:p0 + part, self.off + byteoff:self.off + byteoff + cols * sz]
        return v.bitcast(dtype)


class MK:
    def __init__(self):
        nc = bass.Bass("TRN2", target_bir_lowering=False, dynamic_dma_scratch_size=2048)
        self.nc = nc
        self.P = Prog(nc)
        self._declare_io()
        self._declare_consts()
        self._declare_arena()
        self._psum()
        self._load_consts()

    def _declare_io(self):
        nc = self.nc
        self.inp = {}

        def di(name, shape, dt=F32):
            self.inp[name] = nc.dram_tensor(name, shape, dt, kind="ExternalInput")

        di("x", [T, D])
        di("l0_wq", [D, D]); di("l0_wk", [D, D]); di("l0_wv", [D, D]); di("l0_wo", [D, D])
        di("l0_wa", [D, 16]); di("l0_ba", [1, 16])
        di("l0f_wg", [3072, D], I32); di("l0f_wu", [3072, D], I32); di("l0f_wd", [D, 3072], I32)
        di("l1_wkqv", [4, D, 3072]); di("l1_wm1", [4, D, 2048]); di("l1_bm1", [4, 2048])
        di("l1_wm2", [4, D, D]); di("l1_bm2", [4, D])
        di("l1_gng", [4, D]); di("l1_gnb", [4, D])
        di("l1_wf1", [4, D, 2048]); di("l1_bf1", [4, 2048])
        di("l1_wf2", [4, 2048, D]); di("l1_bf2", [4, D])
        di("l1_lng", [4, D]); di("l1_lnb", [4, D])
        di("f1_gw", [4096, D]); di("f1_uw", [4096, D]); di("f1_dw", [D, 4096])
        di("f1_sin", [1, 1]); di("f1_sind", [1, 1])
        self.y_h = nc.dram_tensor("y", [T, D], F32, kind="ExternalOutput")
        self.xspill_h = nc.dram_tensor("xspill", [NCH, 128, T], F32, kind="Internal")
        self.gwT_h = nc.dram_tensor("gwT", [D, 4096], BF16, kind="Internal")
        self.uwT_h = nc.dram_tensor("uwT", [D, 4096], BF16, kind="Internal")
        self.dwT_h = nc.dram_tensor("dwT", [4096, D], BF16, kind="Internal")

    def _declare_consts(self):
        nc = self.nc
        self.const_h = {}

        def ci(name, arr):
            self.const_h[name] = nc.inline_tensor(np.ascontiguousarray(arr), name)

        ci("id128f", np.eye(128, dtype=np.float32))
        ci("id128b", np.eye(128, dtype=np.float32).astype(BF16NP))
        ci("id16f", np.eye(16, dtype=np.float32))
        ci("id16b", np.eye(16, dtype=np.float32).astype(BF16NP))
        ci("ones_1x128f", np.ones((1, 128), np.float32))
        ci("onescol_b", np.ones((128, 1), np.float32).astype(BF16NP))
        ci("maskbig", np.where(np.arange(128)[:, None] <= np.arange(128)[None, :],
                               0.0, -1e30).astype(np.float32))
        ci("rn", (1.0 + np.sqrt(np.arange(T, dtype=np.float64))).astype(np.float32)[None, :])
        for dk in range(2):
            m = (128 * dk + np.arange(128)[:, None] <= np.arange(256)[None, :])
            ci(f"trimask{dk}", m.astype(np.float32).astype(BF16NP))

    def _declare_arena(self):
        nc = self.nc
        avail = (nc.sbuf_bytes_remaining - 512) // 64 * 64
        self.arena = nc.alloc_sbuf_tensor("arena", [128, avail], U8)
        off = [0]
        self.reg = {}

        def R(name, nbytes):
            r = Region(self.arena, name, off[0], nbytes)
            off[0] += nbytes
            assert off[0] <= avail, (name, off[0], avail)
            self.reg[name] = r
            return r

        self.S0 = R("S0", 65536)
        self.S1 = R("S1", 65536)
        self.PA = R("PA", 32768)
        self.PB = R("PB", 32768)
        self.WP = R("WP", 8192)
        self.SC = R("SC", 8192)
        self.SC.GR = 256
        self.SM = R("SM", avail - off[0])
        self._smoff = {}
        so = [0]

        def A(name, nbytes, align=4):
            o = (so[0] + align - 1) // align * align
            self._smoff[name] = o
            so[0] = o + nbytes

        A("id128f", 512); A("id128b", 256); A("id16f", 64); A("id16b", 32)
        A("ones_1x128f", 512); A("onescol_b", 2)
        A("maskbig", 512)
        A("trimask0", 512); A("trimask1", 512)
        A("rnrow", 1024)
        A("biasT", 512)
        A("bias_nat", 512)
        A("sz", 2080)
        A("szbf", 1040)
        A("szscr", 260)
        A("la_tok", 1024)
        A("wa", 512)
        A("wab", 256)
        A("ba", 64)
        A("ba2", 64)
        A("ones16", 512)
        A("ecol", 512)
        A("lrow2", 1024)
        A("srows", 256)
        A("ccols", 256)
        assert so[0] <= self.SM.nbytes, (so[0], self.SM.nbytes)
        self._sm_used = so[0]
        self._sm_alloc = A
        self._ccols = {}

    def cc(self, value):
        if value not in self._ccols:
            i = len(self._ccols)
            assert i < 64
            ap = self.smv("ccols", F32, 1, byteoff=4 * i)
            self.P.op("vector", lambda e, ap=ap, value=value: e.memset(ap, float(value)),
                      w=(f"cc/{value}",), tag=f"cc_{value}")
            self._ccols[value] = (ap, f"cc/{value}")
        return self._ccols[value]

    def smv(self, name, dtype, cols, part=128, p0=0, byteoff=0):
        return self.SM.ap(dtype, cols, part=part, p0=p0,
                          byteoff=self._smoff[name] + byteoff)

    def wp(self, dtype, cols, granule, part=128, p0=0):
        return self.WP.ap(dtype, cols, part=part, p0=p0, byteoff=granule * 1024)

    def _psum(self):
        nc = self.nc
        self.psb = [nc.alloc_psum_tensor(f"psb{i}", [128, 512], F32) for i in range(4)]
        self.ps_ab = [nc.alloc_psum_tensor(f"ps_ab{i}", [128, 512], F32) for i in range(2)]
        self.ps_st = nc.alloc_psum_tensor("ps_st", [128, 512], F32)
        self.ps_x = nc.alloc_psum_tensor("ps_x", [128, 512], F32)
        assert nc.psum_banks_remaining == 0, nc.psum_banks_remaining
        self.trbanks = [(self.ps_x, "ps_x"), (self.psb[0], "psb/0"),
                        (self.psb[1], "psb/1"), (self.psb[2], "psb/2")]

    def _load_consts(self):
        P = self.P
        for nm, cols, part, dt_ in [
            ("id128f", 128, 128, F32), ("id128b", 128, 128, BF16), ("id16f", 16, 16, F32),
            ("ones_1x128f", 128, 1, F32), ("onescol_b", 1, 128, BF16),
            ("maskbig", 128, 128, F32),
            ("trimask0", 256, 128, BF16), ("trimask1", 256, 128, BF16),
        ]:
            h = self.const_h[nm]
            P.op("sync", lambda e, nm=nm, cols=cols, part=part, dt_=dt_, h=h:
                 e.dma_start(out=self.smv(nm, dt_, cols, part=part), in_=h.ap()),
                 w=(nm,), dma=True, tag=f"ld_{nm}")

    def stage_load_x(self):
        P = self.P
        for tt in range(NTT):
            P.op("sync", lambda e, tt=tt: e.dma_start(
                out=self.S1.ap(F32, D, byteoff=tt * 4096),
                in_=self.inp["x"].ap()[tt * 128:(tt + 1) * 128, :]),
                w=self.S1.K(tt * 4096, 4096), dma=True, tag=f"ld_x{tt}")
        for tt in range(NTT):
            for c in range(NCH):
                i = tt * NCH + c
                pb, pk = self.psb[i % 2], f"psb/{i % 2}"
                P.op("tensor", lambda e, tt=tt, c=c, pb=pb: e.transpose(
                    pb[:, 0:128],
                    self.S1.ap(F32, 128, byteoff=tt * 4096 + c * 512),
                    self.smv("id128f", F32, 128)),
                    r=self.S1.K(tt * 4096 + c * 512, 512) + ("id128f",), w=(pk,), tag=f"trx{tt}_{c}")
                P.op("vector", lambda e, tt=tt, c=c, pb=pb: e.tensor_copy(
                    out=self.S0.ap(F32, 128, byteoff=c * 8192 + tt * 512), in_=pb[:, 0:128]),
                    r=(pk,), w=self.S0.K(c * 8192 + tt * 512, 512), tag=f"evx{tt}_{c}")

    def ln_fm(self, name, src_reg, src_off, dst_reg, dst_off, dst_dtype, scratch,
              gamma=None, beta=None, gb_keys=()):
        P = self.P
        sreg, boff = scratch
        xb = lambda b: sreg.ap(BF16, 512, byteoff=boff + 1024 * (b % 2))
        xs = lambda b: sreg.ap(BF16, 512, byteoff=boff + 2048 + 1024 * (b % 2))
        tmp = sreg.ap(F32, 512, byteoff=boff + 4096)
        kxb = lambda b: sreg.K(boff + 1024 * (b % 2), 1024)
        kxs = lambda b: sreg.K(boff + 2048 + 1024 * (b % 2), 1024)
        ktmp = sreg.K(boff + 4096, 2048)
        dsz = {F32: 4, BF16: 2}[dst_dtype]

        def srow(r):
            return self.SC.ap(F32, 512, part=1, p0=0, byteoff=2048 * r)

        def src_ap(c, s):
            return src_reg.ap(F32, 512, byteoff=src_off(c) + s * 2048)

        def src_k(c, s):
            return src_reg.K(src_off(c) + s * 2048, 2048)

        for s in range(4):
            for c in range(NCH):
                P.op("vector", lambda e, c=c, s=s: e.tensor_copy(
                    out=xb(c), in_=src_ap(c, s)),
                    r=src_k(c, s), w=kxb(c), tag=f"{name}_xb")
                P.op("scalar", lambda e, c=c, s=s: e.activation(
                    out=xs(c), in_=src_ap(c, s), func=Act.Square),
                    r=src_k(c, s), w=kxs(c), tag=f"{name}_xs")
                P.op("tensor", lambda e, c=c, s=s: e.matmul(
                    self.ps_st[0:1, :], self.smv("onescol_b", BF16, 1), xb(c),
                    start=(c == 0), stop=(c == NCH - 1)),
                    r=kxb(c) + ("onescol_b",), w=("ps_st",), tag=f"{name}_ms")
                P.op("tensor", lambda e, c=c, s=s: e.matmul(
                    self.ps_x[0:1, 0:512], self.smv("onescol_b", BF16, 1), xs(c),
                    start=(c == 0), stop=(c == NCH - 1)),
                    r=kxs(c) + ("onescol_b",), w=("ps_x",), tag=f"{name}_mq")
            P.op("vector", lambda e, s=s: e.tensor_scalar(
                out=srow(0), in0=self.ps_st[0:1, :], scalar1=1.0 / D, scalar2=None,
                op0=Alu.mult),
                r=("ps_st", "ps_x"), w=self.SC.K(0, 2048), tag=f"{name}_mu")
            P.op("vector", lambda e, s=s: e.tensor_tensor(
                out=srow(1), in0=srow(0), in1=srow(0), op=Alu.mult),
                r=self.SC.K(0, 2048), w=self.SC.K(2048, 2048), tag=f"{name}_musq")
            P.op("vector", lambda e, s=s: e.scalar_tensor_tensor(
                out=srow(1), in0=self.ps_x[0:1, 0:512], scalar=1.0 / D, in1=srow(1),
                op0=Alu.mult, op1=Alu.subtract),
                r=("ps_x",) + self.SC.K(2048, 2048), w=self.SC.K(2048, 2048), tag=f"{name}_var")
            epsap, epsk = self.cc(1e-5)
            P.op("scalar", lambda e, s=s, epsap=epsap: e.activation(
                out=srow(2), in_=srow(1), func=Act.Ln, bias=epsap[0:1, 0:1], scale=1.0),
                r=self.SC.K(2048, 2048) + (epsk,), w=self.SC.K(4096, 2048), tag=f"{name}_lnv")
            P.op("scalar", lambda e, s=s: e.activation(
                out=srow(2), in_=srow(2), func=Act.Exp, scale=-0.5),
                r=self.SC.K(4096, 2048), w=self.SC.K(4096, 2048), tag=f"{name}_rstd")
            P.op("vector", lambda e, s=s: e.scalar_tensor_tensor(
                out=srow(3), in0=srow(0), scalar=-1.0, in1=srow(2),
                op0=Alu.mult, op1=Alu.mult),
                r=self.SC.K(0, 2048) + self.SC.K(4096, 2048), w=self.SC.K(6144, 2048), tag=f"{name}_nmr")
            P.op("tensor", lambda e, s=s: e.matmul(
                self.ps_ab[0][:, :], self.smv("ones_1x128f", F32, 128, part=1),
                srow(2), start=True, stop=True),
                r=self.SC.K(4096, 2048) + ("ones_1x128f",), w=("ps_ab0",), tag=f"{name}_A")
            P.op("tensor", lambda e, s=s: e.matmul(
                self.ps_ab[1][:, :], self.smv("ones_1x128f", F32, 128, part=1),
                srow(3), start=True, stop=True),
                r=self.SC.K(6144, 2048) + ("ones_1x128f",), w=("ps_ab1",), tag=f"{name}_B")
            for c in range(NCH):
                dst = dst_reg.ap(dst_dtype, 512, byteoff=dst_off(c) + s * 512 * dsz)
                dk = dst_reg.K(dst_off(c) + s * 512 * dsz, 512 * dsz)
                P.op("vector", lambda e, c=c, s=s: e.tensor_tensor(
                    out=tmp, in0=src_ap(c, s), in1=self.ps_ab[0][:, :], op=Alu.mult),
                    r=src_k(c, s) + ("ps_ab0",), w=ktmp, tag=f"{name}_ap1")
                if gamma is None:
                    P.op("vector", lambda e, c=c, s=s, dst=dst: e.tensor_tensor(
                        out=dst, in0=tmp, in1=self.ps_ab[1][:, :], op=Alu.add),
                        r=ktmp + ("ps_ab1",), w=dk, tag=f"{name}_ap2")
                else:
                    P.op("vector", lambda e, c=c, s=s: e.tensor_tensor(
                        out=tmp, in0=tmp, in1=self.ps_ab[1][:, :], op=Alu.add),
                        r=ktmp + ("ps_ab1",), w=ktmp, tag=f"{name}_ap2")
                    P.op("vector", lambda e, c=c, s=s, dst=dst: e.tensor_scalar(
                        out=dst, in0=tmp, scalar1=gamma(c), scalar2=beta(c),
                        op0=Alu.mult, op1=Alu.add),
                        r=ktmp + tuple(gb_keys), w=dk, tag=f"{name}_ap3")

    def stream_w_cols(self, hbm, oc, n_in=D, tag="w"):
        P = self.P
        nin_ch = n_in // 128
        wslot = oc % 2
        wf32 = self.WP.ap(F32, nin_ch * 128, byteoff=0)
        wbf = self.WP.ap(BF16, nin_ch * 128, byteoff=4096 + 2048 * wslot)
        kf = self.WP.K(0, nin_ch * 512)
        kb = self.WP.K(4096 + 2048 * wslot, nin_ch * 256)
        src = hbm.ap()[:, oc * 128:(oc + 1) * 128]
        src = src.rearrange("(kc p) m -> p kc m", p=128)
        P.op("sync", lambda e, src=src, wf32=wf32, nin_ch=nin_ch: e.dma_start(
            out=wf32.rearrange("p (kc m) -> p kc m", kc=nin_ch), in_=src),
            w=kf, dma=True, tag=f"{tag}_dma{oc}")
        P.op("vector", lambda e, wf32=wf32, wbf=wbf: e.tensor_copy(out=wbf, in_=wf32),
             r=kf, w=kb, tag=f"{tag}_cvt{oc}")
        return (lambda kc: self.WP.ap(BF16, 128, byteoff=4096 + 2048 * wslot + kc * 256)), kb

    def stage_l0_attn(self, upto='all'):
        P = self.P
        ln0 = lambda kc, c0, cols: self.S1.ap(BF16, cols, byteoff=kc * 4096 + c0 * 2)
        ln0k = lambda kc, c0, cols: self.S1.K(kc * 4096 + c0 * 2, cols * 2)

        for kc in range(NCH):
            wf32 = self.WP.ap(F32, D, byteoff=0)
            kf = self.WP.K(0, 4096)
            P.op("sync", lambda e, kc=kc, wf32=wf32: e.dma_start(
                out=wf32, in_=self.inp["l0_wv"].ap()[kc * 128:(kc + 1) * 128, :]),
                w=kf, dma=True, tag=f"wv_dma{kc}")
            P.op("vector", lambda e, kc=kc, wf32=wf32: e.tensor_copy(
                out=self.PB.ap(BF16, D, byteoff=kc * 2048), in_=wf32),
                r=kf, w=self.PB.K(kc * 2048, 2048), tag=f"wv_cvt{kc}")
        P.op("sync", lambda e: e.dma_start(
            out=self.smv("wa", F32, 128).rearrange("p (kc m) -> p kc m", kc=8),
            in_=self.inp["l0_wa"].ap().rearrange("(kc p) m -> p kc m", p=128)),
            w=("wa",), dma=True, tag="wa_dma")
        P.op("vector", lambda e: e.tensor_copy(
            out=self.smv("wab", BF16, 128), in_=self.smv("wa", F32, 128)),
            r=("wa",), w=("wab",), tag="wa_cvt")
        P.op("sync", lambda e: e.dma_start(
            out=self.smv("ba", F32, 16, part=1), in_=self.inp["l0_ba"].ap()),
            w=("ba",), dma=True, tag="ba_dma")
        P.op("tensor", lambda e: e.matmul(
            self.ps_x[:, 0:16], self.smv("ones_1x128f", F32, 128, part=1),
            self.smv("ba", F32, 16, part=1), start=True, stop=True),
            r=("ba", "ones_1x128f"), w=("ps_x",), tag="ba_bc")
        P.op("vector", lambda e: e.tensor_copy(
            out=self.smv("ba2", F32, 16), in_=self.ps_x[:, 0:16]),
            r=("ps_x",), w=("ba2",), tag="ba2_ev")
        P.op("vector", lambda e: e.memset(self.smv("ones16", F32, 128, part=16), 1.0),
             w=("ones16",), tag="ones16")
        P.op("vector", lambda e: e.memset(self.smv("sz", F32, 520), 0.0),
             w=("sz",), tag="sz0")
        P.op("vector", lambda e: e.memset(self.smv("szbf", BF16, 520), 0.0),
             w=("szbf",), tag="szbf0")

        eps_ap, eps_k = self.cc(1e-6)
        ntt_run = 1 if upto == "ya0" else NTT
        for tt in range(ntt_run):
            b0, b1 = self.ps_ab[0], self.ps_ab[1]
            for kc in range(NCH) if upto != "ya0" else []:
                for half in range(2):
                    bank = (b0, b1)[half]
                    P.op("tensor", lambda e, tt=tt, kc=kc, half=half, bank=bank: e.matmul(
                        bank[:, 0:512], ln0(kc, tt * 128, 128),
                        self.PB.ap(BF16, 512, byteoff=kc * 2048 + half * 1024),
                        start=(kc == 0), stop=(kc == NCH - 1)),
                        r=ln0k(kc, tt * 128, 128) + self.PB.K(kc * 2048 + half * 1024, 1024),
                        w=(f"ps_ab{half}",), tag=f"v_mm{tt}_{kc}_{half}")
            for kc in range(NCH):
                P.op("tensor", lambda e, tt=tt, kc=kc: e.matmul(
                    self.ps_x[:, 0:16], ln0(kc, tt * 128, 128),
                    self.smv("wab", BF16, 16, byteoff=kc * 32),
                    start=(kc == 0), stop=(kc == NCH - 1)),
                    r=ln0k(kc, tt * 128, 128) + ("wab",), w=("ps_x",), tag=f"ya_mm{tt}_{kc}")
            for half in range(2) if upto != "ya0" else []:
                bank = (b0, b1)[half]
                P.op("vector", lambda e, tt=tt, half=half, bank=bank: e.tensor_copy(
                    out=self.S1.ap(BF16, 512, byteoff=32768 + tt * 2048 + half * 1024),
                    in_=bank[:, 0:512]),
                    r=(f"ps_ab{half}",), w=self.S1.K(32768 + tt * 2048 + half * 1024, 1024),
                    tag=f"v_ev{tt}_{half}")
            if upto in ("proj", "ya0") and tt == 0:
                P.op("vector", lambda e: e.tensor_copy(
                    out=self.S0.ap(F32, 16, byteoff=58368), in_=self.ps_x[:, 0:16]),
                    r=("ps_x",), w=self.S0.K(58368, 64), tag="ya_ps_dbg")
            if upto == "ya0":
                continue
            u = self.SC.ap(F32, 16, byteoff=4864)
            yv = self.SC.ap(F32, 16, byteoff=4928)
            mv = self.SC.ap(F32, 16, byteoff=4992)
            av = self.SC.ap(F32, 16, byteoff=5056)
            ku = self.SC.K(4864, 256)
            la = self.smv("la_tok", F32, 16, byteoff=tt * 64)
            one_ap, one_k = self.cc(1.0)
            P.op("vector", lambda e, tt=tt, u=u: e.tensor_tensor(
                out=u, in0=self.ps_x[:, 0:16], in1=self.smv("ba2", F32, 16), op=Alu.add),
                r=("ps_x", "ba2"), w=ku, tag=f"la_u{tt}")
            P.op("vector", lambda e, tt=tt, yv=yv, u=u: e.tensor_scalar(
                out=yv, in0=u, scalar1=2.0, scalar2=None, op0=Alu.add),
                r=ku, w=ku, tag=f"la_y{tt}")
            P.op("vector", lambda e, tt=tt, mv=mv, yv=yv: e.tensor_scalar(
                out=mv, in0=yv, scalar1=0.0, scalar2=None, op0=Alu.min),
                r=ku, w=ku, tag=f"la_m{tt}")
            P.op("scalar", lambda e, tt=tt, av=av, yv=yv: e.activation(
                out=av, in_=yv, func=Act.Abs),
                r=ku, w=ku, tag=f"la_a{tt}")
            P.op("scalar", lambda e, tt=tt, av=av: e.activation(
                out=av, in_=av, func=Act.Exp, scale=-1.0),
                r=ku, w=ku, tag=f"la_e{tt}")
            P.op("scalar", lambda e, tt=tt, av=av, one_ap=one_ap: e.activation(
                out=av, in_=av, func=Act.Ln, bias=one_ap[:, 0:1], scale=1.0),
                r=ku + (one_k,), w=ku, tag=f"la_l{tt}")
            P.op("vector", lambda e, tt=tt, la=la, mv=mv, av=av: e.tensor_tensor(
                out=la, in0=mv, in1=av, op=Alu.subtract),
                r=ku, w=("la_tok",), tag=f"la_f{tt}")
            if upto in ("proj", "ya0"):
                P.op("vector", lambda e, tt=tt, yv=yv: e.tensor_copy(
                    out=self.S0.ap(F32, 16, byteoff=57344 + tt * 64), in_=yv),
                    r=ku, w=self.S0.K(57344 + tt * 64, 64), tag=f"la_dbg{tt}")

        if upto == "ya0":
            return
        def proj_elu(dst_reg, w_name, tag):
            for oc in range(NCH):
                wbf, wkeys = self.stream_w_cols(self.inp[w_name], oc, tag=tag)
                for s in range(4):
                    bank = self.psb[(oc * 4 + s) % 4]
                    bk = f"psb/{(oc * 4 + s) % 4}"
                    for kc in range(NCH):
                        P.op("tensor", lambda e, oc=oc, s=s, kc=kc, bank=bank, wbf=wbf: e.matmul(
                            bank[:, :], wbf(kc), ln0(kc, s * 512, 512),
                            start=(kc == 0), stop=(kc == NCH - 1)),
                            r=wkeys + ln0k(kc, s * 512, 512), w=(bk,), tag=f"{tag}mm{oc}_{s}")
                    m = self.SC.ap(F32, 512, byteoff=0)
                    ev = self.SC.ap(F32, 512, byteoff=2048)
                    P.op("vector", lambda e, bank=bank, m=m: e.tensor_scalar(
                        out=m, in0=bank[:, :], scalar1=0.0, scalar2=None, op0=Alu.min),
                        r=(bk,), w=self.SC.K(0, 2048), tag=f"{tag}min{oc}_{s}")
                    P.op("scalar", lambda e, m=m, ev=ev: e.activation(
                        out=ev, in_=m, func=Act.Exp),
                        r=self.SC.K(0, 2048), w=self.SC.K(2048, 2048), tag=f"{tag}exp{oc}_{s}")
                    P.op("vector", lambda e, oc=oc, s=s, bank=bank, ev=ev, dst_reg=dst_reg:
                         e.scalar_tensor_tensor(
                             out=dst_reg.ap(BF16, 512, byteoff=oc * 4096 + s * 1024),
                             in0=bank[:, :], scalar=0.0, in1=ev,
                             op0=Alu.max, op1=Alu.add),
                         r=(bk,) + self.SC.K(2048, 2048),
                         w=dst_reg.K(oc * 4096 + s * 1024, 1024), tag=f"{tag}out{oc}_{s}")

        proj_elu(self.PB, "l0_wk", "k")
        proj_elu(self.PA, "l0_wq", "q")
        if upto == "proj":
            return

        kfm = lambda p, half, c0, cols: self.PB.ap(BF16, cols, byteoff=p * 4096 + c0 * 2, part=64, p0=half * 64)
        kfmk = lambda p, c0, cols: self.PB.K(p * 4096 + c0 * 2, cols * 2)
        qfm = lambda p, half, c0, cols: self.PA.ap(BF16, cols, byteoff=p * 4096 + c0 * 2, part=64, p0=half * 64)
        qfmk = lambda p, c0, cols: self.PA.K(p * 4096 + c0 * 2, cols * 2)
        LT = lambda c: self.WP.ap(F32, 128, part=16, byteoff=c * 512)
        LTk = lambda c: self.WP.K(c * 512, 512)
        lasl = self.SC.ap(F32, 128, part=16, byteoff=5120)
        lasl_k = self.SC.K(5120, 512)
        negLT = self.SC.ap(F32, 128, part=16, byteoff=5632)
        negLT_k = self.SC.K(5632, 512)
        Etb = self.SC.ap(BF16, 128, part=16, byteoff=6144)
        Etb_k = self.SC.K(6144, 256)
        ECh = self.SC.ap(F32, 128, part=16, byteoff=6400)
        ECh_k = self.SC.K(6400, 512)
        ECt = self.SC.ap(F32, 16, byteoff=6912)
        ECt_k = self.SC.K(6912, 64)
        ltok = self.SC.ap(F32, 16, byteoff=6976)
        ltok_k = self.SC.K(6976, 64)
        lrow = lambda sl: self.smv("lrow2", F32, 128, part=1, byteoff=512 * (sl % 2))
        lrow_k = lambda sl: (f"lrow2/{sl % 2}",)
        e2sb = self.SC.ap(BF16, 128, byteoff=7168)
        e2sb_k = self.SC.K(7168, 256)
        ktok = lambda cols, off: self.SC.ap(BF16, cols, byteoff=0 + off)
        kttok = lambda cols, off: self.SC.ap(BF16, cols, byteoff=2048 + off)
        dmat = self.SC.ap(F32, 128, byteoff=4096)
        dmat_k = self.SC.K(4096, 512)
        sd = self.SC.ap(BF16, 128, byteoff=4608)
        sd_k = self.SC.K(4608, 256)
        qts = self.SC.ap(BF16, 128, byteoff=7424)
        qts_k = self.SC.K(7424, 256)
        lam = self.smv("ecol", F32, 1)
        lam_k = ("ecol",)
        dcol = self.smv("srows", F32, 1)
        rcol = self.smv("srows", F32, 1, byteoff=4)
        dcol_k = ("srows",)

        gla = lambda tt, c0, cols: self.S1.ap(BF16, cols, byteoff=tt * 2048 + c0 * 2)
        glak = lambda tt, c0, cols: self.S1.K(tt * 2048 + c0 * 2, cols * 2)

        for c in range(NTT):
            P.op("tensor", lambda e, c=c: e.transpose(
                self.ps_x[0:16, 0:128],
                self.smv("la_tok", F32, 16, byteoff=c * 64),
                self.smv("id128f", F32, 128)),
                r=("la_tok", "id128f"), w=("ps_x",), tag=f"sc_latr{c}")
            P.op("vector", lambda e, c=c: e.tensor_copy(out=lasl, in_=self.ps_x[0:16, 0:128]),
                 r=("ps_x",), w=lasl_k, tag=f"sc_laev{c}")
            P.op("vector", lambda e, c=c: e.tensor_tensor_scan(
                out=LT(c), data0=self.smv("ones16", F32, 128, part=16), data1=lasl,
                initial=0.0, op0=Alu.mult, op1=Alu.add),
                r=lasl_k + ("ones16",), w=LTk(c), tag=f"sc_scan{c}")
            P.op("vector", lambda e, c=c: e.tensor_scalar(
                out=negLT, in0=LT(c), scalar1=-1.0, scalar2=None, op0=Alu.mult),
                r=LTk(c), w=negLT_k, tag=f"sc_neg{c}")
            P.op("scalar", lambda e, c=c: e.activation(
                out=Etb, in_=LT(c), func=Act.Exp),
                r=LTk(c), w=Etb_k, tag=f"sc_et{c}")
            P.op("vector", lambda e, c=c: e.tensor_scalar(
                out=ECh, in0=LT(c), scalar1=LT(c)[:, 127:128], scalar2=None,
                op0=Alu.subtract),
                r=LTk(c), w=ECh_k, tag=f"sc_ech{c}")
            P.op("scalar", lambda e: e.activation(
                out=ECh, in_=ECh, func=Act.Exp, scale=-1.0),
                r=ECh_k, w=ECh_k, tag=f"sc_eche{c}")
            P.op("tensor", lambda e: e.transpose(
                self.ps_x[0:128, 0:16], ECh, self.smv("id16f", F32, 16, part=16)),
                r=ECh_k + ("id16f",), w=("ps_x",), tag=f"sc_ect{c}")
            P.op("vector", lambda e: e.tensor_copy(out=ECt, in_=self.ps_x[0:128, 0:16]),
                 r=("ps_x",), w=ECt_k, tag=f"sc_ectev{c}")
            P.op("tensor", lambda e, c=c: e.transpose(
                self.ps_x[0:128, 16:32], LT(c), self.smv("id16f", F32, 16, part=16)),
                r=LTk(c) + ("id16f",), w=("ps_x",), tag=f"sc_lttr{c}")
            P.op("vector", lambda e: e.tensor_copy(out=ltok, in_=self.ps_x[0:128, 16:32]),
                 r=("ps_x",), w=ltok_k, tag=f"sc_ltev{c}")
            for p in range(NCH):
                P.op("tensor", lambda e, c=c, p=p: e.transpose(
                    self.ps_x[:, 0:512].bitcast(BF16)[:, 0:128],
                    self.PB.ap(BF16, 128, byteoff=p * 4096 + c * 256),
                    self.smv("id128b", BF16, 128)),
                    r=kfmk(p, c * 128, 128) + ("id128b",), w=("ps_x",), tag=f"sc_ktr{c}_{p}")
                P.op("vector", lambda e, p=p: e.tensor_copy(
                    out=ktok(128, p * 256), in_=self.ps_x[:, 0:512].bitcast(BF16)[:, 0:128]),
                    r=("ps_x",), w=self.SC.K(p * 256, 256), tag=f"sc_ktev{c}_{p}")
            for h in range(16):
                P.op("vector", lambda e, h=h: e.tensor_scalar(
                    out=kttok(64, h * 128), in0=ktok(64, h * 128),
                    scalar1=ECt[:, h:h + 1], scalar2=None, op0=Alu.mult),
                    r=self.SC.K(h * 128, 128) + ECt_k,
                    w=self.SC.K(2048 + h * 128, 128), tag=f"sc_kt{c}_{h}")
            for p in range(NCH):
                e2bank = self.ps_ab[p % 2]
                e2k = f"ps_ab{p % 2}"
                for half in range(2):
                    h = 2 * p + half
                    P.op("tensor", lambda e, c=c, h=h, half=half, e2bank=e2bank: e.matmul(
                        e2bank[0:1, 0:128], self.smv("id16f", F32, 1, part=16, byteoff=h * 4),
                        LT(c), start=True, stop=True),
                        r=LTk(c) + ("id16f",), w=(e2k,), tag=f"sc_lr{c}_{h}")
                    P.op("vector", lambda e, half=half, e2bank=e2bank: e.tensor_copy(
                        out=lrow(half), in_=e2bank[0:1, 0:128]),
                        r=(e2k,), w=lrow_k(half), tag=f"sc_lrev{c}_{h}")
                P.op("tensor", lambda e, e2bank=e2bank: e.matmul(
                    e2bank[0:64, 0:128], self.smv("ones_1x128f", F32, 64, part=1),
                    lrow(0), start=True, stop=True),
                    r=lrow_k(0) + ("ones_1x128f",), w=(e2k,), tag=f"sc_b2a{c}_{p}")
                P.op("tensor", lambda e, e2bank=e2bank: e.matmul(
                    e2bank[64:128, 0:128], self.smv("ones_1x128f", F32, 64, part=1),
                    lrow(1), start=True, stop=True),
                    r=lrow_k(1) + ("ones_1x128f",), w=(e2k,), tag=f"sc_b2b{c}_{p}")
                P.op("scalar", lambda e, e2bank=e2bank: e.activation(
                    out=lam, in_=e2bank[:, 127:128], func=Act.Exp),
                    r=(e2k,), w=lam_k, tag=f"sc_lam{c}_{p}")
                P.op("scalar", lambda e, e2bank=e2bank: e.activation(
                    out=e2sb, in_=e2bank[:, 0:128], func=Act.Exp),
                    r=(e2k,), w=e2sb_k, tag=f"sc_e2{c}_{p}")
                P.op("vector", lambda e, c=c, p=p: e.tensor_tensor(
                    out=qts, in0=self.PA.ap(BF16, 128, byteoff=p * 4096 + c * 256),
                    in1=e2sb, op=Alu.mult),
                    r=qfmk(p, c * 128, 128) + e2sb_k, w=qts_k, tag=f"sc_qt{c}_{p}")
                P.op("vector", lambda e, p=p: e.tensor_scalar(
                    out=self.smv("szscr", F32, 65), in0=self.smv("sz", F32, 65, byteoff=p * 260),
                    scalar1=lam[:, 0:1], scalar2=None, op0=Alu.mult),
                    r=("sz",) + lam_k, w=("szscr",), tag=f"sc_su1{c}_{p}")
                for half in range(2):
                    h = 2 * p + half
                    sb = self.psb[h % 2]
                    sbk = f"psb/{h % 2}"
                    db = self.psb[2 + h % 2]
                    dbk = f"psb/{2 + h % 2}"
                    P.op("tensor", lambda e, c=c, p=p, half=half, sb=sb: e.matmul(
                        sb[:, 0:128], kfm(p, half, c * 128, 128), qfm(p, half, c * 128, 128),
                        start=True, stop=True),
                        r=kfmk(p, c * 128, 128) + qfmk(p, c * 128, 128), w=(sbk,),
                        tag=f"sc_sT{c}_{h}")
                    P.op("tensor", lambda e, half=half, db=db: e.matmul(
                        db[:, 0:128], self.smv("ones_1x128f", F32, 128, part=1),
                        lrow(half), start=True, stop=True),
                        r=lrow_k(half) + ("ones_1x128f",), w=(dbk,), tag=f"sc_d1{c}_{h}")
                    P.op("vector", lambda e, h=h, db=db: e.scalar_tensor_tensor(
                        out=dmat, in0=db[:, 0:128], scalar=ltok[:, h:h + 1],
                        in1=self.smv("maskbig", F32, 128), op0=Alu.subtract, op1=Alu.min),
                        r=(dbk, "maskbig") + ltok_k, w=dmat_k, tag=f"sc_dmin{c}_{h}")
                    P.op("scalar", lambda e: e.activation(out=dmat, in_=dmat, func=Act.Exp),
                         r=dmat_k, w=dmat_k, tag=f"sc_dexp{c}_{h}")
                    P.op("vector", lambda e, sb=sb: e.tensor_tensor(
                        out=sd, in0=sb[:, 0:128], in1=dmat, op=Alu.mult),
                        r=(sbk,) + dmat_k, w=sd_k, tag=f"sc_sd{c}_{h}")
                    P.op("tensor", lambda e, c=c, h=h: e.matmul(
                        self.ps_st[:, 0:64], sd,
                        self.S1.ap(BF16, 64, byteoff=32768 + c * 2048 + h * 128),
                        start=True, stop=False),
                        r=sd_k + self.S1.K(32768 + c * 2048 + h * 128, 128),
                        w=("ps_st",), tag=f"sc_nv{c}_{h}")
                    P.op("tensor", lambda e, c=c, p=p, half=half: e.matmul(
                        self.ps_st[:, 0:64], qts[half * 64:half * 64 + 64, :],
                        self.smv("szbf", BF16, 64, byteoff=p * 130, part=64, p0=half * 64),
                        start=False, stop=True),
                        r=qts_k + ("szbf",), w=("ps_st",), tag=f"sc_ni{c}_{h}")
                    P.op("tensor", lambda e, c=c, h=h: e.matmul(
                        self.ps_st[:, 128:129], sd, self.smv("onescol_b", BF16, 1),
                        start=True, stop=False),
                        r=sd_k + ("onescol_b",), w=("ps_st",), tag=f"sc_nd{c}_{h}")
                    P.op("tensor", lambda e, c=c, p=p, half=half: e.matmul(
                        self.ps_st[:, 128:129], qts[half * 64:half * 64 + 64, :],
                        self.smv("szbf", BF16, 1, byteoff=p * 130 + 128, part=64, p0=half * 64),
                        start=False, stop=True),
                        r=qts_k + ("szbf",), w=("ps_st",), tag=f"sc_nid{c}_{h}")
                    P.op("vector", lambda e: e.tensor_scalar(
                        out=dcol, in0=self.ps_st[:, 128:129], scalar1=1e-6, scalar2=None,
                        op0=Alu.add),
                        r=("ps_st",), w=dcol_k, tag=f"sc_de{c}_{h}")
                    P.op("vector", lambda e: e.reciprocal(out=rcol, in_=dcol),
                         r=dcol_k, w=dcol_k, tag=f"sc_rc{c}_{h}")
                    P.op("scalar", lambda e, c=c, h=h: e.mul(
                        gla(c, h * 64, 64), self.ps_st[:, 0:64], rcol[:, 0:1]),
                        r=("ps_st",) + dcol_k, w=glak(c, h * 64, 64), tag=f"sc_gv{c}_{h}")
                    pr = half * 64
                    pcol = half * 128
                    P.op("tensor", lambda e, c=c, h=h, pr=pr, pcol=pcol: e.matmul(
                        self.ps_x[pr:pr + 64, pcol:pcol + 64], kttok(64, h * 128),
                        self.S1.ap(BF16, 64, byteoff=32768 + c * 2048 + h * 128),
                        start=True, stop=True),
                        r=self.SC.K(2048 + h * 128, 128) + self.S1.K(32768 + c * 2048 + h * 128, 128),
                        w=("ps_x",), tag=f"sc_sv{c}_{h}")
                    P.op("tensor", lambda e, c=c, h=h, pr=pr, pcol=pcol: e.matmul(
                        self.ps_x[pr:pr + 64, pcol + 64:pcol + 65], kttok(64, h * 128),
                        self.smv("onescol_b", BF16, 1),
                        start=True, stop=True),
                        r=self.SC.K(2048 + h * 128, 128) + ("onescol_b",),
                        w=("ps_x",), tag=f"sc_sz{c}_{h}")
                for half in range(2):
                    pr = half * 64
                    pcol = half * 128
                    P.op("vector", lambda e, p=p, pr=pr, pcol=pcol: e.tensor_tensor(
                        out=self.smv("sz", F32, 65, byteoff=p * 260, part=64, p0=pr),
                        in0=self.smv("szscr", F32, 65, part=64, p0=pr),
                        in1=self.ps_x[pr:pr + 64, pcol:pcol + 65], op=Alu.add),
                        r=("szscr", "ps_x"), w=("sz",), tag=f"sc_su2{c}_{p}_{half}")
                P.op("gpsimd", lambda e, p=p: e.tensor_copy(
                    out=self.smv("szbf", BF16, 65, byteoff=p * 130),
                    in_=self.smv("sz", F32, 65, byteoff=p * 260)),
                    r=("sz",), w=("szbf",), tag=f"sc_su3{c}_{p}")

        if upto == "scan":
            return
        for p in range(NCH):
            for tt in range(NTT):
                P.op("tensor", lambda e, p=p, tt=tt: e.transpose(
                    self.ps_x[:, 0:512].bitcast(BF16)[:, 0:128],
                    gla(tt, p * 128, 128), self.smv("id128b", BF16, 128)),
                    r=glak(tt, p * 128, 128) + ("id128b",), w=("ps_x",), tag=f"gf_tr{p}_{tt}")
                P.op("vector", lambda e, p=p, tt=tt: e.tensor_copy(
                    out=self.PA.ap(BF16, 128, byteoff=p * 4096 + tt * 256),
                    in_=self.ps_x[:, 0:512].bitcast(BF16)[:, 0:128]),
                    r=("ps_x",), w=self.PA.K(p * 4096 + tt * 256, 256), tag=f"gf_ev{p}_{tt}")

        for oc in range(NCH):
            wbf, wkeys = self.stream_w_cols(self.inp["l0_wo"], oc, tag="wo")
            for s in range(4):
                bank = self.psb[(oc * 4 + s) % 4]
                bk = f"psb/{(oc * 4 + s) % 4}"
                for kc in range(NCH):
                    P.op("tensor", lambda e, oc=oc, s=s, kc=kc, bank=bank, wbf=wbf: e.matmul(
                        bank[:, :], wbf(kc),
                        self.PA.ap(BF16, 512, byteoff=kc * 4096 + s * 1024),
                        start=(kc == 0), stop=(kc == NCH - 1)),
                        r=wkeys + self.PA.K(kc * 4096 + s * 1024, 1024), w=(bk,),
                        tag=f"wo_mm{oc}_{s}_{kc}")
                P.op("vector", lambda e, oc=oc, s=s, bank=bank: e.tensor_tensor(
                    out=self.S0.ap(F32, 512, byteoff=oc * 8192 + s * 2048),
                    in0=bank[:, :], in1=self.S0.ap(F32, 512, byteoff=oc * 8192 + s * 2048),
                    op=Alu.add),
                    r=(bk,) + self.S0.K(oc * 8192 + s * 2048, 2048),
                    w=self.S0.K(oc * 8192 + s * 2048, 2048), tag=f"wo_ev{oc}_{s}")

    def stage_l0_ffn(self):
        P = self.P
        ln1 = lambda kc, c0, cols: self.S1.ap(BF16, cols, byteoff=kc * 4096 + c0 * 2)
        ln1k = lambda kc, c0, cols: self.S1.K(kc * 4096 + c0 * 2, cols * 2)
        psx_bf = lambda: self.ps_x[:, 0:512].bitcast(BF16)

        def hid_ap(oc, c0, cols):
            reg = self.PA if oc < 16 else self.PB
            off = (oc if oc < 16 else oc - 16) * 2048
            return reg.ap(BF16, cols, byteoff=off + c0 * 2)

        def hid_k(oc, c0, cols):
            reg = self.PA if oc < 16 else self.PB
            off = (oc if oc < 16 else oc - 16) * 2048
            return reg.K(off + c0 * 2, cols * 2)

        def stage_wT(hbm, oc, slot, tag):
            wi = self.WP.ap(I32, D, byteoff=0)
            ki = self.WP.K(0, 4096)
            wb = self.WP.ap(BF16, D, byteoff=4096 + 2048 * (slot % 2))
            kb = self.WP.K(4096 + 2048 * (slot % 2), 2048)
            P.op("sync", lambda e, oc=oc, wi=wi: e.dma_start(
                out=wi, in_=hbm.ap()[oc * 128:(oc + 1) * 128, :]),
                w=ki, dma=True, tag=f"{tag}dma{oc}")
            P.op("vector", lambda e, wi=wi, wb=wb: e.tensor_copy(out=wb, in_=wi),
                 r=ki, w=kb, tag=f"{tag}cvt{oc}")
            st_off = (slot % 4) * 2048
            for kc in range(NCH):
                P.op("tensor", lambda e, kc=kc, wb=wb: e.transpose(
                    psx_bf()[:, 0:128], wb[:, kc * 128:(kc + 1) * 128],
                    self.smv("id128b", BF16, 128)),
                    r=kb + ("id128b",), w=("ps_x",), tag=f"{tag}tr{oc}_{kc}")
                P.op("vector", lambda e, kc=kc, st_off=st_off: e.tensor_copy(
                    out=self.SC.ap(BF16, 128, byteoff=st_off + kc * 256),
                    in_=psx_bf()[:, 0:128]),
                    r=("ps_x",), w=self.SC.K(st_off + kc * 256, 256), tag=f"{tag}trev{oc}_{kc}")
            keys = self.SC.K(st_off, 2048)
            return (lambda kc, st_off=st_off: self.SC.ap(BF16, 128, byteoff=st_off + kc * 256)), keys

        for half in range(2):
            t0 = half * 1024
            for oc in range(24):
                gT, gkeys = stage_wT(self.inp["l0f_wg"], oc, 2 * oc, "wg")
                uT, ukeys = stage_wT(self.inp["l0f_wu"], oc, 2 * oc + 1, "wu")
                for s in range(2):
                    c0 = t0 + s * 512
                    bg = self.psb[(oc * 2 + s) % 2]
                    bgk = f"psb/{(oc * 2 + s) % 2}"
                    bu = self.psb[2 + (oc * 2 + s) % 2]
                    buk = f"psb/{2 + (oc * 2 + s) % 2}"
                    for kc in range(NCH):
                        P.op("tensor", lambda e, kc=kc, c0=c0, bg=bg, gT=gT: e.matmul(
                            bg[:, :], gT(kc), ln1(kc, c0, 512),
                            start=(kc == 0), stop=(kc == NCH - 1)),
                            r=gkeys + ln1k(kc, c0, 512), w=(bgk,), tag=f"g_mm{half}_{oc}_{s}")
                        P.op("tensor", lambda e, kc=kc, c0=c0, bu=bu, uT=uT: e.matmul(
                            bu[:, :], uT(kc), ln1(kc, c0, 512),
                            start=(kc == 0), stop=(kc == NCH - 1)),
                            r=ukeys + ln1k(kc, c0, 512), w=(buk,), tag=f"u_mm{half}_{oc}_{s}")
                    sg = self.S1.ap(F32, 512, byteoff=32768 + (s % 2) * 2048)
                    sgk = self.S1.K(32768 + (s % 2) * 2048, 2048)
                    P.op("scalar", lambda e, bg=bg, sg=sg: e.activation(
                        out=sg, in_=bg[:, :], func=Act.Silu, scale=1.0 / 127.0),
                        r=(bgk,), w=sgk, tag=f"sil{half}_{oc}_{s}")
                    P.op("vector", lambda e, oc=oc, s=s, bu=bu, sg=sg: e.tensor_tensor(
                        out=hid_ap(oc, s * 512, 512), in0=bu[:, :], in1=sg, op=Alu.mult),
                        r=(buk,) + sgk, w=hid_k(oc, s * 512, 512), tag=f"hid{half}_{oc}_{s}")
            for dc in range(NCH):
                lhsT = {}
                for q in range(4):
                    wi = self.WP.ap(I32, 768, byteoff=0)
                    ki = self.WP.K(0, 3072)
                    wb = self.WP.ap(BF16, 768, byteoff=4096 + 2048 * (q % 2))
                    kb = self.WP.K(4096 + 2048 * (q % 2), 1536)
                    P.op("sync", lambda e, dc=dc, q=q, wi=wi: e.dma_start(
                        out=wi, in_=self.inp["l0f_wd"].ap()[dc * 128:(dc + 1) * 128,
                                                            q * 768:(q + 1) * 768]),
                        w=ki, dma=True, tag=f"wd_dma{dc}_{q}")
                    P.op("vector", lambda e, wi=wi, wb=wb: e.tensor_copy(out=wb, in_=wi),
                         r=ki, w=kb, tag=f"wd_cvt{dc}_{q}")
                    for j in range(6):
                        hc = q * 6 + j
                        st_off = (hc % 24) * 256
                        P.op("tensor", lambda e, j=j, wb=wb: e.transpose(
                            psx_bf()[:, 0:128], wb[:, j * 128:(j + 1) * 128],
                            self.smv("id128b", BF16, 128)),
                            r=kb + ("id128b",), w=("ps_x",), tag=f"wd_tr{dc}_{hc}")
                        P.op("vector", lambda e, st_off=st_off: e.tensor_copy(
                            out=self.SC.ap(BF16, 128, byteoff=st_off),
                            in_=psx_bf()[:, 0:128]),
                            r=("ps_x",), w=self.SC.K(st_off, 256), tag=f"wd_trev{dc}_{hc}")
                        lhsT[hc] = (self.SC.ap(BF16, 128, byteoff=st_off),
                                    self.SC.K(st_off, 256))
                for s in range(2):
                    c0 = t0 + s * 512
                    bank = self.psb[(dc * 2 + s) % 4]
                    bk = f"psb/{(dc * 2 + s) % 4}"
                    for hc in range(24):
                        ap_, kk = lhsT[hc]
                        P.op("tensor", lambda e, hc=hc, c0=c0, t0=t0, bank=bank, ap_=ap_: e.matmul(
                            bank[:, :], ap_, hid_ap(hc, c0 - t0, 512),
                            start=(hc == 0), stop=(hc == 23)),
                            r=kk + hid_k(hc, c0 - t0, 512), w=(bk,), tag=f"d_mm{half}_{dc}_{hc}")
                    P.op("vector", lambda e, dc=dc, c0=c0, bank=bank: e.scalar_tensor_tensor(
                        out=self.S0.ap(F32, 512, byteoff=dc * 8192 + c0 * 4),
                        in0=bank[:, :], scalar=1.0 / 16129.0,
                        in1=self.S0.ap(F32, 512, byteoff=dc * 8192 + c0 * 4),
                        op0=Alu.mult, op1=Alu.add),
                        r=(bk,) + self.S0.K(dc * 8192 + c0 * 4, 2048),
                        w=self.S0.K(dc * 8192 + c0 * 4, 2048), tag=f"d_ev{half}_{dc}_{s}")

    def stage_l1_attn(self, nlayers=4, upto='all'):
        P = self.P

        for c in range(NCH):
            P.op("sync", lambda e, c=c: e.dma_start(
                out=self.xspill_h.ap()[c], in_=self.S0.ap(F32, T, byteoff=c * 8192)),
                r=self.S0.K(c * 8192, 8192), dma=True, tag=f"xsp{c}")

        h_ap = lambda kc, c0, cols: self.S1.ap(F32, cols, byteoff=kc * 8192 + c0 * 4)
        h_k = lambda kc, c0, cols: self.S1.K(kc * 8192 + c0 * 4, cols * 4)
        hbf = lambda kc, c0, cols: self.S0.ap(BF16, cols, byteoff=kc * 4096 + c0 * 2)
        hbf_k = lambda kc, c0, cols: self.S0.K(kc * 4096 + c0 * 2, cols * 2)
        r_ap = hbf
        r_k = hbf_k
        kfm = lambda kc, c0, cols: self.PA.ap(BF16, cols, byteoff=kc * 4096 + c0 * 2)
        kfm_k = lambda kc, c0, cols: self.PA.K(kc * 4096 + c0 * 2, cols * 2)
        vtok = lambda tt, c0, cols: self.PB.ap(BF16, cols, byteoff=tt * 2048 + c0 * 2)
        vtok_k = lambda tt, c0, cols: self.PB.K(tt * 2048 + c0 * 2, cols * 2)
        wvst = lambda kc, c0, cols: self.S0.ap(BF16, cols, byteoff=32768 + kc * 2048 + c0 * 2)
        wvst_k = lambda kc, c0, cols: self.S0.K(32768 + kc * 2048 + c0 * 2, cols * 2)
        qsl = lambda b, kc, c0, cols: self.S0.ap(BF16, cols, byteoff=49152 + b * 4096 + kc * 512 + c0 * 2)
        qsl_k = lambda b, kc, c0, cols: self.S0.K(49152 + b * 4096 + kc * 512 + c0 * 2, cols * 2)
        scB = self.S0.ap(F32, 256, byteoff=57344)
        scB_k = self.S0.K(57344, 1024)
        psx_bf = lambda: self.ps_x[:, 0:512].bitcast(BF16)

        def stream_cols(src_ap2d, oc, slot, tag, n_in=D):
            nch = n_in // 128
            wf32 = self.WP.ap(F32, nch * 128, byteoff=0)
            kf = self.WP.K(0, nch * 512)
            wb_off = 4096 + 2048 * (slot % 2)
            wbf = self.WP.ap(BF16, nch * 128, byteoff=wb_off)
            kb = self.WP.K(wb_off, nch * 256)
            src = src_ap2d[:, oc * 128:(oc + 1) * 128].rearrange("(kc p) m -> p kc m", p=128)
            P.op("sync", lambda e, src=src, wf32=wf32, nch=nch: e.dma_start(
                out=wf32.rearrange("p (kc m) -> p kc m", kc=nch), in_=src),
                w=kf, dma=True, tag=f"{tag}dma{oc}")
            P.op("vector", lambda e, wf32=wf32, wbf=wbf: e.tensor_copy(out=wbf, in_=wf32),
                 r=kf, w=kb, tag=f"{tag}cvt{oc}")
            return (lambda kc, wb_off=wb_off: self.WP.ap(BF16, 128, byteoff=wb_off + kc * 256)), kb

        def proj_fm(src_ap2d, oc_list, dst_ap, dst_k, tag, mov_ap, mov_k, nstr=4,
                    evac="copy", bias_col=None, bias_keys=(), extra=None, scale=None):
            for i, oc in enumerate(oc_list):
                wbf, wkeys = stream_cols(src_ap2d, oc, i, tag)
                for s in range(nstr):
                    bank = self.psb[(i * nstr + s) % 4]
                    bk = f"psb/{(i * nstr + s) % 4}"
                    for kc in range(NCH):
                        P.op("tensor", lambda e, oc=oc, s=s, kc=kc, bank=bank, wbf=wbf: e.matmul(
                            bank[:, :], wbf(kc), mov_ap(kc, s * 512, 512),
                            start=(kc == 0), stop=(kc == NCH - 1)),
                            r=wkeys + mov_k(kc, s * 512, 512), w=(bk,), tag=f"{tag}mm{oc}_{s}")
                    if evac == "copy":
                        P.op("vector", lambda e, oc=oc, s=s, bank=bank: e.tensor_copy(
                            out=dst_ap(oc, s * 512, 512), in_=bank[:, :]),
                            r=(bk,), w=dst_k(oc, s * 512, 512), tag=f"{tag}ev{oc}_{s}")
                    elif evac == "scale":
                        P.op("vector", lambda e, oc=oc, s=s, bank=bank: e.tensor_scalar(
                            out=dst_ap(oc, s * 512, 512), in0=bank[:, :],
                            scalar1=scale, scalar2=None, op0=Alu.mult),
                            r=(bk,), w=dst_k(oc, s * 512, 512), tag=f"{tag}ev{oc}_{s}")
                    elif evac == "custom":
                        extra(oc, s, bank, bk)

        for l in range(nlayers):
            wkqv2d = self.inp["l1_wkqv"].ap()[l]
            wm1_2d = self.inp["l1_wm1"].ap()[l]
            wm2_2d = self.inp["l1_wm2"].ap()[l]
            wf1_2d = self.inp["l1_wf1"].ap()[l]
            wf2_2d = self.inp["l1_wf2"].ap()[l]

            def load_bias(hbm_row, n, col0, tag):
                nch = n // 128
                P.op("sync", lambda e, hbm_row=hbm_row, nch=nch: e.dma_start(
                    out=self.smv("bias_nat", F32, 128, part=nch),
                    in_=hbm_row.rearrange("(c p) -> c p", p=128)),
                    w=("bias_nat",), dma=True, tag=f"b_{tag}dma")
                P.op("tensor", lambda e, nch=nch: e.transpose(
                    self.ps_x[0:128, 0:nch], self.smv("bias_nat", F32, 128, part=nch),
                    self.smv("id16f", F32, nch, part=nch)),
                    r=("bias_nat", "id16f"), w=("ps_x",), tag=f"b_{tag}tr")
                P.op("vector", lambda e, nch=nch, col0=col0: e.tensor_copy(
                    out=self.smv("biasT", F32, nch, byteoff=col0 * 4),
                    in_=self.ps_x[0:128, 0:nch]),
                    r=("ps_x",), w=("biasT",), tag=f"b_{tag}ev")

            load_bias(self.inp["l1_bm1"].ap()[l], 2048, 0, f"bm1_{l}")
            load_bias(self.inp["l1_bm2"].ap()[l], D, 16, f"bm2_{l}")
            load_bias(self.inp["l1_gng"].ap()[l], D, 24, f"gng_{l}")
            load_bias(self.inp["l1_gnb"].ap()[l], D, 32, f"gnb_{l}")
            load_bias(self.inp["l1_bf1"].ap()[l], 2048, 40, f"bf1_{l}")
            load_bias(self.inp["l1_bf2"].ap()[l], D, 56, f"bf2_{l}")
            load_bias(self.inp["l1_lng"].ap()[l], D, 64, f"lng_{l}")
            load_bias(self.inp["l1_lnb"].ap()[l], D, 72, f"lnb_{l}")
            bcol = lambda c: self.smv("biasT", F32, 1, byteoff=c * 4)

            for kc in range(NCH):
                for s in range(2):
                    P.op("vector", lambda e, kc=kc, s=s: e.tensor_copy(
                        out=hbf(kc, s * 1024, 1024), in_=h_ap(kc, s * 1024, 1024)),
                        r=h_k(kc, s * 1024, 1024), w=hbf_k(kc, s * 1024, 1024),
                        tag=f"hbf{l}_{kc}_{s}")

            for kc in range(NCH):
                wf32 = self.WP.ap(F32, D, byteoff=(kc % 2) * 4096)
                kf = self.WP.K((kc % 2) * 4096, 4096)
                P.op("sync", lambda e, kc=kc, wf32=wf32, wkqv2d=wkqv2d: e.dma_start(
                    out=wf32, in_=wkqv2d[kc * 128:(kc + 1) * 128, 2048:3072]),
                    w=kf, dma=True, tag=f"wv{l}dma{kc}")
                P.op("vector", lambda e, kc=kc, wf32=wf32: e.tensor_copy(
                    out=wvst(kc, 0, D), in_=wf32),
                    r=kf, w=wvst_k(kc, 0, D), tag=f"wv{l}cvt{kc}")
            for tt in range(NTT):
                for half in range(2):
                    bank = self.ps_ab[half]
                    bkk = f"ps_ab{half}"
                    for kc in range(NCH):
                        P.op("tensor", lambda e, tt=tt, half=half, kc=kc, bank=bank: e.matmul(
                            bank[:, :], hbf(kc, tt * 128, 128), wvst(kc, half * 512, 512),
                            start=(kc == 0), stop=(kc == NCH - 1)),
                            r=hbf_k(kc, tt * 128, 128) + wvst_k(kc, half * 512, 512),
                            w=(bkk,), tag=f"v{l}mm{tt}_{half}")
                    P.op("vector", lambda e, tt=tt, half=half, bank=bank: e.tensor_copy(
                        out=vtok(tt, half * 512, 512), in_=bank[:, :]),
                        r=(bkk,), w=vtok_k(tt, half * 512, 512), tag=f"v{l}ev{tt}_{half}")
            qwst_k = lambda oc: self.S0.K(32768 + oc * 2048, 2048)
            for oc in range(NCH):
                qstg = self.S0.ap(F32, 1024, byteoff=49152 + (oc % 2) * 4096)
                qsk = self.S0.K(49152 + (oc % 2) * 4096, 4096)
                src = wkqv2d[:, 1024 + oc * 128:1024 + (oc + 1) * 128].rearrange(
                    "(kc p) m -> p kc m", p=128)
                P.op("sync", lambda e, src=src, qstg=qstg: e.dma_start(
                    out=qstg.rearrange("p (kc m) -> p kc m", kc=NCH), in_=src),
                    w=qsk, dma=True, tag=f"qw{l}dma{oc}")
                P.op("vector", lambda e, oc=oc, qstg=qstg: e.tensor_copy(
                    out=self.S0.ap(BF16, 1024, byteoff=32768 + oc * 2048), in_=qstg),
                    r=qsk, w=qwst_k(oc), tag=f"qw{l}cvt{oc}")
            proj_fm(wkqv2d, list(range(8)), kfm, kfm_k, f"k{l}_",
                    hbf, hbf_k, evac="copy")

            for g in range(8):
                qb = g % 2
                for oc in range(NCH):
                    bank = self.psb[oc % 2]
                    bk = f"psb/{oc % 2}"
                    for kc in range(NCH):
                        P.op("tensor", lambda e, g=g, oc=oc, kc=kc, bank=bank: e.matmul(
                            bank[:, 0:256],
                            self.S0.ap(BF16, 128, byteoff=32768 + oc * 2048 + kc * 256),
                            hbf(kc, g * 256, 256),
                            start=(kc == 0), stop=(kc == NCH - 1)),
                            r=qwst_k(oc) + hbf_k(kc, g * 256, 256), w=(bk,), tag=f"q{l}mm{g}_{oc}")
                    P.op("vector", lambda e, g=g, oc=oc, bank=bank, qb=qb: e.tensor_scalar(
                        out=qsl(qb, oc, 0, 256), in0=bank[:, 0:256],
                        scalar1=1.0 / 32.0, scalar2=None, op0=Alu.mult),
                        r=(bk,), w=qsl_k(qb, oc, 0, 256), tag=f"q{l}ev{g}_{oc}")
                nkt = 2 * g + 2
                for kt in range(nkt):
                    bank = self.psb[2 + kt % 2]
                    bk = f"psb/{2 + kt % 2}"
                    for kc in range(NCH):
                        P.op("tensor", lambda e, g=g, kt=kt, kc=kc, bank=bank, qb=qb: e.matmul(
                            bank[:, 0:256], kfm(kc, kt * 128, 128), qsl(qb, kc, 0, 256),
                            start=(kc == 0), stop=(kc == NCH - 1)),
                            r=kfm_k(kc, kt * 128, 128) + qsl_k(qb, kc, 0, 256),
                            w=(bk,), tag=f"s{l}mm{g}_{kt}")
                    if kt >= 2 * g:
                        dk = kt - 2 * g
                        P.op("vector", lambda e, kt=kt, dk=dk, bank=bank: e.tensor_tensor(
                            out=self.SC.ap(BF16, 256, byteoff=kt * 512),
                            in0=bank[:, 0:256], in1=self.smv(f"trimask{dk}", BF16, 256),
                            op=Alu.mult),
                            r=(bk, f"trimask{dk}"), w=self.SC.K(kt * 512, 512),
                            tag=f"s{l}ev{g}_{kt}")
                    else:
                        P.op("vector", lambda e, kt=kt, bank=bank: e.tensor_copy(
                            out=self.SC.ap(BF16, 256, byteoff=kt * 512), in_=bank[:, 0:256]),
                            r=(bk,), w=self.SC.K(kt * 512, 512), tag=f"s{l}ev{g}_{kt}")
                for kt in range(nkt):
                    P.op("tensor", lambda e, kt=kt, nkt=nkt: e.matmul(
                        self.ps_st[0:1, 0:256], self.smv("onescol_b", BF16, 1),
                        self.SC.ap(BF16, 256, byteoff=kt * 512),
                        start=(kt == 0), stop=(kt == nkt - 1)),
                        r=self.SC.K(kt * 512, 512) + ("onescol_b",), w=("ps_st",),
                        tag=f"d{l}mm{g}_{kt}")
                P.op("sync", lambda e, g=g: e.dma_start(
                    out=self.smv("rnrow", F32, 256, part=1),
                    in_=self.const_h["rn"].ap()[0:1, g * 256:(g + 1) * 256]),
                    w=("rnrow",), dma=True, tag=f"rn{l}{g}")
                lr = lambda: self.smv("lrow2", F32, 256, part=1)
                P.op("scalar", lambda e: e.activation(
                    out=lr(), in_=self.ps_st[0:1, 0:256], func=Act.Abs),
                    r=("ps_st",), w=("lrow2/0",), tag=f"sc{l}abs{g}")
                P.op("vector", lambda e: e.tensor_tensor(
                    out=lr(), in0=lr(), in1=self.smv("rnrow", F32, 256, part=1),
                    op=Alu.max),
                    r=("lrow2/0", "rnrow"), w=("lrow2/0",), tag=f"sc{l}max{g}")
                P.op("vector", lambda e: e.reciprocal(out=lr(), in_=lr()),
                     r=("lrow2/0",), w=("lrow2/0",), tag=f"sc{l}rc{g}")
                P.op("tensor", lambda e: e.matmul(
                    self.ps_ab[0][:, 0:256], self.smv("ones_1x128f", F32, 128, part=1),
                    lr(), start=True, stop=True),
                    r=("lrow2/0", "ones_1x128f"), w=("ps_ab0",), tag=f"sc{l}bc{g}")
                P.op("vector", lambda e: e.tensor_copy(out=scB, in_=self.ps_ab[0][:, 0:256]),
                     r=("ps_ab0",), w=scB_k, tag=f"sc{l}ev{g}")
                for vc in range(NCH):
                    bank = self.psb[vc % 2]
                    bk = f"psb/{vc % 2}"
                    for kt in range(nkt):
                        P.op("tensor", lambda e, g=g, vc=vc, kt=kt, nkt=nkt, bank=bank: e.matmul(
                            bank[:, 0:256], vtok(kt, vc * 128, 128),
                            self.SC.ap(BF16, 256, byteoff=kt * 512),
                            start=(kt == 0), stop=(kt == nkt - 1)),
                            r=vtok_k(kt, vc * 128, 128) + self.SC.K(kt * 512, 512),
                            w=(bk,), tag=f"r{l}mm{g}_{vc}_{kt}")
                    P.op("vector", lambda e, g=g, vc=vc, bank=bank: e.tensor_tensor(
                        out=r_ap(vc, g * 256, 256), in0=bank[:, 0:256], in1=scB,
                        op=Alu.mult),
                        r=(bk,) + scB_k, w=r_k(vc, g * 256, 256), tag=f"r{l}ev{g}_{vc}")

            for oc in range(NCH):
                aT, akeys = stream_cols(wm1_2d, oc, 0, f"a{l}_")
                gT, gkeys = stream_cols(wm1_2d, oc + 8, 1, f"g{l}_")
                for s in range(4):
                    ba_ = self.psb[s % 2]
                    bak = f"psb/{s % 2}"
                    bg_ = self.psb[2 + s % 2]
                    bgk = f"psb/{2 + s % 2}"
                    for kc in range(NCH):
                        P.op("tensor", lambda e, oc=oc, s=s, kc=kc, ba_=ba_, aT=aT: e.matmul(
                            ba_[:, :], aT(kc), r_ap(kc, s * 512, 512),
                            start=(kc == 0), stop=(kc == NCH - 1)),
                            r=akeys + r_k(kc, s * 512, 512), w=(bak,), tag=f"a{l}mm{oc}_{s}")
                        P.op("tensor", lambda e, oc=oc, s=s, kc=kc, bg_=bg_, gT=gT: e.matmul(
                            bg_[:, :], gT(kc), r_ap(kc, s * 512, 512),
                            start=(kc == 0), stop=(kc == NCH - 1)),
                            r=gkeys + r_k(kc, s * 512, 512), w=(bgk,), tag=f"g{l}mm{oc}_{s}")
                    sg = self.S0.ap(F32, 512, byteoff=57344 + 2048)
                    sgk = self.S0.K(57344 + 2048, 2048)
                    P.op("scalar", lambda e, oc=oc, bg_=bg_, sg=sg: e.activation(
                        out=sg, in_=bg_[:, :], func=Act.Silu, bias=bcol(8 + oc), scale=1.0),
                        r=(bgk, "biasT"), w=sgk, tag=f"sil{l}_{oc}_{s}")
                    P.op("vector", lambda e, oc=oc, s=s, ba_=ba_, sg=sg: e.scalar_tensor_tensor(
                        out=self.PA.ap(BF16, 512, byteoff=oc * 4096 + s * 1024),
                        in0=ba_[:, :], scalar=bcol(oc), in1=sg,
                        op0=Alu.add, op1=Alu.mult),
                        r=(bak, "biasT") + sgk,
                        w=self.PA.K(oc * 4096 + s * 1024, 1024), tag=f"pr{l}_{oc}_{s}")

            prod_ap = lambda kc, c0, cols: self.PA.ap(BF16, cols, byteoff=kc * 4096 + c0 * 2)
            prod_k = lambda kc, c0, cols: self.PA.K(kc * 4096 + c0 * 2, cols * 2)
            m2_ap = lambda kc, c0, cols: self.PB.ap(BF16, cols, byteoff=kc * 4096 + c0 * 2)
            m2_k = lambda kc, c0, cols: self.PB.K(kc * 4096 + c0 * 2, cols * 2)
            nmm = [0]
            for oc in range(NCH):
                wT, wkeys = stream_cols(wm2_2d, oc, oc, f"m2{l}_")
                for s in range(4):
                    bank = self.psb[(oc * 4 + s) % 4]
                    bk = f"psb/{(oc * 4 + s) % 4}"
                    for kc in range(NCH):
                        P.op("tensor", lambda e, oc=oc, s=s, kc=kc, bank=bank, wT=wT: e.matmul(
                            bank[:, :], wT(kc), prod_ap(kc, s * 512, 512),
                            start=(kc == 0), stop=(kc == NCH - 1)),
                            r=wkeys + prod_k(kc, s * 512, 512), w=(bk,), tag=f"m2{l}mm{oc}_{s}")
                    P.op("vector", lambda e, oc=oc, s=s, bank=bank: e.tensor_scalar(
                        out=m2_ap(oc, s * 512, 512), in0=bank[:, :],
                        scalar1=bcol(16 + oc), scalar2=None, op0=Alu.add),
                        r=(bk, "biasT"), w=m2_k(oc, s * 512, 512), tag=f"m2{l}ev{oc}_{s}")
            for oc in range(NCH):
                for s in range(4):
                    i = oc * 4 + s
                    P.op("scalar", lambda e, oc=oc, s=s: e.activation(
                        out=self.SC.ap(BF16, 512, byteoff=0), in_=m2_ap(oc, s * 512, 512),
                        func=Act.Square),
                        r=m2_k(oc, s * 512, 512), w=self.SC.K(0, 1024), tag=f"gnsq{l}_{oc}_{s}")
                    P.op("tensor", lambda e, i=i, oc=oc, s=s: e.matmul(
                        self.ps_st[0:1, 0:512], self.smv("onescol_b", BF16, 1),
                        m2_ap(oc, s * 512, 512), start=(i == 0), stop=(i == 31)),
                        r=m2_k(oc, s * 512, 512) + ("onescol_b",), w=("ps_st",),
                        tag=f"gns{l}_{oc}_{s}")
                    P.op("tensor", lambda e, i=i: e.matmul(
                        self.ps_x[0:1, 0:512], self.smv("onescol_b", BF16, 1),
                        self.SC.ap(BF16, 512, byteoff=0), start=(i == 0), stop=(i == 31)),
                        r=self.SC.K(0, 1024) + ("onescol_b",), w=("ps_x",),
                        tag=f"gnq{l}_{oc}_{s}")
            gs = lambda i: self.smv("srows", F32, 1, part=1, byteoff=i * 4)
            P.op("vector", lambda e: e.reduce_sum(
                out=gs(0), in_=self.ps_st[0:1, 0:512], axis=AX.X, op=Alu.add)
                if False else e.tensor_reduce(
                out=gs(0), in_=self.ps_st[0:1, 0:512], axis=AX.X, op=Alu.add),
                r=("ps_st",), w=("srows",), tag=f"gnr1{l}")
            P.op("vector", lambda e: e.tensor_reduce(
                out=gs(1), in_=self.ps_x[0:1, 0:512], axis=AX.X, op=Alu.add),
                r=("ps_x",), w=("srows",), tag=f"gnr2{l}")
            NT2 = float(T * D)
            P.op("vector", lambda e: e.tensor_scalar(
                out=gs(0), in0=gs(0), scalar1=1.0 / NT2, scalar2=None, op0=Alu.mult),
                r=("srows",), w=("srows",), tag=f"gnmu{l}")
            P.op("vector", lambda e: e.tensor_tensor(
                out=gs(2), in0=gs(0), in1=gs(0), op=Alu.mult),
                r=("srows",), w=("srows",), tag=f"gnmsq{l}")
            P.op("vector", lambda e: e.scalar_tensor_tensor(
                out=gs(1), in0=gs(1), scalar=1.0 / NT2, in1=gs(2),
                op0=Alu.mult, op1=Alu.subtract),
                r=("srows",), w=("srows",), tag=f"gnvar{l}")
            epsap, epsk = self.cc(1e-5)
            P.op("scalar", lambda e, epsap=epsap: e.activation(
                out=gs(1), in_=gs(1), func=Act.Ln, bias=epsap[0:1, 0:1], scale=1.0),
                r=("srows", epsk), w=("srows",), tag=f"gnln{l}")
            P.op("scalar", lambda e: e.activation(
                out=gs(1), in_=gs(1), func=Act.Exp, scale=-0.5),
                r=("srows",), w=("srows",), tag=f"gnrs{l}")
            P.op("tensor", lambda e: e.matmul(
                self.ps_x[:, 0:2], self.smv("ones_1x128f", F32, 128, part=1),
                self.smv("srows", F32, 2, part=1), start=True, stop=True),
                r=("srows", "ones_1x128f"), w=("ps_x",), tag=f"gnbc{l}")
            P.op("vector", lambda e: e.tensor_copy(
                out=self.smv("ecol", F32, 2), in_=self.ps_x[:, 0:2]),
                r=("ps_x",), w=("ecol",), tag=f"gnbe{l}")
            mu_c = lambda: self.smv("ecol", F32, 1)
            rs_c = lambda: self.smv("ecol", F32, 1, byteoff=4)
            P.op("vector", lambda e: e.tensor_scalar(
                out=self.smv("ecol", F32, 8, byteoff=8),
                in0=self.smv("biasT", F32, 8, byteoff=24 * 4),
                scalar1=rs_c(), scalar2=None, op0=Alu.mult),
                r=("biasT", "ecol"), w=("ecol",), tag=f"gnA{l}")
            P.op("vector", lambda e: e.scalar_tensor_tensor(
                out=self.smv("ecol", F32, 8, byteoff=40),
                in0=self.smv("ecol", F32, 8, byteoff=8), scalar=mu_c(),
                in1=self.smv("biasT", F32, 8, byteoff=32 * 4),
                op0=Alu.mult, op1=Alu.subtract),
                r=("ecol", "biasT"), w=("ecol",), tag=f"gnB{l}")
            Acol = lambda c: self.smv("ecol", F32, 1, byteoff=8 + c * 4)
            Bcol = lambda c: self.smv("ecol", F32, 1, byteoff=40 + c * 4)
            y_ap = lambda kc, c0, cols: self.S0.ap(F32, cols, byteoff=kc * 8192 + c0 * 4)
            y_k = lambda kc, c0, cols: self.S0.K(kc * 8192 + c0 * 4, cols * 4)
            for kc in range(NCH):
                for s in range(4):
                    P.op("vector", lambda e, kc=kc, s=s: e.tensor_scalar(
                        out=self.SC.ap(F32, 512, byteoff=2048),
                        in0=m2_ap(kc, s * 512, 512), scalar1=Acol(kc), scalar2=Bcol(kc),
                        op0=Alu.mult, op1=Alu.subtract),
                        r=m2_k(kc, s * 512, 512) + ("ecol",), w=self.SC.K(2048, 2048),
                        tag=f"y{l}a{kc}_{s}")
                    P.op("vector", lambda e, kc=kc, s=s: e.tensor_tensor(
                        out=y_ap(kc, s * 512, 512), in0=self.SC.ap(F32, 512, byteoff=2048),
                        in1=h_ap(kc, s * 512, 512), op=Alu.add),
                        r=self.SC.K(2048, 2048) + h_k(kc, s * 512, 512),
                        w=y_k(kc, s * 512, 512), tag=f"y{l}b{kc}_{s}")

            ybf = lambda kc, c0, cols: self.PB.ap(BF16, cols, byteoff=kc * 4096 + c0 * 2)
            ybf_k = lambda kc, c0, cols: self.PB.K(kc * 4096 + c0 * 2, cols * 2)
            for sp in range(2):
                for sl in range(2):
                    s = sp * 2 + sl
                    for kc in range(NCH):
                        P.op("vector", lambda e, kc=kc, s=s: e.tensor_copy(
                            out=ybf(kc, s * 512, 512), in_=y_ap(kc, s * 512, 512)),
                            r=y_k(kc, s * 512, 512), w=ybf_k(kc, s * 512, 512),
                            tag=f"ybf{l}_{kc}_{s}")
                for fo in range(16):
                    wT, wkeys = stream_cols(wf1_2d, fo, fo, f"f1{l}p{sp}_")
                    for sl in range(2):
                        s = sp * 2 + sl
                        bank = self.psb[2 * (fo % 2) + sl]
                        bk = f"psb/{2 * (fo % 2) + sl}"
                        for kc in range(NCH):
                            P.op("tensor", lambda e, fo=fo, s=s, kc=kc, bank=bank, wT=wT: e.matmul(
                                bank[:, :], wT(kc), ybf(kc, s * 512, 512),
                                start=(kc == 0), stop=(kc == NCH - 1)),
                                r=wkeys + ybf_k(kc, s * 512, 512), w=(bk,), tag=f"f1{l}mm{s}_{fo}")
                        P.op("scalar", lambda e, fo=fo, sl=sl, bank=bank: e.activation(
                            out=self.PA.ap(BF16, 512, byteoff=fo * 2048 + sl * 1024),
                            in_=bank[:, :],
                            func=Act.Gelu, bias=bcol(40 + fo), scale=1.0),
                            r=(bk, "biasT"), w=self.PA.K(fo * 2048 + sl * 1024, 1024),
                            tag=f"f1{l}ev{s}_{fo}")
                for oc in range(NCH):
                    banks = (self.ps_ab[0], self.ps_ab[1])
                    bks = ("ps_ab0", "ps_ab1")
                    for hhalf in range(2):
                        wT, wkeys = stream_cols(
                            wf2_2d[hhalf * 1024:(hhalf + 1) * 1024, :], oc,
                            2 * oc + hhalf, f"f2{l}p{sp}h{hhalf}_")
                        for kc in range(NCH):
                            kci = hhalf * 8 + kc
                            for sl in range(2):
                                P.op("tensor", lambda e, kci=kci, kc=kc, sl=sl, bank=banks[sl],
                                     wT=wT: e.matmul(
                                    bank[:, :], wT(kc),
                                    self.PA.ap(BF16, 512, byteoff=kci * 2048 + sl * 1024),
                                    start=(kci == 0), stop=(kci == 15)),
                                    r=wkeys + self.PA.K(kci * 2048 + sl * 1024, 1024),
                                    w=(bks[sl],), tag=f"f2{l}mm{sp}_{oc}_{kci}_{sl}")
                    for sl in range(2):
                        s = sp * 2 + sl
                        P.op("vector", lambda e, oc=oc, s=s, bank=banks[sl]: e.scalar_tensor_tensor(
                            out=y_ap(oc, s * 512, 512), in0=bank[:, :], scalar=bcol(56 + oc),
                            in1=y_ap(oc, s * 512, 512), op0=Alu.add, op1=Alu.add),
                            r=(bks[sl], "biasT") + y_k(oc, s * 512, 512),
                            w=y_k(oc, s * 512, 512), tag=f"f2{l}ev{s}_{oc}")
            self.ln_fm(f"iln{l}", self.S0, lambda c: c * 8192,
                       self.S1, lambda c: c * 8192, F32,
                       scratch=(self.SC, 0),
                       gamma=lambda c: self.smv("biasT", F32, 1, byteoff=(64 + c) * 4),
                       beta=lambda c: self.smv("biasT", F32, 1, byteoff=(72 + c) * 4),
                       gb_keys=("biasT",))

        if upto == "h":
            return
        for c in range(NCH):
            P.op("sync", lambda e, c=c: e.dma_start(
                out=self.S0.ap(F32, T, byteoff=c * 8192), in_=self.xspill_h.ap()[c]),
                w=self.S0.K(c * 8192, 8192), dma=True, tag=f"xrs{c}")
            for s in range(4):
                P.op("vector", lambda e, c=c, s=s: e.tensor_tensor(
                    out=self.S0.ap(F32, 512, byteoff=c * 8192 + s * 2048),
                    in0=self.S0.ap(F32, 512, byteoff=c * 8192 + s * 2048),
                    in1=self.S1.ap(F32, 512, byteoff=c * 8192 + s * 2048), op=Alu.add),
                    r=self.S0.K(c * 8192 + s * 2048, 2048) + self.S1.K(c * 8192 + s * 2048, 2048),
                    w=self.S0.K(c * 8192 + s * 2048, 2048), tag=f"xadd{c}_{s}")

    def quant_chain(self, x_ap, x_keys, rdq2_col, dqh_col, col_keys,
                    out_ap, out_keys, scr, scr_off, tag, ncols=1024):
        P = self.P
        nb = ncols // 16
        CM = float(3 * 2 ** 22)
        buf = lambda i: scr.ap(F32, ncols, byteoff=scr_off + i * 4096)
        bk = lambda i: scr.K(scr_off + i * 4096, 4096)
        kbuf = lambda i: scr.ap(BF16, ncols, byteoff=scr_off + 16384 + i * 2048)
        kk = lambda i: scr.K(scr_off + 16384 + i * 2048, 2048)
        g3 = lambda ap_: ap_.rearrange("p (g s) -> p g s", s=16)

        def bc3(col_ap):
            import concourse.bass as _b
            pd = list(col_ap.ap[0])
            return _b.AP(col_ap.tensor, col_ap.offset, [pd, [1, nb], [0, 16]])

        P.op("vector", lambda e: e.tensor_tensor(
            out=g3(buf(0)), in0=g3(x_ap), in1=bc3(rdq2_col), op=Alu.mult),
            r=x_keys + col_keys, w=bk(0), tag=f"{tag}u")
        P.op("scalar", lambda e: e.activation(out=buf(1), in_=buf(0), func=Act.Abs),
             r=bk(0), w=bk(1), tag=f"{tag}abs")
        P.op("vector", lambda e: e.tensor_scalar(
            out=buf(2), in0=buf(1), scalar1=4.0, scalar2=0.0, op0=Alu.subtract, op1=Alu.max),
            r=bk(1), w=bk(2), tag=f"{tag}r1")
        P.op("vector", lambda e: e.tensor_scalar(
            out=buf(3), in0=buf(1), scalar1=8.0, scalar2=0.0, op0=Alu.subtract, op1=Alu.max),
            r=bk(1), w=bk(3), tag=f"{tag}r2")
        P.op("vector", lambda e: e.scalar_tensor_tensor(
            out=buf(2), in0=buf(3), scalar=0.5, in1=buf(2), op0=Alu.mult, op1=Alu.add),
            r=bk(2) + bk(3), w=bk(2), tag=f"{tag}w2")
        P.op("vector", lambda e: e.scalar_tensor_tensor(
            out=buf(0), in0=buf(2), scalar=-0.5, in1=buf(1), op0=Alu.mult, op1=Alu.add),
            r=bk(1) + bk(2), w=bk(0), tag=f"{tag}g")
        P.op("vector", lambda e: e.tensor_scalar(
            out=buf(1), in0=buf(0), scalar1=CM, scalar2=CM, op0=Alu.add, op1=Alu.subtract),
            r=bk(0), w=bk(1), tag=f"{tag}k")
        P.op("vector", lambda e: e.tensor_scalar(
            out=kbuf(0), in0=buf(1), scalar1=7.0, scalar2=0.0, op0=Alu.min, op1=Alu.max),
            r=bk(1), w=kk(0), tag=f"{tag}kc")
        P.op("vector", lambda e: e.tensor_scalar(
            out=kbuf(1), in0=kbuf(0), scalar1=4.0, scalar2=0.0, op0=Alu.subtract, op1=Alu.max),
            r=kk(0), w=kk(1), tag=f"{tag}s1")
        P.op("vector", lambda e: e.tensor_scalar(
            out=kbuf(2), in0=kbuf(0), scalar1=6.0, scalar2=0.0, op0=Alu.subtract, op1=Alu.max),
            r=kk(0), w=kk(2), tag=f"{tag}s2")
        P.op("vector", lambda e: e.scalar_tensor_tensor(
            out=kbuf(1), in0=kbuf(2), scalar=2.0, in1=kbuf(1), op0=Alu.mult, op1=Alu.add),
            r=kk(1) + kk(2), w=kk(1), tag=f"{tag}t")
        P.op("gpsimd", lambda e: e.tensor_tensor(
            out=kbuf(0), in0=kbuf(1), in1=kbuf(0), op=Alu.add),
            r=kk(0) + kk(1), w=kk(0), tag=f"{tag}q2")
        P.op("scalar", lambda e: e.activation(out=kbuf(2), in_=x_ap, func=Act.Sign),
             r=x_keys, w=kk(2), tag=f"{tag}sg")
        P.op("vector", lambda e: e.tensor_tensor(
            out=g3(buf(2)), in0=g3(kbuf(0)), in1=bc3(dqh_col), op=Alu.mult),
            r=kk(0) + col_keys, w=bk(2), tag=f"{tag}vd")
        P.op("vector", lambda e: e.tensor_tensor(
            out=out_ap, in0=buf(2), in1=kbuf(2), op=Alu.mult),
            r=bk(2) + kk(2), w=out_keys, tag=f"{tag}out")

    def amax16(self, src_ap, src_keys, am_ap, am_keys, tag, ncols=1024):
        g3 = lambda ap_: ap_.rearrange("p (g s) -> p g s", s=16)
        self.P.op("vector", lambda e: e.tensor_reduce(
            out=am_ap.unsqueeze(2), in_=g3(src_ap), axis=AX.X, op=Alu.max,
            apply_absolute_value=True),
            r=src_keys, w=am_keys, tag=f"{tag}am")

    def scales_from_amax(self, am_ap, am_keys, gs_off, rdq2_col, dqh_col, col_keys,
                         scr, scr_off, tag, nb=64):
        P = self.P
        CM = float(3 * 2 ** 22)
        sc = scr.ap(F32, nb, byteoff=scr_off)
        sck = scr.K(scr_off, nb * 4)
        Sb = scr.ap(F32, nb, byteoff=scr_off + 256)
        Sbk = scr.K(scr_off + 256, nb * 4)
        mkb = scr.ap(U8, nb, byteoff=scr_off + 512)
        mkk = scr.K(scr_off + 512, nb)
        hb = scr.ap(F32, nb, byteoff=scr_off + 576)
        hbk = scr.K(scr_off + 576, nb * 4)
        f8 = scr.ap(FP8, nb, byteoff=scr_off + 832)
        f8k = scr.K(scr_off + 832, nb)
        s8 = scr.ap(F32, nb, byteoff=scr_off + 896)
        s8k = scr.K(scr_off + 896, nb * 4)
        gcol = lambda i: self.smv("ecol", F32, 1, byteoff=gs_off + i * 4)
        P.op("vector", lambda e: e.tensor_scalar(
            out=sc, in0=am_ap, scalar1=gcol(0), scalar2=448.0, op0=Alu.mult, op1=Alu.min),
            r=am_keys + ("ecol",), w=sck, tag=f"{tag}sc")
        P.op("vector", lambda e: e.tensor_single_scalar(
            out=mkb, in_=sc, scalar=0.03125, op=Alu.is_ge),
            r=sck, w=mkk, tag=f"{tag}mk")
        P.op("vector", lambda e: e.tensor_scalar(
            out=Sb, in0=sc, scalar1=512.0, scalar2=CM, op0=Alu.mult, op1=Alu.add),
            r=sck, w=Sbk, tag=f"{tag}S1")
        P.op("vector", lambda e: e.tensor_scalar(
            out=Sb, in0=Sb, scalar1=CM, scalar2=1.0 / 512.0, op0=Alu.subtract, op1=Alu.mult),
            r=Sbk, w=Sbk, tag=f"{tag}S2")
        P.op("vector", lambda e: e.tensor_scalar(
            out=hb, in0=sc, scalar1=0.5, scalar2=None, op0=Alu.mult),
            r=sck, w=hbk, tag=f"{tag}h")
        P.op("vector", lambda e: e.tensor_copy(out=f8, in_=hb), r=hbk, w=f8k, tag=f"{tag}f8")
        P.op("vector", lambda e: e.tensor_copy(out=s8, in_=f8), r=f8k, w=s8k, tag=f"{tag}f8b")
        P.op("vector", lambda e: e.tensor_scalar(
            out=s8, in0=s8, scalar1=2.0, scalar2=None, op0=Alu.mult),
            r=s8k, w=s8k, tag=f"{tag}N")
        P.op("vector", lambda e: e.select(out=hb, mask=mkb, on_true=s8, on_false=Sb, add_drain=True),
             r=s8k + mkk + Sbk, w=hbk, tag=f"{tag}sel")
        P.op("vector", lambda e: e.tensor_scalar(
            out=sc, in0=hb, scalar1=1e-30, scalar2=None, op0=Alu.max),
            r=hbk, w=sck, tag=f"{tag}mx")
        P.op("vector", lambda e: e.reciprocal(out=sc, in_=sc), r=sck, w=sck, tag=f"{tag}rc")
        P.op("vector", lambda e, rdq2_col=rdq2_col: e.tensor_scalar(
            out=rdq2_col, in0=sc, scalar1=gcol(1), scalar2=None, op0=Alu.mult),
            r=sck + ("ecol",), w=col_keys, tag=f"{tag}rdq")
        P.op("vector", lambda e, dqh_col=dqh_col: e.tensor_scalar(
            out=dqh_col, in0=hb, scalar1=gcol(2), scalar2=None, op0=Alu.mult),
            r=hbk + ("ecol",), w=col_keys, tag=f"{tag}dqh")

    def make_gs_cols(self, gs_src_ap, gs_src_keys, gs_off, tag):
        P = self.P
        t = lambda i: self.smv("srows", F32, 1, part=1, byteoff=32 + i * 4)
        P.op("vector", lambda e: e.tensor_scalar(
            out=t(0), in0=gs_src_ap, scalar1=1.0 / 6.0, scalar2=None, op0=Alu.mult),
            r=gs_src_keys, w=("srows",), tag=f"{tag}a")
        P.op("vector", lambda e: e.tensor_scalar(
            out=t(1), in0=gs_src_ap, scalar1=2.0, scalar2=None, op0=Alu.mult),
            r=gs_src_keys, w=("srows",), tag=f"{tag}b")
        P.op("vector", lambda e: e.reciprocal(out=t(2), in_=t(1)),
             r=("srows",), w=("srows",), tag=f"{tag}c")
        P.op("tensor", lambda e: e.matmul(
            self.ps_x[:, 0:3], self.smv("ones_1x128f", F32, 128, part=1),
            self.smv("srows", F32, 3, part=1, byteoff=32), start=True, stop=True),
            r=("srows", "ones_1x128f"), w=("ps_x",), tag=f"{tag}d")
        P.op("vector", lambda e: e.tensor_copy(
            out=self.smv("ecol", F32, 3, byteoff=gs_off), in_=self.ps_x[:, 0:3]),
            r=("ps_x",), w=("ecol",), tag=f"{tag}e")

    def stage_l1_ffn(self, upto='all'):
        P = self.P
        for nm, off in (("f1_sin", 64), ("f1_sind", 76)):
            P.op("sync", lambda e, nm=nm: e.dma_start(
                out=self.smv("srows", F32, 1, part=1, byteoff=28), in_=self.inp[nm].ap()),
                w=("srows",), dma=True, tag=f"ld{nm}")
            self.make_gs_cols(self.smv("srows", F32, 1, part=1, byteoff=28),
                              ("srows",), off, f"gs{nm}")

        def quant_weight(hbm, n_o, n_i, outT_h, wtag):
            nsl = n_o // 128
            nsub = n_i // 1024
            gmax = self.smv("ecol", F32, 1, byteoff=108)
            smax = self.smv("ecol", F32, 1, byteoff=104)
            for sl in range(nsl):
                wf = self.WP.ap(F32, n_i, byteoff=(sl % 2) * 4096) if n_i <= 1024 else None
                if n_i <= 1024:
                    kf = self.WP.K((sl % 2) * 4096, n_i * 4)
                    P.op("sync", lambda e, sl=sl, wf=wf, hbm=hbm: e.dma_start(
                        out=wf, in_=hbm.ap()[sl * 128:(sl + 1) * 128, :]),
                        w=kf, dma=True, tag=f"{wtag}Adma{sl}")
                    P.op("vector", lambda e, wf=wf: e.tensor_reduce(
                        out=smax, in_=wf, axis=AX.X, op=Alu.max,
                        apply_absolute_value=True),
                        r=kf, w=("ecol",), tag=f"{wtag}Ared{sl}")
                    ops = [0]
                else:
                    for sub in range(nsub):
                        wfs = self.WP.ap(F32, 1024, byteoff=(sub % 2) * 4096)
                        kfs = self.WP.K((sub % 2) * 4096, 4096)
                        P.op("sync", lambda e, sl=sl, sub=sub, wfs=wfs, hbm=hbm: e.dma_start(
                            out=wfs, in_=hbm.ap()[sl * 128:(sl + 1) * 128,
                                                  sub * 1024:(sub + 1) * 1024]),
                            w=kfs, dma=True, tag=f"{wtag}Adma{sl}_{sub}")
                        if sub == 0:
                            P.op("vector", lambda e, wfs=wfs: e.tensor_reduce(
                                out=smax, in_=wfs, axis=AX.X, op=Alu.max,
                                apply_absolute_value=True),
                                r=kfs, w=("ecol",), tag=f"{wtag}Ared{sl}_{sub}")
                        else:
                            P.op("vector", lambda e, wfs=wfs: e.tensor_reduce(
                                out=self.smv("ecol", F32, 1, byteoff=100), in_=wfs,
                                axis=AX.X, op=Alu.max, apply_absolute_value=True),
                                r=kfs, w=("ecol",), tag=f"{wtag}Ared{sl}_{sub}")
                            P.op("vector", lambda e: e.tensor_tensor(
                                out=smax, in0=smax,
                                in1=self.smv("ecol", F32, 1, byteoff=100), op=Alu.max),
                                r=("ecol",), w=("ecol",), tag=f"{wtag}Am{sl}_{sub}")
                if sl == 0:
                    P.op("vector", lambda e: e.tensor_copy(out=gmax, in_=smax),
                         r=("ecol",), w=("ecol",), tag=f"{wtag}Ainit")
                else:
                    P.op("vector", lambda e: e.tensor_tensor(
                        out=gmax, in0=gmax, in1=smax, op=Alu.max),
                        r=("ecol",), w=("ecol",), tag=f"{wtag}Aacc{sl}")
            P.op("tensor", lambda e: e.transpose(
                self.ps_x[0:1, 0:128], gmax, self.smv("id128f", F32, 128)),
                r=("ecol", "id128f"), w=("ps_x",), tag=f"{wtag}Atr")
            gsr = self.smv("srows", F32, 1, part=1, byteoff=28)
            P.op("vector", lambda e, gsr=gsr: e.tensor_reduce(
                out=gsr, in_=self.ps_x[0:1, 0:128], axis=AX.X, op=Alu.max),
                r=("ps_x",), w=("srows",), tag=f"{wtag}Ag")
            P.op("vector", lambda e, gsr=gsr: e.reciprocal(out=gsr, in_=gsr),
                 r=("srows",), w=("srows",), tag=f"{wtag}Arc")
            P.op("vector", lambda e, gsr=gsr: e.tensor_scalar(
                out=gsr, in0=gsr, scalar1=2688.0, scalar2=None, op0=Alu.mult),
                r=("srows",), w=("srows",), tag=f"{wtag}Ascale")
            self.make_gs_cols(gsr, ("srows",), 88, f"{wtag}gsc")
            for sl in range(nsl):
                for sub in range(nsub):
                    wfs = self.WP.ap(F32, 1024, byteoff=(sub % 2) * 4096)
                    kfs = self.WP.K((sub % 2) * 4096, 4096)
                    P.op("sync", lambda e, sl=sl, sub=sub, wfs=wfs, hbm=hbm: e.dma_start(
                        out=wfs, in_=hbm.ap()[sl * 128:(sl + 1) * 128,
                                              sub * 1024:(sub + 1) * 1024]),
                        w=kfs, dma=True, tag=f"{wtag}Bdma{sl}_{sub}")
                    am = self.S1.ap(F32, 64, byteoff=28672)
                    amk = self.S1.K(28672, 256)
                    self.amax16(wfs, kfs, am, amk, f"{wtag}B{sl}_{sub}")
                    rq = self.S1.ap(F32, 64, byteoff=29696)
                    dq = self.S1.ap(F32, 64, byteoff=29952)
                    ck = self.S1.K(29696, 512)
                    self.scales_from_amax(am, amk, 88, rq, dq, ck, self.S1, 24576,
                                          f"{wtag}S{sl}_{sub}")
                    qout = self.S1.ap(BF16, 1024, byteoff=45056 + sub * 2048)
                    qok = self.S1.K(45056 + sub * 2048, 2048)
                    self.quant_chain(wfs, kfs, rq, dq, ck, qout, qok,
                                     self.S1, 0, f"{wtag}Q{sl}_{sub}")
                for ic in range(n_i // 128):
                    trb, trk = self.trbanks[ic % 4]
                    P.op("tensor", lambda e, ic=ic, trb=trb: e.transpose(
                        trb[:, 0:512].bitcast(BF16)[:, 0:128],
                        self.S1.ap(BF16, 128, byteoff=45056 + ic * 256),
                        self.smv("id128b", BF16, 128)),
                        r=self.S1.K(45056 + ic * 256, 256) + ("id128b",),
                        w=(trk,), tag=f"{wtag}tr{sl}_{ic}")
                    P.op("vector", lambda e, ic=ic, trb=trb: e.tensor_copy(
                        out=self.PA.ap(BF16, 128, byteoff=ic * 256),
                        in_=trb[:, 0:512].bitcast(BF16)[:, 0:128]),
                        r=(trk,), w=self.PA.K(ic * 256, 256), tag=f"{wtag}te{sl}_{ic}")
                P.op("sync", lambda e, sl=sl, outT_h=outT_h, n_i=n_i: e.dma_start(
                    out=outT_h.ap()[:, sl * 128:(sl + 1) * 128].rearrange(
                        "(ic p) m -> p ic m", p=128),
                    in_=self.PA.ap(BF16, n_i, byteoff=0).rearrange(
                        "p (ic m) -> p ic m", m=128)),
                    r=self.PA.K(0, n_i * 2), dma=True, tag=f"{wtag}st{sl}")

        if upto != "nowq":
            quant_weight(self.inp["f1_gw"], 4096, D, self.gwT_h, "qg")
            quant_weight(self.inp["f1_uw"], 4096, D, self.uwT_h, "qu")
            quant_weight(self.inp["f1_dw"], D, 4096, self.dwT_h, "qd")
        if upto == "wq":
            return

        self.ln_fm("ln3", self.S0, lambda c: c * 8192, self.S1, lambda c: c * 8192,
                   F32, scratch=(self.PB, 0))
        l3t = lambda tt, c0, cols: (self.PA if tt < 8 else self.PB).ap(
            F32, cols, byteoff=(tt % 8) * 4096 + c0 * 4)
        l3t_k = lambda tt, c0, cols: (self.PA if tt < 8 else self.PB).K(
            (tt % 8) * 4096 + c0 * 4, cols * 4)
        for tt in range(NTT):
            for c in range(NCH):
                trb, trk = self.trbanks[(tt * NCH + c) % 4]
                P.op("tensor", lambda e, tt=tt, c=c, trb=trb: e.transpose(
                    trb[:, 0:128], self.S1.ap(F32, 128, byteoff=c * 8192 + tt * 512),
                    self.smv("id128f", F32, 128)),
                    r=self.S1.K(c * 8192 + tt * 512, 512) + ("id128f",),
                    w=(trk,), tag=f"l3t{tt}_{c}")
                P.op("vector", lambda e, tt=tt, c=c, trb=trb: e.tensor_copy(
                    out=l3t(tt, c * 128, 128), in_=trb[:, 0:128]),
                    r=(trk,), w=l3t_k(tt, c * 128, 128), tag=f"l3e{tt}_{c}")
        xqfm = lambda kc, c0, cols: self.S1.ap(BF16, cols, byteoff=kc * 4096 + c0 * 2)
        xqfm_k = lambda kc, c0, cols: self.S1.K(kc * 4096 + c0 * 2, cols * 2)
        for tt in range(NTT):
            am = self.S1.ap(F32, 64, byteoff=55296)
            amk = self.S1.K(55296, 256)
            self.amax16(l3t(tt, 0, 1024), l3t_k(tt, 0, 1024), am, amk, f"xq{tt}")
            rq = self.S1.ap(F32, 64, byteoff=63488)
            dq = self.S1.ap(F32, 64, byteoff=63744)
            ck = self.S1.K(63488, 512)
            self.scales_from_amax(am, amk, 64, rq, dq, ck, self.S1, 56320, f"xqs{tt}")
            xqt = self.S1.ap(BF16, 1024, byteoff=59392 + (tt % 2) * 2048)
            xqt_k = self.S1.K(59392 + (tt % 2) * 2048, 2048)
            self.quant_chain(l3t(tt, 0, 1024), l3t_k(tt, 0, 1024), rq, dq, ck,
                             xqt, xqt_k, self.S1, 32768, f"xqc{tt}")
            for c in range(NCH):
                trb, trk = self.trbanks[c % 4]
                P.op("tensor", lambda e, tt=tt, c=c, xqt=xqt, trb=trb: e.transpose(
                    trb[:, 0:512].bitcast(BF16)[:, 0:128],
                    xqt[:, c * 128:(c + 1) * 128], self.smv("id128b", BF16, 128)),
                    r=xqt_k + ("id128b",), w=(trk,), tag=f"xqt{tt}_{c}")
                P.op("vector", lambda e, tt=tt, c=c, trb=trb: e.tensor_copy(
                    out=xqfm(c, tt * 128, 128), in_=trb[:, 0:512].bitcast(BF16)[:, 0:128]),
                    r=(trk,), w=xqfm_k(c, tt * 128, 128), tag=f"xqe{tt}_{c}")

        if upto in ("xq", "nowq_xq"):
            return
        hid_bf = lambda tl, c0, cols: self.PB.ap(BF16, cols, byteoff=tl * 8192 + c0 * 2)
        hid_bf_k = lambda tl, c0, cols: self.PB.K(tl * 8192 + c0 * 2, cols * 2)
        hqfm = lambda hc, tl, c0, cols: self.S1.ap(BF16, cols, byteoff=32768 + hc * 1024 + tl * 256 + c0 * 2)
        hqfm_k = lambda hc, tl, c0, cols: self.S1.K(32768 + hc * 1024 + tl * 256 + c0 * 2, cols * 2)
        for st in range(1 if upto == "strip0" else 4):
            for hs in range(8):
                for w_i, (wh, wnm) in enumerate(((self.gwT_h, "g"), (self.uwT_h, "u"))):
                    wbuf = self.WP if (hs * 2 + w_i) % 2 == 0 else self.SC
                    for hf in range(2):
                        P.op("sync", lambda e, hs=hs, hf=hf, wh=wh, wbuf=wbuf: e.dma_start(
                            out=wbuf.ap(BF16, 2048, byteoff=hf * 4096).rearrange(
                                "p (ic m) -> p ic m", ic=4),
                            in_=wh.ap()[hf * 512:(hf + 1) * 512,
                                        hs * 512:(hs + 1) * 512].rearrange(
                                "(ic p) m -> p ic m", p=128)),
                            w=wbuf.K(hf * 4096, 4096), dma=True,
                            tag=f"w{wnm}{st}_{hs}_{hf}")
                    for tl in range(4):
                        tt = st * 4 + tl
                        bank = self.psb[tl]
                        bk = f"psb/{tl}"
                        for ic in range(NCH):
                            hf = ic // 4
                            icl = ic % 4
                            P.op("tensor", lambda e, tt=tt, hs=hs, ic=ic, hf=hf, icl=icl,
                                 bank=bank, wbuf=wbuf: e.matmul(
                                 bank[:, :], xqfm(ic, tt * 128, 128),
                                 wbuf.ap(BF16, 512, byteoff=hf * 4096 + icl * 1024),
                                 start=(ic == 0), stop=(ic == NCH - 1)),
                                 r=xqfm_k(ic, tt * 128, 128) + wbuf.K(hf * 4096 + icl * 1024, 1024),
                                 w=(bk,), tag=f"{wnm}mm{st}_{hs}_{tl}")
                        if w_i == 0:
                            P.op("scalar", lambda e, tl=tl, bank=bank: e.activation(
                                out=self.PA.ap(F32, 512, byteoff=tl * 2048),
                                in_=bank[:, :], func=Act.Silu),
                                r=(bk,), w=self.PA.K(tl * 2048, 2048),
                                tag=f"sil{st}_{hs}_{tl}")
                        else:
                            P.op("vector", lambda e, tl=tl, bank=bank: e.tensor_tensor(
                                out=self.PA.ap(F32, 512, byteoff=8192),
                                in0=bank[:, :], in1=self.PA.ap(F32, 512, byteoff=tl * 2048),
                                op=Alu.mult),
                                r=(bk,) + self.PA.K(tl * 2048, 2048),
                                w=self.PA.K(8192, 2048), tag=f"hid{st}_{hs}_{tl}")
                            P.op("vector", lambda e, tl=tl, hs=hs: e.tensor_reduce(
                                out=self.PA.ap(F32, 32, byteoff=24576 + tl * 1024 + hs * 128).unsqueeze(2),
                                in_=self.PA.ap(F32, 512, byteoff=8192).rearrange(
                                    "p (g s) -> p g s", s=16),
                                axis=AX.X, op=Alu.max, apply_absolute_value=True),
                                r=self.PA.K(8192, 2048),
                                w=self.PA.K(24576 + tl * 1024 + hs * 128, 128),
                                tag=f"ham{st}_{hs}_{tl}")
                            P.op("vector", lambda e, tl=tl, hs=hs: e.tensor_copy(
                                out=hid_bf(tl, hs * 512, 512),
                                in_=self.PA.ap(F32, 512, byteoff=8192)),
                                r=self.PA.K(8192, 2048), w=hid_bf_k(tl, hs * 512, 512),
                                tag=f"hbf{st}_{hs}_{tl}")
            for tl in range(4):
                for sub in range(4):
                    am = self.PA.ap(F32, 64, byteoff=24576 + tl * 1024 + sub * 256)
                    amk = self.PA.K(24576 + tl * 1024 + sub * 256, 256)
                    rq = self.PA.ap(F32, 64, byteoff=28928)
                    dq = self.PA.ap(F32, 64, byteoff=29184)
                    ck = self.PA.K(28928, 512)
                    self.scales_from_amax(am, amk, 76, rq, dq, ck, self.PA, 22528,
                                          f"hs{st}_{tl}_{sub}")
                    self.quant_chain(hid_bf(tl, sub * 1024, 1024),
                                     hid_bf_k(tl, sub * 1024, 1024), rq, dq, ck,
                                     hid_bf(tl, sub * 1024, 1024),
                                     hid_bf_k(tl, sub * 1024, 1024),
                                     self.PA, 0, f"hq{st}_{tl}_{sub}")
                for hc in range(32):
                    trb, trk = self.trbanks[hc % 4]
                    P.op("tensor", lambda e, tl=tl, hc=hc, trb=trb: e.transpose(
                        trb[:, 0:512].bitcast(BF16)[:, 0:128],
                        hid_bf(tl, hc * 128, 128), self.smv("id128b", BF16, 128)),
                        r=hid_bf_k(tl, hc * 128, 128) + ("id128b",), w=(trk,),
                        tag=f"hqt{st}_{tl}_{hc}")
                    P.op("vector", lambda e, tl=tl, hc=hc, trb=trb: e.tensor_copy(
                        out=hqfm(hc, tl, 0, 128),
                        in_=trb[:, 0:512].bitcast(BF16)[:, 0:128]),
                        r=(trk,), w=hqfm_k(hc, tl, 0, 128), tag=f"hqe{st}_{tl}_{hc}")
            if upto == "strip0":
                return
            for dc in range(NCH):
                wbuf = self.SC if dc % 2 == 0 else self.WP
                for hf in range(2):
                    P.op("sync", lambda e, dc=dc, hf=hf, wbuf=wbuf: e.dma_start(
                        out=wbuf.ap(BF16, 2048, byteoff=hf * 4096).rearrange(
                            "p (ic m) -> p ic m", ic=16),
                        in_=self.dwT_h.ap()[hf * 2048:(hf + 1) * 2048,
                                            dc * 128:(dc + 1) * 128].rearrange(
                            "(ic p) m -> p ic m", p=128)),
                        w=wbuf.K(hf * 4096, 4096), dma=True, tag=f"dw{st}_{dc}_{hf}")
                bank = self.ps_ab[dc % 2]
                bk = f"ps_ab{dc % 2}"
                for hc in range(32):
                    hf = hc // 16
                    hcl = hc % 16
                    P.op("tensor", lambda e, st=st, dc=dc, hc=hc, hf=hf, hcl=hcl,
                         bank=bank, wbuf=wbuf: e.matmul(
                         bank[:, :], wbuf.ap(BF16, 128, byteoff=hf * 4096 + hcl * 256),
                         hqfm(hc, 0, 0, 512), start=(hc == 0), stop=(hc == 31)),
                         r=wbuf.K(hf * 4096 + hcl * 256, 256) + self.S1.K(32768 + hc * 1024, 1024),
                         w=(bk,), tag=f"dmm{st}_{dc}_{hc}")
                P.op("vector", lambda e, st=st, dc=dc, bank=bank: e.tensor_tensor(
                    out=self.S0.ap(F32, 512, byteoff=dc * 8192 + st * 2048),
                    in0=bank[:, :], in1=self.S0.ap(F32, 512, byteoff=dc * 8192 + st * 2048),
                    op=Alu.add),
                    r=(bk,) + self.S0.K(dc * 8192 + st * 2048, 2048),
                    w=self.S0.K(dc * 8192 + st * 2048, 2048), tag=f"dev{st}_{dc}")

    def debug_out_fm(self, region, chunk_off, dtype=F32, scale=1.0):
        P = self.P
        dsz = {F32: 4, BF16: 2}[dtype]
        for tt in range(NTT):
            for c in range(NCH):
                rk = region.K(chunk_off(c) + tt * 128 * dsz, 128 * dsz)
                if dtype == F32:
                    P.op("tensor", lambda e, tt=tt, c=c: e.transpose(
                        self.ps_x[:, 0:128],
                        region.ap(F32, 128, byteoff=chunk_off(c) + tt * 512),
                        self.smv("id128f", F32, 128)),
                        r=rk + ("id128f",), w=("ps_x",), tag=f"dbgt{tt}_{c}")
                    src2 = self.ps_x[:, 0:128]
                else:
                    P.op("tensor", lambda e, tt=tt, c=c: e.transpose(
                        self.ps_x[:, 0:512].bitcast(BF16)[:, 0:128],
                        region.ap(BF16, 128, byteoff=chunk_off(c) + tt * 256),
                        self.smv("id128b", BF16, 128)),
                        r=rk + ("id128b",), w=("ps_x",), tag=f"dbgt{tt}_{c}")
                    src2 = self.ps_x[:, 0:512].bitcast(BF16)[:, 0:128]
                P.op("vector", lambda e, tt=tt, c=c, src2=src2: e.tensor_scalar(
                    out=self.WP.ap(F32, 128, byteoff=c * 512), in0=src2,
                    scalar1=float(scale), scalar2=None, op0=Alu.mult),
                    r=("ps_x",), w=self.WP.K(c * 512, 512), tag=f"dbge{tt}_{c}")
            tok = P.op("sync", lambda e, tt=tt: e.dma_start(
                out=self.y_h.ap()[tt * 128:(tt + 1) * 128, :],
                in_=self.WP.ap(F32, D, byteoff=0)),
                r=self.WP.K(0, 4096), dma=True, tag=f"dbgs{tt}")
            P.final_waits.add(tok)

    def debug_out_tok(self, region, tile_off, dtype=BF16):
        P = self.P
        dsz = {F32: 4, BF16: 2}[dtype]
        for tt in range(NTT):
            P.op("vector", lambda e, tt=tt: e.tensor_copy(
                out=self.WP.ap(F32, D, byteoff=0),
                in_=region.ap(dtype, D, byteoff=tile_off(tt))),
                r=region.K(tile_off(tt), D * dsz), w=self.WP.K(0, 4096), tag=f"dtk{tt}")
            tok = P.op("sync", lambda e, tt=tt: e.dma_start(
                out=self.y_h.ap()[tt * 128:(tt + 1) * 128, :],
                in_=self.WP.ap(F32, D, byteoff=0)),
                r=self.WP.K(0, 4096), dma=True, tag=f"dtks{tt}")
            P.final_waits.add(tok)

    def finalize(self, listing=None):
        return self.P.finalize(listing=listing)


def in_map_for_core(ins, core):
    return {
        "x": np.ascontiguousarray(ins["L0_attn_x"][core]),
        "l0_wq": ins["L0_attn_Wq"], "l0_wk": ins["L0_attn_Wk"],
        "l0_wv": ins["L0_attn_Wv"], "l0_wo": ins["L0_attn_Wo"],
        "l0_wa": ins["L0_attn_Wa"], "l0_ba": np.ascontiguousarray(ins["L0_attn_ba"].reshape(1, 16)),
        "l0f_wg": ins["L0_ffn_w_gate"], "l0f_wu": ins["L0_ffn_w_up"],
        "l0f_wd": ins["L0_ffn_w_down"],
        "l1_wkqv": ins["L1_attn_Wkqv"], "l1_wm1": ins["L1_attn_Wm1"],
        "l1_bm1": ins["L1_attn_bm1"], "l1_wm2": ins["L1_attn_Wm2"],
        "l1_bm2": ins["L1_attn_bm2"], "l1_gng": ins["L1_attn_gn_g"],
        "l1_gnb": ins["L1_attn_gn_b"], "l1_wf1": ins["L1_attn_Wf1"],
        "l1_bf1": ins["L1_attn_bf1"], "l1_wf2": ins["L1_attn_Wf2"],
        "l1_bf2": ins["L1_attn_bf2"], "l1_lng": ins["L1_attn_ln_g"],
        "l1_lnb": ins["L1_attn_ln_b"],
        "f1_gw": ins["L1_ffn_gate_w"], "f1_uw": ins["L1_ffn_up_w"],
        "f1_dw": ins["L1_ffn_down_w"],
        "f1_sin": np.ascontiguousarray(ins["L1_ffn_s_in"].reshape(1, 1)),
        "f1_sind": np.ascontiguousarray(ins["L1_ffn_s_in_down"].reshape(1, 1)),
    }


def _build():
    m = MK()
    m.stage_load_x()
    m.ln_fm("ln0", m.S0, lambda c: c * 8192, m.S1, lambda c: c * 4096, BF16,
            scratch=(m.S1, 32768 + 8192))
    m.stage_l0_attn()
    m.ln_fm("ln1", m.S0, lambda c: c * 8192, m.S1, lambda c: c * 4096, BF16,
            scratch=(m.S1, 32768 + 8192))
    m.stage_l0_ffn()
    m.ln_fm("ln2", m.S0, lambda c: c * 8192, m.S1, lambda c: c * 8192, F32,
            scratch=(m.PB, 0))
    m.stage_l1_attn()
    m.stage_l1_ffn()
    m.debug_out_fm(m.S0, lambda c: c * 8192, dtype=F32)
    m.finalize()
    return m


def kernel(**inputs):
    from concourse.bass_utils import run_bass_kernel_spmd
    m = _build()
    in_maps = [in_map_for_core(inputs, core) for core in range(8)]
    res = run_bass_kernel_spmd(m.nc, in_maps, core_ids=list(range(8)))
    out = np.stack([np.asarray(res.results[c]["y"]) for c in range(8)], axis=0)
    return out.astype(np.float32)


# version 08
# speedup vs baseline: 1.1205x; 1.0049x over previous
import numpy as np
import ml_dtypes
from contextlib import ExitStack
import concourse.bass as bass
import concourse.mybir as mybir

F32 = mybir.dt.float32
BF16 = mybir.dt.bfloat16
I32 = mybir.dt.int32
U8 = mybir.dt.uint8
FP8 = mybir.dt.float8e4
Alu = mybir.AluOpType
Act = mybir.ActivationFunctionType
AX = mybir.AxisListType
BF16NP = ml_dtypes.bfloat16

T = 2048
D = 1024
NCH = 8
NTT = 16

COMPUTE = ("tensor", "vector", "scalar", "gpsimd")
DMA_Q = {"sync": "q_sync", "gpsimd": "q_gps", "scalar": "q_act", "vector": "q_dve", "tensor": "q_pe"}


class Prog:
    def __init__(self, nc):
        self.nc = nc
        self.seq = {e: [] for e in ("tensor", "vector", "scalar", "gpsimd", "sync")}
        self.cnt = {e: 0 for e in COMPUTE}
        for q in DMA_Q.values():
            self.cnt[q] = 0
        self.state = {}
        self.dep_targets = set()
        self.final_waits = set()
        self.tok_tag = {}

    def _read_deps(self, keys):
        d = set()
        for k in keys:
            st = self.state.get(k)
            if st and st[0] is not None:
                d.add(st[0])
        return d

    def _write_deps(self, keys):
        d = set()
        for k in keys:
            st = self.state.get(k)
            if st:
                if st[0] is not None:
                    d.add(st[0])
                d.update(st[1])
        return d

    def op(self, eng, fn, r=(), w=(), dma=False, tag=""):
        deps = self._read_deps(r) | self._write_deps(w)
        if dma:
            q = DMA_Q[eng]
            token = (q, self.cnt[q])
            self.cnt[q] += 1
        else:
            token = (eng, self.cnt[eng])
            self.cnt[eng] += 1
        eff = set(deps)
        for t in eff:
            if t[0] in COMPUTE:
                self.dep_targets.add(t)
        for k in r:
            self.state.setdefault(k, [None, set()])[1].add(token)
        for k in w:
            self.state[k] = [token, set()]
        self.seq[eng].append((fn, eff, token, dma, tag))
        self.tok_tag[token] = f"{token[0]}:{token[1]}:{tag}"
        return token

    def finalize(self, last_engine="sync", listing=None):
        nc = self.nc
        lst = open(listing, "w") if listing else None
        for t in self.final_waits:
            if t[0] in COMPUTE:
                self.dep_targets.add(t)
        marked = {e: sorted(i for (ee, i) in self.dep_targets if ee == e) for e in COMPUTE}
        rank = {}
        for e in COMPUTE:
            for n, i in enumerate(marked[e]):
                rank[(e, i)] = n + 1
        es = ExitStack()
        sems = {}
        NQS = 16
        for e in COMPUTE:
            sems[e] = es.enter_context(nc.semaphore(f"s_{e}"))
        qsems = {}
        for q in sorted(set(DMA_Q.values())):
            qsems[q] = [es.enter_context(nc.semaphore(f"s_{q}_{i}")) for i in range(NQS)]

        def wait_val(tok):
            de, di = tok
            if de in COMPUTE:
                return sems[de], rank[tok]
            return qsems[de][di % NQS], 16 * (di // NQS + 1)

        def emit_stream(eng_obj, eng_name):
            waited = {}

            def wait(tok):
                sem, val = wait_val(tok)
                if waited.get(sem.name, 0) < val:
                    eng_obj.wait_ge(sem, val)
                    waited[sem.name] = val

            for (fn, deps, token, dma, tag) in self.seq[eng_name]:
                wl = []
                for tok in sorted(deps):
                    sem, val = wait_val(tok)
                    if waited.get(sem.name, 0) < val:
                        wl.append((sem.name, val))
                for tok in sorted(deps):
                    wait(tok)
                if dma and token[1] >= NQS:
                    wait((token[0], token[1] - NQS))
                inst = fn(eng_obj)
                if dma:
                    inst.then_inc(qsems[token[0]][token[1] % NQS], 16)
                elif token in rank:
                    inst.then_inc(sems[token[0]], 1)
                if lst:
                    inc = "DMA+16" if dma else ("+1" if token in rank else "")
                    dt_ = [self.tok_tag.get(t, t) for t in sorted(deps)]
                    lst.write(f"{eng_name} {token[1]} {tag} waits={wl} deps={dt_} {inc}\n")
            if eng_name == last_engine:
                for tok in sorted(self.final_waits):
                    wait(tok)

        with es, nc.Block() as block:
            block.sync(lambda e: emit_stream(e, "sync"))
            block.vector(lambda e: emit_stream(e, "vector"))
            block.scalar(lambda e: emit_stream(e, "scalar"))
            block.tensor(lambda e: emit_stream(e, "tensor"))
            block.gpsimd(lambda e: emit_stream(e, "gpsimd"))
        if lst:
            lst.close()
        return sum(len(s) for s in self.seq.values())


class Region:
    GR = 2048

    def __init__(self, arena, name, off, nbytes):
        self.arena = arena
        self.name = name
        self.off = off
        self.nbytes = nbytes

    def K(self, byteoff, nbytes):
        g0 = byteoff // self.GR
        g1 = (byteoff + nbytes - 1) // self.GR
        return tuple(f"{self.name}g/{g}" for g in range(g0, g1 + 1))

    def ap(self, dtype, cols, part=128, byteoff=0, p0=0):
        sz = {F32: 4, BF16: 2, I32: 4, U8: 1, FP8: 1}[dtype]
        assert byteoff >= 0 and byteoff + cols * sz <= self.nbytes, \
            (self.name, byteoff, cols, sz, self.nbytes)
        v = self.arena[p0:p0 + part, self.off + byteoff:self.off + byteoff + cols * sz]
        return v.bitcast(dtype)


class MK:
    def __init__(self):
        nc = bass.Bass("TRN2", target_bir_lowering=False, dynamic_dma_scratch_size=2048)
        self.nc = nc
        self.P = Prog(nc)
        self._declare_io()
        self._declare_consts()
        self._declare_arena()
        self._psum()
        self._load_consts()

    def _declare_io(self):
        nc = self.nc
        self.inp = {}

        def di(name, shape, dt=F32):
            self.inp[name] = nc.dram_tensor(name, shape, dt, kind="ExternalInput")

        di("x", [T, D])
        di("l0_wq", [D, D]); di("l0_wk", [D, D]); di("l0_wv", [D, D]); di("l0_wo", [D, D])
        di("l0_wa", [D, 16]); di("l0_ba", [1, 16])
        di("l0f_wg", [3072, D], I32); di("l0f_wu", [3072, D], I32); di("l0f_wd", [D, 3072], I32)
        di("l1_wkqv", [4, D, 3072]); di("l1_wm1", [4, D, 2048]); di("l1_bm1", [4, 2048])
        di("l1_wm2", [4, D, D]); di("l1_bm2", [4, D])
        di("l1_gng", [4, D]); di("l1_gnb", [4, D])
        di("l1_wf1", [4, D, 2048]); di("l1_bf1", [4, 2048])
        di("l1_wf2", [4, 2048, D]); di("l1_bf2", [4, D])
        di("l1_lng", [4, D]); di("l1_lnb", [4, D])
        di("f1_gw", [4096, D]); di("f1_uw", [4096, D]); di("f1_dw", [D, 4096])
        di("f1_sin", [1, 1]); di("f1_sind", [1, 1])
        self.y_h = nc.dram_tensor("y", [T, D], F32, kind="ExternalOutput")
        self.xspill_h = nc.dram_tensor("xspill", [NCH, 128, T], F32, kind="Internal")
        self.gwT_h = nc.dram_tensor("gwT", [D, 4096], BF16, kind="Internal")
        self.uwT_h = nc.dram_tensor("uwT", [D, 4096], BF16, kind="Internal")
        self.dwT_h = nc.dram_tensor("dwT", [4096, D], BF16, kind="Internal")

    def _declare_consts(self):
        nc = self.nc
        self.const_h = {}

        def ci(name, arr):
            self.const_h[name] = nc.inline_tensor(np.ascontiguousarray(arr), name)

        ci("id128f", np.eye(128, dtype=np.float32))
        ci("id128b", np.eye(128, dtype=np.float32).astype(BF16NP))
        ci("id16f", np.eye(16, dtype=np.float32))
        ci("id16b", np.eye(16, dtype=np.float32).astype(BF16NP))
        ci("ones_1x128f", np.ones((1, 128), np.float32))
        ci("onescol_b", np.ones((128, 1), np.float32).astype(BF16NP))
        ci("maskbig", np.where(np.arange(128)[:, None] <= np.arange(128)[None, :],
                               0.0, -1e30).astype(np.float32))
        ci("rn", (1.0 + np.sqrt(np.arange(T, dtype=np.float64))).astype(np.float32)[None, :])
        for dk in range(2):
            m = (128 * dk + np.arange(128)[:, None] <= np.arange(256)[None, :])
            ci(f"trimask{dk}", m.astype(np.float32).astype(BF16NP))

    def _declare_arena(self):
        nc = self.nc
        avail = (nc.sbuf_bytes_remaining - 512) // 64 * 64
        self.arena = nc.alloc_sbuf_tensor("arena", [128, avail], U8)
        off = [0]
        self.reg = {}

        def R(name, nbytes):
            r = Region(self.arena, name, off[0], nbytes)
            off[0] += nbytes
            assert off[0] <= avail, (name, off[0], avail)
            self.reg[name] = r
            return r

        self.S0 = R("S0", 65536)
        self.S1 = R("S1", 65536)
        self.PA = R("PA", 32768)
        self.PB = R("PB", 32768)
        self.WP = R("WP", 8192)
        self.SC = R("SC", 8192)
        self.SC.GR = 256
        self.SM = R("SM", avail - off[0])
        self._smoff = {}
        so = [0]

        def A(name, nbytes, align=4):
            o = (so[0] + align - 1) // align * align
            self._smoff[name] = o
            so[0] = o + nbytes

        A("id128f", 512); A("id128b", 256); A("id16f", 64); A("id16b", 32)
        A("ones_1x128f", 512); A("onescol_b", 2)
        A("maskbig", 512)
        A("trimask0", 512); A("trimask1", 512)
        A("rnrow", 1024)
        A("biasT", 512)
        A("bias_nat", 512)
        A("sz", 2080)
        A("szbf", 1040)
        A("szscr", 260)
        A("la_tok", 1024)
        A("wa", 512)
        A("wab", 256)
        A("ba", 64)
        A("ba2", 64)
        A("ones16", 512)
        A("ecol", 512)
        A("lrow2", 1024)
        A("srows", 256)
        A("ccols", 256)
        assert so[0] <= self.SM.nbytes, (so[0], self.SM.nbytes)
        self._sm_used = so[0]
        self._sm_alloc = A
        self._ccols = {}

    def cc(self, value):
        if value not in self._ccols:
            i = len(self._ccols)
            assert i < 64
            ap = self.smv("ccols", F32, 1, byteoff=4 * i)
            self.P.op("vector", lambda e, ap=ap, value=value: e.memset(ap, float(value)),
                      w=(f"cc/{value}",), tag=f"cc_{value}")
            self._ccols[value] = (ap, f"cc/{value}")
        return self._ccols[value]

    def smv(self, name, dtype, cols, part=128, p0=0, byteoff=0):
        return self.SM.ap(dtype, cols, part=part, p0=p0,
                          byteoff=self._smoff[name] + byteoff)

    def wp(self, dtype, cols, granule, part=128, p0=0):
        return self.WP.ap(dtype, cols, part=part, p0=p0, byteoff=granule * 1024)

    def _psum(self):
        nc = self.nc
        self.psb = [nc.alloc_psum_tensor(f"psb{i}", [128, 512], F32) for i in range(4)]
        self.ps_ab = [nc.alloc_psum_tensor(f"ps_ab{i}", [128, 512], F32) for i in range(2)]
        self.ps_st = nc.alloc_psum_tensor("ps_st", [128, 512], F32)
        self.ps_x = nc.alloc_psum_tensor("ps_x", [128, 512], F32)
        assert nc.psum_banks_remaining == 0, nc.psum_banks_remaining
        self.trbanks = [(self.ps_x, "ps_x"), (self.psb[0], "psb/0"),
                        (self.psb[1], "psb/1"), (self.psb[2], "psb/2")]

    def _load_consts(self):
        P = self.P
        for nm, cols, part, dt_ in [
            ("id128f", 128, 128, F32), ("id128b", 128, 128, BF16), ("id16f", 16, 16, F32),
            ("ones_1x128f", 128, 1, F32), ("onescol_b", 1, 128, BF16),
            ("maskbig", 128, 128, F32),
            ("trimask0", 256, 128, BF16), ("trimask1", 256, 128, BF16),
        ]:
            h = self.const_h[nm]
            P.op("sync", lambda e, nm=nm, cols=cols, part=part, dt_=dt_, h=h:
                 e.dma_start(out=self.smv(nm, dt_, cols, part=part), in_=h.ap()),
                 w=(nm,), dma=True, tag=f"ld_{nm}")

    def stage_load_x(self):
        P = self.P
        for tt in range(NTT):
            P.op("sync", lambda e, tt=tt: e.dma_start(
                out=self.S1.ap(F32, D, byteoff=tt * 4096),
                in_=self.inp["x"].ap()[tt * 128:(tt + 1) * 128, :]),
                w=self.S1.K(tt * 4096, 4096), dma=True, tag=f"ld_x{tt}")
        for tt in range(NTT):
            for c in range(NCH):
                i = tt * NCH + c
                pb, pk = self.psb[i % 2], f"psb/{i % 2}"
                P.op("tensor", lambda e, tt=tt, c=c, pb=pb: e.transpose(
                    pb[:, 0:128],
                    self.S1.ap(F32, 128, byteoff=tt * 4096 + c * 512),
                    self.smv("id128f", F32, 128)),
                    r=self.S1.K(tt * 4096 + c * 512, 512) + ("id128f",), w=(pk,), tag=f"trx{tt}_{c}")
                P.op("vector", lambda e, tt=tt, c=c, pb=pb: e.tensor_copy(
                    out=self.S0.ap(F32, 128, byteoff=c * 8192 + tt * 512), in_=pb[:, 0:128]),
                    r=(pk,), w=self.S0.K(c * 8192 + tt * 512, 512), tag=f"evx{tt}_{c}")

    def ln_fm(self, name, src_reg, src_off, dst_reg, dst_off, dst_dtype, scratch,
              gamma=None, beta=None, gb_keys=()):
        P = self.P
        sreg, boff = scratch
        xb = lambda b: sreg.ap(BF16, 512, byteoff=boff + 1024 * (b % 2))
        xs = lambda b: sreg.ap(BF16, 512, byteoff=boff + 2048 + 1024 * (b % 2))
        tmp = sreg.ap(F32, 512, byteoff=boff + 4096)
        kxb = lambda b: sreg.K(boff + 1024 * (b % 2), 1024)
        kxs = lambda b: sreg.K(boff + 2048 + 1024 * (b % 2), 1024)
        ktmp = sreg.K(boff + 4096, 2048)
        dsz = {F32: 4, BF16: 2}[dst_dtype]

        def srow(r):
            return self.SC.ap(F32, 512, part=1, p0=0, byteoff=2048 * r)

        def src_ap(c, s):
            return src_reg.ap(F32, 512, byteoff=src_off(c) + s * 2048)

        def src_k(c, s):
            return src_reg.K(src_off(c) + s * 2048, 2048)

        for s in range(4):
            for c in range(NCH):
                P.op("vector", lambda e, c=c, s=s: e.tensor_copy(
                    out=xb(c), in_=src_ap(c, s)),
                    r=src_k(c, s), w=kxb(c), tag=f"{name}_xb")
                P.op("scalar", lambda e, c=c, s=s: e.activation(
                    out=xs(c), in_=src_ap(c, s), func=Act.Square),
                    r=src_k(c, s), w=kxs(c), tag=f"{name}_xs")
                P.op("tensor", lambda e, c=c, s=s: e.matmul(
                    self.ps_st[0:1, :], self.smv("onescol_b", BF16, 1), xb(c),
                    start=(c == 0), stop=(c == NCH - 1)),
                    r=kxb(c) + ("onescol_b",), w=("ps_st",), tag=f"{name}_ms")
                P.op("tensor", lambda e, c=c, s=s: e.matmul(
                    self.ps_x[0:1, 0:512], self.smv("onescol_b", BF16, 1), xs(c),
                    start=(c == 0), stop=(c == NCH - 1)),
                    r=kxs(c) + ("onescol_b",), w=("ps_x",), tag=f"{name}_mq")
            P.op("vector", lambda e, s=s: e.tensor_scalar(
                out=srow(0), in0=self.ps_st[0:1, :], scalar1=1.0 / D, scalar2=None,
                op0=Alu.mult),
                r=("ps_st", "ps_x"), w=self.SC.K(0, 2048), tag=f"{name}_mu")
            P.op("vector", lambda e, s=s: e.tensor_tensor(
                out=srow(1), in0=srow(0), in1=srow(0), op=Alu.mult),
                r=self.SC.K(0, 2048), w=self.SC.K(2048, 2048), tag=f"{name}_musq")
            P.op("vector", lambda e, s=s: e.scalar_tensor_tensor(
                out=srow(1), in0=self.ps_x[0:1, 0:512], scalar=1.0 / D, in1=srow(1),
                op0=Alu.mult, op1=Alu.subtract),
                r=("ps_x",) + self.SC.K(2048, 2048), w=self.SC.K(2048, 2048), tag=f"{name}_var")
            epsap, epsk = self.cc(1e-5)
            P.op("scalar", lambda e, s=s, epsap=epsap: e.activation(
                out=srow(2), in_=srow(1), func=Act.Ln, bias=epsap[0:1, 0:1], scale=1.0),
                r=self.SC.K(2048, 2048) + (epsk,), w=self.SC.K(4096, 2048), tag=f"{name}_lnv")
            P.op("scalar", lambda e, s=s: e.activation(
                out=srow(2), in_=srow(2), func=Act.Exp, scale=-0.5),
                r=self.SC.K(4096, 2048), w=self.SC.K(4096, 2048), tag=f"{name}_rstd")
            P.op("vector", lambda e, s=s: e.scalar_tensor_tensor(
                out=srow(3), in0=srow(0), scalar=-1.0, in1=srow(2),
                op0=Alu.mult, op1=Alu.mult),
                r=self.SC.K(0, 2048) + self.SC.K(4096, 2048), w=self.SC.K(6144, 2048), tag=f"{name}_nmr")
            P.op("tensor", lambda e, s=s: e.matmul(
                self.ps_ab[0][:, :], self.smv("ones_1x128f", F32, 128, part=1),
                srow(2), start=True, stop=True),
                r=self.SC.K(4096, 2048) + ("ones_1x128f",), w=("ps_ab0",), tag=f"{name}_A")
            P.op("tensor", lambda e, s=s: e.matmul(
                self.ps_ab[1][:, :], self.smv("ones_1x128f", F32, 128, part=1),
                srow(3), start=True, stop=True),
                r=self.SC.K(6144, 2048) + ("ones_1x128f",), w=("ps_ab1",), tag=f"{name}_B")
            for c in range(NCH):
                dst = dst_reg.ap(dst_dtype, 512, byteoff=dst_off(c) + s * 512 * dsz)
                dk = dst_reg.K(dst_off(c) + s * 512 * dsz, 512 * dsz)
                P.op("vector", lambda e, c=c, s=s: e.tensor_tensor(
                    out=tmp, in0=src_ap(c, s), in1=self.ps_ab[0][:, :], op=Alu.mult),
                    r=src_k(c, s) + ("ps_ab0",), w=ktmp, tag=f"{name}_ap1")
                if gamma is None:
                    P.op("vector", lambda e, c=c, s=s, dst=dst: e.tensor_tensor(
                        out=dst, in0=tmp, in1=self.ps_ab[1][:, :], op=Alu.add),
                        r=ktmp + ("ps_ab1",), w=dk, tag=f"{name}_ap2")
                else:
                    P.op("vector", lambda e, c=c, s=s: e.tensor_tensor(
                        out=tmp, in0=tmp, in1=self.ps_ab[1][:, :], op=Alu.add),
                        r=ktmp + ("ps_ab1",), w=ktmp, tag=f"{name}_ap2")
                    P.op("vector", lambda e, c=c, s=s, dst=dst: e.tensor_scalar(
                        out=dst, in0=tmp, scalar1=gamma(c), scalar2=beta(c),
                        op0=Alu.mult, op1=Alu.add),
                        r=ktmp + tuple(gb_keys), w=dk, tag=f"{name}_ap3")

    def stream_w_cols(self, hbm, oc, n_in=D, tag="w"):
        P = self.P
        nin_ch = n_in // 128
        wslot = oc % 2
        wf32 = self.WP.ap(F32, nin_ch * 128, byteoff=0)
        wbf = self.WP.ap(BF16, nin_ch * 128, byteoff=4096 + 2048 * wslot)
        kf = self.WP.K(0, nin_ch * 512)
        kb = self.WP.K(4096 + 2048 * wslot, nin_ch * 256)
        src = hbm.ap()[:, oc * 128:(oc + 1) * 128]
        src = src.rearrange("(kc p) m -> p kc m", p=128)
        P.op("sync", lambda e, src=src, wf32=wf32, nin_ch=nin_ch: e.dma_start(
            out=wf32.rearrange("p (kc m) -> p kc m", kc=nin_ch), in_=src),
            w=kf, dma=True, tag=f"{tag}_dma{oc}")
        P.op("vector", lambda e, wf32=wf32, wbf=wbf: e.tensor_copy(out=wbf, in_=wf32),
             r=kf, w=kb, tag=f"{tag}_cvt{oc}")
        return (lambda kc: self.WP.ap(BF16, 128, byteoff=4096 + 2048 * wslot + kc * 256)), kb

    def stage_l0_attn(self, upto='all'):
        P = self.P
        ln0 = lambda kc, c0, cols: self.S1.ap(BF16, cols, byteoff=kc * 4096 + c0 * 2)
        ln0k = lambda kc, c0, cols: self.S1.K(kc * 4096 + c0 * 2, cols * 2)

        for kc in range(NCH):
            wf32 = self.WP.ap(F32, D, byteoff=0)
            kf = self.WP.K(0, 4096)
            P.op("sync", lambda e, kc=kc, wf32=wf32: e.dma_start(
                out=wf32, in_=self.inp["l0_wv"].ap()[kc * 128:(kc + 1) * 128, :]),
                w=kf, dma=True, tag=f"wv_dma{kc}")
            P.op("vector", lambda e, kc=kc, wf32=wf32: e.tensor_copy(
                out=self.PB.ap(BF16, D, byteoff=kc * 2048), in_=wf32),
                r=kf, w=self.PB.K(kc * 2048, 2048), tag=f"wv_cvt{kc}")
        P.op("sync", lambda e: e.dma_start(
            out=self.smv("wa", F32, 128).rearrange("p (kc m) -> p kc m", kc=8),
            in_=self.inp["l0_wa"].ap().rearrange("(kc p) m -> p kc m", p=128)),
            w=("wa",), dma=True, tag="wa_dma")
        P.op("vector", lambda e: e.tensor_copy(
            out=self.smv("wab", BF16, 128), in_=self.smv("wa", F32, 128)),
            r=("wa",), w=("wab",), tag="wa_cvt")
        P.op("sync", lambda e: e.dma_start(
            out=self.smv("ba", F32, 16, part=1), in_=self.inp["l0_ba"].ap()),
            w=("ba",), dma=True, tag="ba_dma")
        P.op("tensor", lambda e: e.matmul(
            self.ps_x[:, 0:16], self.smv("ones_1x128f", F32, 128, part=1),
            self.smv("ba", F32, 16, part=1), start=True, stop=True),
            r=("ba", "ones_1x128f"), w=("ps_x",), tag="ba_bc")
        P.op("vector", lambda e: e.tensor_copy(
            out=self.smv("ba2", F32, 16), in_=self.ps_x[:, 0:16]),
            r=("ps_x",), w=("ba2",), tag="ba2_ev")
        P.op("vector", lambda e: e.memset(self.smv("ones16", F32, 128, part=16), 1.0),
             w=("ones16",), tag="ones16")
        P.op("vector", lambda e: e.memset(self.smv("sz", F32, 520), 0.0),
             w=("sz",), tag="sz0")
        P.op("vector", lambda e: e.memset(self.smv("szbf", BF16, 520), 0.0),
             w=("szbf",), tag="szbf0")

        eps_ap, eps_k = self.cc(1e-6)
        ntt_run = 1 if upto == "ya0" else NTT
        for tt in range(ntt_run):
            b0, b1 = self.ps_ab[0], self.ps_ab[1]
            for kc in range(NCH) if upto != "ya0" else []:
                for half in range(2):
                    bank = (b0, b1)[half]
                    P.op("tensor", lambda e, tt=tt, kc=kc, half=half, bank=bank: e.matmul(
                        bank[:, 0:512], ln0(kc, tt * 128, 128),
                        self.PB.ap(BF16, 512, byteoff=kc * 2048 + half * 1024),
                        start=(kc == 0), stop=(kc == NCH - 1)),
                        r=ln0k(kc, tt * 128, 128) + self.PB.K(kc * 2048 + half * 1024, 1024),
                        w=(f"ps_ab{half}",), tag=f"v_mm{tt}_{kc}_{half}")
            for kc in range(NCH):
                P.op("tensor", lambda e, tt=tt, kc=kc: e.matmul(
                    self.ps_x[:, 0:16], ln0(kc, tt * 128, 128),
                    self.smv("wab", BF16, 16, byteoff=kc * 32),
                    start=(kc == 0), stop=(kc == NCH - 1)),
                    r=ln0k(kc, tt * 128, 128) + ("wab",), w=("ps_x",), tag=f"ya_mm{tt}_{kc}")
            for half in range(2) if upto != "ya0" else []:
                bank = (b0, b1)[half]
                P.op("vector", lambda e, tt=tt, half=half, bank=bank: e.tensor_copy(
                    out=self.S1.ap(BF16, 512, byteoff=32768 + tt * 2048 + half * 1024),
                    in_=bank[:, 0:512]),
                    r=(f"ps_ab{half}",), w=self.S1.K(32768 + tt * 2048 + half * 1024, 1024),
                    tag=f"v_ev{tt}_{half}")
            if upto in ("proj", "ya0") and tt == 0:
                P.op("vector", lambda e: e.tensor_copy(
                    out=self.S0.ap(F32, 16, byteoff=58368), in_=self.ps_x[:, 0:16]),
                    r=("ps_x",), w=self.S0.K(58368, 64), tag="ya_ps_dbg")
            if upto == "ya0":
                continue
            u = self.SC.ap(F32, 16, byteoff=4864)
            yv = self.SC.ap(F32, 16, byteoff=4928)
            mv = self.SC.ap(F32, 16, byteoff=4992)
            av = self.SC.ap(F32, 16, byteoff=5056)
            ku = self.SC.K(4864, 256)
            la = self.smv("la_tok", F32, 16, byteoff=tt * 64)
            one_ap, one_k = self.cc(1.0)
            P.op("vector", lambda e, tt=tt, u=u: e.tensor_tensor(
                out=u, in0=self.ps_x[:, 0:16], in1=self.smv("ba2", F32, 16), op=Alu.add),
                r=("ps_x", "ba2"), w=ku, tag=f"la_u{tt}")
            P.op("vector", lambda e, tt=tt, yv=yv, u=u: e.tensor_scalar(
                out=yv, in0=u, scalar1=2.0, scalar2=None, op0=Alu.add),
                r=ku, w=ku, tag=f"la_y{tt}")
            P.op("vector", lambda e, tt=tt, mv=mv, yv=yv: e.tensor_scalar(
                out=mv, in0=yv, scalar1=0.0, scalar2=None, op0=Alu.min),
                r=ku, w=ku, tag=f"la_m{tt}")
            P.op("scalar", lambda e, tt=tt, av=av, yv=yv: e.activation(
                out=av, in_=yv, func=Act.Abs),
                r=ku, w=ku, tag=f"la_a{tt}")
            P.op("scalar", lambda e, tt=tt, av=av: e.activation(
                out=av, in_=av, func=Act.Exp, scale=-1.0),
                r=ku, w=ku, tag=f"la_e{tt}")
            P.op("scalar", lambda e, tt=tt, av=av, one_ap=one_ap: e.activation(
                out=av, in_=av, func=Act.Ln, bias=one_ap[:, 0:1], scale=1.0),
                r=ku + (one_k,), w=ku, tag=f"la_l{tt}")
            P.op("vector", lambda e, tt=tt, la=la, mv=mv, av=av: e.tensor_tensor(
                out=la, in0=mv, in1=av, op=Alu.subtract),
                r=ku, w=("la_tok",), tag=f"la_f{tt}")
            if upto in ("proj", "ya0"):
                P.op("vector", lambda e, tt=tt, yv=yv: e.tensor_copy(
                    out=self.S0.ap(F32, 16, byteoff=57344 + tt * 64), in_=yv),
                    r=ku, w=self.S0.K(57344 + tt * 64, 64), tag=f"la_dbg{tt}")

        if upto == "ya0":
            return
        def proj_elu(dst_reg, w_name, tag):
            for oc in range(NCH):
                wbf, wkeys = self.stream_w_cols(self.inp[w_name], oc, tag=tag)
                for s in range(4):
                    bank = self.psb[(oc * 4 + s) % 4]
                    bk = f"psb/{(oc * 4 + s) % 4}"
                    for kc in range(NCH):
                        P.op("tensor", lambda e, oc=oc, s=s, kc=kc, bank=bank, wbf=wbf: e.matmul(
                            bank[:, :], wbf(kc), ln0(kc, s * 512, 512),
                            start=(kc == 0), stop=(kc == NCH - 1)),
                            r=wkeys + ln0k(kc, s * 512, 512), w=(bk,), tag=f"{tag}mm{oc}_{s}")
                    m = self.SC.ap(F32, 512, byteoff=0)
                    ev = self.SC.ap(F32, 512, byteoff=2048)
                    P.op("vector", lambda e, bank=bank, m=m: e.tensor_scalar(
                        out=m, in0=bank[:, :], scalar1=0.0, scalar2=None, op0=Alu.min),
                        r=(bk,), w=self.SC.K(0, 2048), tag=f"{tag}min{oc}_{s}")
                    P.op("scalar", lambda e, m=m, ev=ev: e.activation(
                        out=ev, in_=m, func=Act.Exp),
                        r=self.SC.K(0, 2048), w=self.SC.K(2048, 2048), tag=f"{tag}exp{oc}_{s}")
                    P.op("vector", lambda e, oc=oc, s=s, bank=bank, ev=ev, dst_reg=dst_reg:
                         e.scalar_tensor_tensor(
                             out=dst_reg.ap(BF16, 512, byteoff=oc * 4096 + s * 1024),
                             in0=bank[:, :], scalar=0.0, in1=ev,
                             op0=Alu.max, op1=Alu.add),
                         r=(bk,) + self.SC.K(2048, 2048),
                         w=dst_reg.K(oc * 4096 + s * 1024, 1024), tag=f"{tag}out{oc}_{s}")

        proj_elu(self.PB, "l0_wk", "k")
        proj_elu(self.PA, "l0_wq", "q")
        if upto == "proj":
            return

        kfm = lambda p, half, c0, cols: self.PB.ap(BF16, cols, byteoff=p * 4096 + c0 * 2, part=64, p0=half * 64)
        kfmk = lambda p, c0, cols: self.PB.K(p * 4096 + c0 * 2, cols * 2)
        qfm = lambda p, half, c0, cols: self.PA.ap(BF16, cols, byteoff=p * 4096 + c0 * 2, part=64, p0=half * 64)
        qfmk = lambda p, c0, cols: self.PA.K(p * 4096 + c0 * 2, cols * 2)
        LT = lambda c: self.WP.ap(F32, 128, part=16, byteoff=c * 512)
        LTk = lambda c: self.WP.K(c * 512, 512)
        lasl = self.SC.ap(F32, 128, part=16, byteoff=5120)
        lasl_k = self.SC.K(5120, 512)
        negLT = self.SC.ap(F32, 128, part=16, byteoff=5632)
        negLT_k = self.SC.K(5632, 512)
        Etb = self.SC.ap(BF16, 128, part=16, byteoff=6144)
        Etb_k = self.SC.K(6144, 256)
        ECh = self.SC.ap(F32, 128, part=16, byteoff=6400)
        ECh_k = self.SC.K(6400, 512)
        ECt = self.SC.ap(F32, 16, byteoff=6912)
        ECt_k = self.SC.K(6912, 64)
        ltok = self.SC.ap(F32, 16, byteoff=6976)
        ltok_k = self.SC.K(6976, 64)
        lrow = lambda sl: self.smv("lrow2", F32, 128, part=1, byteoff=512 * (sl % 2))
        lrow_k = lambda sl: (f"lrow2/{sl % 2}",)
        e2sb = self.SC.ap(BF16, 128, byteoff=7168)
        e2sb_k = self.SC.K(7168, 256)
        ktok = lambda cols, off: self.SC.ap(BF16, cols, byteoff=0 + off)
        kttok = lambda cols, off: self.SC.ap(BF16, cols, byteoff=2048 + off)
        dmat = self.SC.ap(F32, 128, byteoff=4096)
        dmat_k = self.SC.K(4096, 512)
        sd = self.SC.ap(BF16, 128, byteoff=4608)
        sd_k = self.SC.K(4608, 256)
        qts = self.SC.ap(BF16, 128, byteoff=7424)
        qts_k = self.SC.K(7424, 256)
        lam = self.smv("ecol", F32, 1)
        lam_k = ("ecol",)
        dcol = self.smv("srows", F32, 1)
        rcol = self.smv("srows", F32, 1, byteoff=4)
        dcol_k = ("srows",)

        gla = lambda tt, c0, cols: self.S1.ap(BF16, cols, byteoff=tt * 2048 + c0 * 2)
        glak = lambda tt, c0, cols: self.S1.K(tt * 2048 + c0 * 2, cols * 2)

        for c in range(NTT):
            P.op("tensor", lambda e, c=c: e.transpose(
                self.ps_x[0:16, 0:128],
                self.smv("la_tok", F32, 16, byteoff=c * 64),
                self.smv("id128f", F32, 128)),
                r=("la_tok", "id128f"), w=("ps_x",), tag=f"sc_latr{c}")
            P.op("vector", lambda e, c=c: e.tensor_copy(out=lasl, in_=self.ps_x[0:16, 0:128]),
                 r=("ps_x",), w=lasl_k, tag=f"sc_laev{c}")
            P.op("vector", lambda e, c=c: e.tensor_tensor_scan(
                out=LT(c), data0=self.smv("ones16", F32, 128, part=16), data1=lasl,
                initial=0.0, op0=Alu.mult, op1=Alu.add),
                r=lasl_k + ("ones16",), w=LTk(c), tag=f"sc_scan{c}")
            P.op("vector", lambda e, c=c: e.tensor_scalar(
                out=negLT, in0=LT(c), scalar1=-1.0, scalar2=None, op0=Alu.mult),
                r=LTk(c), w=negLT_k, tag=f"sc_neg{c}")
            P.op("scalar", lambda e, c=c: e.activation(
                out=Etb, in_=LT(c), func=Act.Exp),
                r=LTk(c), w=Etb_k, tag=f"sc_et{c}")
            P.op("vector", lambda e, c=c: e.tensor_scalar(
                out=ECh, in0=LT(c), scalar1=LT(c)[:, 127:128], scalar2=None,
                op0=Alu.subtract),
                r=LTk(c), w=ECh_k, tag=f"sc_ech{c}")
            P.op("scalar", lambda e: e.activation(
                out=ECh, in_=ECh, func=Act.Exp, scale=-1.0),
                r=ECh_k, w=ECh_k, tag=f"sc_eche{c}")
            P.op("tensor", lambda e: e.transpose(
                self.ps_x[0:128, 0:16], ECh, self.smv("id16f", F32, 16, part=16)),
                r=ECh_k + ("id16f",), w=("ps_x",), tag=f"sc_ect{c}")
            P.op("vector", lambda e: e.tensor_copy(out=ECt, in_=self.ps_x[0:128, 0:16]),
                 r=("ps_x",), w=ECt_k, tag=f"sc_ectev{c}")
            P.op("tensor", lambda e, c=c: e.transpose(
                self.ps_x[0:128, 16:32], LT(c), self.smv("id16f", F32, 16, part=16)),
                r=LTk(c) + ("id16f",), w=("ps_x",), tag=f"sc_lttr{c}")
            P.op("vector", lambda e: e.tensor_copy(out=ltok, in_=self.ps_x[0:128, 16:32]),
                 r=("ps_x",), w=ltok_k, tag=f"sc_ltev{c}")
            for p in range(NCH):
                P.op("tensor", lambda e, c=c, p=p: e.transpose(
                    self.ps_x[:, 0:512].bitcast(BF16)[:, 0:128],
                    self.PB.ap(BF16, 128, byteoff=p * 4096 + c * 256),
                    self.smv("id128b", BF16, 128)),
                    r=kfmk(p, c * 128, 128) + ("id128b",), w=("ps_x",), tag=f"sc_ktr{c}_{p}")
                P.op("vector", lambda e, p=p: e.tensor_copy(
                    out=ktok(128, p * 256), in_=self.ps_x[:, 0:512].bitcast(BF16)[:, 0:128]),
                    r=("ps_x",), w=self.SC.K(p * 256, 256), tag=f"sc_ktev{c}_{p}")
            for h in range(16):
                P.op("vector", lambda e, h=h: e.tensor_scalar(
                    out=kttok(64, h * 128), in0=ktok(64, h * 128),
                    scalar1=ECt[:, h:h + 1], scalar2=None, op0=Alu.mult),
                    r=self.SC.K(h * 128, 128) + ECt_k,
                    w=self.SC.K(2048 + h * 128, 128), tag=f"sc_kt{c}_{h}")
            for p in range(NCH):
                e2bank = self.ps_ab[p % 2]
                e2k = f"ps_ab{p % 2}"
                for half in range(2):
                    h = 2 * p + half
                    P.op("tensor", lambda e, c=c, h=h, half=half, e2bank=e2bank: e.matmul(
                        e2bank[0:1, 0:128], self.smv("id16f", F32, 1, part=16, byteoff=h * 4),
                        LT(c), start=True, stop=True),
                        r=LTk(c) + ("id16f",), w=(e2k,), tag=f"sc_lr{c}_{h}")
                    P.op("vector", lambda e, half=half, e2bank=e2bank: e.tensor_copy(
                        out=lrow(half), in_=e2bank[0:1, 0:128]),
                        r=(e2k,), w=lrow_k(half), tag=f"sc_lrev{c}_{h}")
                P.op("tensor", lambda e, e2bank=e2bank: e.matmul(
                    e2bank[0:64, 0:128], self.smv("ones_1x128f", F32, 64, part=1),
                    lrow(0), start=True, stop=True),
                    r=lrow_k(0) + ("ones_1x128f",), w=(e2k,), tag=f"sc_b2a{c}_{p}")
                P.op("tensor", lambda e, e2bank=e2bank: e.matmul(
                    e2bank[64:128, 0:128], self.smv("ones_1x128f", F32, 64, part=1),
                    lrow(1), start=True, stop=True),
                    r=lrow_k(1) + ("ones_1x128f",), w=(e2k,), tag=f"sc_b2b{c}_{p}")
                P.op("scalar", lambda e, e2bank=e2bank: e.activation(
                    out=lam, in_=e2bank[:, 127:128], func=Act.Exp),
                    r=(e2k,), w=lam_k, tag=f"sc_lam{c}_{p}")
                P.op("scalar", lambda e, e2bank=e2bank: e.activation(
                    out=e2sb, in_=e2bank[:, 0:128], func=Act.Exp),
                    r=(e2k,), w=e2sb_k, tag=f"sc_e2{c}_{p}")
                P.op("vector", lambda e, c=c, p=p: e.tensor_tensor(
                    out=qts, in0=self.PA.ap(BF16, 128, byteoff=p * 4096 + c * 256),
                    in1=e2sb, op=Alu.mult),
                    r=qfmk(p, c * 128, 128) + e2sb_k, w=qts_k, tag=f"sc_qt{c}_{p}")
                P.op("vector", lambda e, p=p: e.tensor_scalar(
                    out=self.smv("szscr", F32, 65), in0=self.smv("sz", F32, 65, byteoff=p * 260),
                    scalar1=lam[:, 0:1], scalar2=None, op0=Alu.mult),
                    r=("sz",) + lam_k, w=("szscr",), tag=f"sc_su1{c}_{p}")
                for half in range(2):
                    h = 2 * p + half
                    sb = self.psb[h % 2]
                    sbk = f"psb/{h % 2}"
                    db = self.psb[2 + h % 2]
                    dbk = f"psb/{2 + h % 2}"
                    P.op("tensor", lambda e, c=c, p=p, half=half, sb=sb: e.matmul(
                        sb[:, 0:128], kfm(p, half, c * 128, 128), qfm(p, half, c * 128, 128),
                        start=True, stop=True),
                        r=kfmk(p, c * 128, 128) + qfmk(p, c * 128, 128), w=(sbk,),
                        tag=f"sc_sT{c}_{h}")
                    P.op("tensor", lambda e, half=half, db=db: e.matmul(
                        db[:, 0:128], self.smv("ones_1x128f", F32, 128, part=1),
                        lrow(half), start=True, stop=True),
                        r=lrow_k(half) + ("ones_1x128f",), w=(dbk,), tag=f"sc_d1{c}_{h}")
                    P.op("vector", lambda e, h=h, db=db: e.scalar_tensor_tensor(
                        out=dmat, in0=db[:, 0:128], scalar=ltok[:, h:h + 1],
                        in1=self.smv("maskbig", F32, 128), op0=Alu.subtract, op1=Alu.min),
                        r=(dbk, "maskbig") + ltok_k, w=dmat_k, tag=f"sc_dmin{c}_{h}")
                    P.op("scalar", lambda e: e.activation(out=dmat, in_=dmat, func=Act.Exp),
                         r=dmat_k, w=dmat_k, tag=f"sc_dexp{c}_{h}")
                    P.op("vector", lambda e, sb=sb: e.tensor_tensor(
                        out=sd, in0=sb[:, 0:128], in1=dmat, op=Alu.mult),
                        r=(sbk,) + dmat_k, w=sd_k, tag=f"sc_sd{c}_{h}")
                    P.op("tensor", lambda e, c=c, h=h: e.matmul(
                        self.ps_st[:, 0:64], sd,
                        self.S1.ap(BF16, 64, byteoff=32768 + c * 2048 + h * 128),
                        start=True, stop=False),
                        r=sd_k + self.S1.K(32768 + c * 2048 + h * 128, 128),
                        w=("ps_st",), tag=f"sc_nv{c}_{h}")
                    P.op("tensor", lambda e, c=c, p=p, half=half: e.matmul(
                        self.ps_st[:, 0:64], qts[half * 64:half * 64 + 64, :],
                        self.smv("szbf", BF16, 64, byteoff=p * 130, part=64, p0=half * 64),
                        start=False, stop=True),
                        r=qts_k + ("szbf",), w=("ps_st",), tag=f"sc_ni{c}_{h}")
                    P.op("tensor", lambda e, c=c, h=h: e.matmul(
                        self.ps_st[:, 128:129], sd, self.smv("onescol_b", BF16, 1),
                        start=True, stop=False),
                        r=sd_k + ("onescol_b",), w=("ps_st",), tag=f"sc_nd{c}_{h}")
                    P.op("tensor", lambda e, c=c, p=p, half=half: e.matmul(
                        self.ps_st[:, 128:129], qts[half * 64:half * 64 + 64, :],
                        self.smv("szbf", BF16, 1, byteoff=p * 130 + 128, part=64, p0=half * 64),
                        start=False, stop=True),
                        r=qts_k + ("szbf",), w=("ps_st",), tag=f"sc_nid{c}_{h}")
                    P.op("vector", lambda e: e.tensor_scalar(
                        out=dcol, in0=self.ps_st[:, 128:129], scalar1=1e-6, scalar2=None,
                        op0=Alu.add),
                        r=("ps_st",), w=dcol_k, tag=f"sc_de{c}_{h}")
                    P.op("vector", lambda e: e.reciprocal(out=rcol, in_=dcol),
                         r=dcol_k, w=dcol_k, tag=f"sc_rc{c}_{h}")
                    P.op("scalar", lambda e, c=c, h=h: e.mul(
                        gla(c, h * 64, 64), self.ps_st[:, 0:64], rcol[:, 0:1]),
                        r=("ps_st",) + dcol_k, w=glak(c, h * 64, 64), tag=f"sc_gv{c}_{h}")
                    pr = half * 64
                    pcol = half * 128
                    P.op("tensor", lambda e, c=c, h=h, pr=pr, pcol=pcol: e.matmul(
                        self.ps_x[pr:pr + 64, pcol:pcol + 64], kttok(64, h * 128),
                        self.S1.ap(BF16, 64, byteoff=32768 + c * 2048 + h * 128),
                        start=True, stop=True),
                        r=self.SC.K(2048 + h * 128, 128) + self.S1.K(32768 + c * 2048 + h * 128, 128),
                        w=("ps_x",), tag=f"sc_sv{c}_{h}")
                    P.op("tensor", lambda e, c=c, h=h, pr=pr, pcol=pcol: e.matmul(
                        self.ps_x[pr:pr + 64, pcol + 64:pcol + 65], kttok(64, h * 128),
                        self.smv("onescol_b", BF16, 1),
                        start=True, stop=True),
                        r=self.SC.K(2048 + h * 128, 128) + ("onescol_b",),
                        w=("ps_x",), tag=f"sc_sz{c}_{h}")
                for half in range(2):
                    pr = half * 64
                    pcol = half * 128
                    P.op("vector", lambda e, p=p, pr=pr, pcol=pcol: e.tensor_tensor(
                        out=self.smv("sz", F32, 65, byteoff=p * 260, part=64, p0=pr),
                        in0=self.smv("szscr", F32, 65, part=64, p0=pr),
                        in1=self.ps_x[pr:pr + 64, pcol:pcol + 65], op=Alu.add),
                        r=("szscr", "ps_x"), w=("sz",), tag=f"sc_su2{c}_{p}_{half}")
                P.op("gpsimd", lambda e, p=p: e.tensor_copy(
                    out=self.smv("szbf", BF16, 65, byteoff=p * 130),
                    in_=self.smv("sz", F32, 65, byteoff=p * 260)),
                    r=("sz",), w=("szbf",), tag=f"sc_su3{c}_{p}")

        if upto == "scan":
            return
        for p in range(NCH):
            for tt in range(NTT):
                P.op("tensor", lambda e, p=p, tt=tt: e.transpose(
                    self.ps_x[:, 0:512].bitcast(BF16)[:, 0:128],
                    gla(tt, p * 128, 128), self.smv("id128b", BF16, 128)),
                    r=glak(tt, p * 128, 128) + ("id128b",), w=("ps_x",), tag=f"gf_tr{p}_{tt}")
                P.op("vector", lambda e, p=p, tt=tt: e.tensor_copy(
                    out=self.PA.ap(BF16, 128, byteoff=p * 4096 + tt * 256),
                    in_=self.ps_x[:, 0:512].bitcast(BF16)[:, 0:128]),
                    r=("ps_x",), w=self.PA.K(p * 4096 + tt * 256, 256), tag=f"gf_ev{p}_{tt}")

        for oc in range(NCH):
            wbf, wkeys = self.stream_w_cols(self.inp["l0_wo"], oc, tag="wo")
            for s in range(4):
                bank = self.psb[(oc * 4 + s) % 4]
                bk = f"psb/{(oc * 4 + s) % 4}"
                for kc in range(NCH):
                    P.op("tensor", lambda e, oc=oc, s=s, kc=kc, bank=bank, wbf=wbf: e.matmul(
                        bank[:, :], wbf(kc),
                        self.PA.ap(BF16, 512, byteoff=kc * 4096 + s * 1024),
                        start=(kc == 0), stop=(kc == NCH - 1)),
                        r=wkeys + self.PA.K(kc * 4096 + s * 1024, 1024), w=(bk,),
                        tag=f"wo_mm{oc}_{s}_{kc}")
                P.op("vector", lambda e, oc=oc, s=s, bank=bank: e.tensor_tensor(
                    out=self.S0.ap(F32, 512, byteoff=oc * 8192 + s * 2048),
                    in0=bank[:, :], in1=self.S0.ap(F32, 512, byteoff=oc * 8192 + s * 2048),
                    op=Alu.add),
                    r=(bk,) + self.S0.K(oc * 8192 + s * 2048, 2048),
                    w=self.S0.K(oc * 8192 + s * 2048, 2048), tag=f"wo_ev{oc}_{s}")

    def stage_l0_ffn(self):
        P = self.P
        ln1 = lambda kc, c0, cols: self.S1.ap(BF16, cols, byteoff=kc * 4096 + c0 * 2)
        ln1k = lambda kc, c0, cols: self.S1.K(kc * 4096 + c0 * 2, cols * 2)
        psx_bf = lambda: self.ps_x[:, 0:512].bitcast(BF16)

        def hid_ap(oc, c0, cols):
            reg = self.PA if oc < 16 else self.PB
            off = (oc if oc < 16 else oc - 16) * 2048
            return reg.ap(BF16, cols, byteoff=off + c0 * 2)

        def hid_k(oc, c0, cols):
            reg = self.PA if oc < 16 else self.PB
            off = (oc if oc < 16 else oc - 16) * 2048
            return reg.K(off + c0 * 2, cols * 2)

        def stage_wT(hbm, oc, slot, tag):
            wi = self.WP.ap(I32, D, byteoff=0)
            ki = self.WP.K(0, 4096)
            wb = self.WP.ap(BF16, D, byteoff=4096 + 2048 * (slot % 2))
            kb = self.WP.K(4096 + 2048 * (slot % 2), 2048)
            P.op("sync", lambda e, oc=oc, wi=wi: e.dma_start(
                out=wi, in_=hbm.ap()[oc * 128:(oc + 1) * 128, :]),
                w=ki, dma=True, tag=f"{tag}dma{oc}")
            P.op("vector", lambda e, wi=wi, wb=wb: e.tensor_copy(out=wb, in_=wi),
                 r=ki, w=kb, tag=f"{tag}cvt{oc}")
            st_off = (slot % 4) * 2048
            for kc in range(NCH):
                P.op("tensor", lambda e, kc=kc, wb=wb: e.transpose(
                    psx_bf()[:, 0:128], wb[:, kc * 128:(kc + 1) * 128],
                    self.smv("id128b", BF16, 128)),
                    r=kb + ("id128b",), w=("ps_x",), tag=f"{tag}tr{oc}_{kc}")
                P.op("vector", lambda e, kc=kc, st_off=st_off: e.tensor_copy(
                    out=self.SC.ap(BF16, 128, byteoff=st_off + kc * 256),
                    in_=psx_bf()[:, 0:128]),
                    r=("ps_x",), w=self.SC.K(st_off + kc * 256, 256), tag=f"{tag}trev{oc}_{kc}")
            keys = self.SC.K(st_off, 2048)
            return (lambda kc, st_off=st_off: self.SC.ap(BF16, 128, byteoff=st_off + kc * 256)), keys

        for half in range(2):
            t0 = half * 1024
            for oc in range(24):
                gT, gkeys = stage_wT(self.inp["l0f_wg"], oc, 2 * oc, "wg")
                uT, ukeys = stage_wT(self.inp["l0f_wu"], oc, 2 * oc + 1, "wu")
                for s in range(2):
                    c0 = t0 + s * 512
                    bg = self.psb[(oc * 2 + s) % 2]
                    bgk = f"psb/{(oc * 2 + s) % 2}"
                    bu = self.psb[2 + (oc * 2 + s) % 2]
                    buk = f"psb/{2 + (oc * 2 + s) % 2}"
                    for kc in range(NCH):
                        P.op("tensor", lambda e, kc=kc, c0=c0, bg=bg, gT=gT: e.matmul(
                            bg[:, :], gT(kc), ln1(kc, c0, 512),
                            start=(kc == 0), stop=(kc == NCH - 1)),
                            r=gkeys + ln1k(kc, c0, 512), w=(bgk,), tag=f"g_mm{half}_{oc}_{s}")
                        P.op("tensor", lambda e, kc=kc, c0=c0, bu=bu, uT=uT: e.matmul(
                            bu[:, :], uT(kc), ln1(kc, c0, 512),
                            start=(kc == 0), stop=(kc == NCH - 1)),
                            r=ukeys + ln1k(kc, c0, 512), w=(buk,), tag=f"u_mm{half}_{oc}_{s}")
                    sg = self.S1.ap(F32, 512, byteoff=32768 + (s % 2) * 2048)
                    sgk = self.S1.K(32768 + (s % 2) * 2048, 2048)
                    P.op("scalar", lambda e, bg=bg, sg=sg: e.activation(
                        out=sg, in_=bg[:, :], func=Act.Silu, scale=1.0 / 127.0),
                        r=(bgk,), w=sgk, tag=f"sil{half}_{oc}_{s}")
                    P.op("vector", lambda e, oc=oc, s=s, bu=bu, sg=sg: e.tensor_tensor(
                        out=hid_ap(oc, s * 512, 512), in0=bu[:, :], in1=sg, op=Alu.mult),
                        r=(buk,) + sgk, w=hid_k(oc, s * 512, 512), tag=f"hid{half}_{oc}_{s}")
            for dc in range(NCH):
                lhsT = {}
                for q in range(4):
                    wi = self.WP.ap(I32, 768, byteoff=0)
                    ki = self.WP.K(0, 3072)
                    wb = self.WP.ap(BF16, 768, byteoff=4096 + 2048 * (q % 2))
                    kb = self.WP.K(4096 + 2048 * (q % 2), 1536)
                    P.op("sync", lambda e, dc=dc, q=q, wi=wi: e.dma_start(
                        out=wi, in_=self.inp["l0f_wd"].ap()[dc * 128:(dc + 1) * 128,
                                                            q * 768:(q + 1) * 768]),
                        w=ki, dma=True, tag=f"wd_dma{dc}_{q}")
                    P.op("vector", lambda e, wi=wi, wb=wb: e.tensor_copy(out=wb, in_=wi),
                         r=ki, w=kb, tag=f"wd_cvt{dc}_{q}")
                    for j in range(6):
                        hc = q * 6 + j
                        st_off = (hc % 24) * 256
                        P.op("tensor", lambda e, j=j, wb=wb: e.transpose(
                            psx_bf()[:, 0:128], wb[:, j * 128:(j + 1) * 128],
                            self.smv("id128b", BF16, 128)),
                            r=kb + ("id128b",), w=("ps_x",), tag=f"wd_tr{dc}_{hc}")
                        P.op("vector", lambda e, st_off=st_off: e.tensor_copy(
                            out=self.SC.ap(BF16, 128, byteoff=st_off),
                            in_=psx_bf()[:, 0:128]),
                            r=("ps_x",), w=self.SC.K(st_off, 256), tag=f"wd_trev{dc}_{hc}")
                        lhsT[hc] = (self.SC.ap(BF16, 128, byteoff=st_off),
                                    self.SC.K(st_off, 256))
                for s in range(2):
                    c0 = t0 + s * 512
                    bank = self.psb[(dc * 2 + s) % 4]
                    bk = f"psb/{(dc * 2 + s) % 4}"
                    for hc in range(24):
                        ap_, kk = lhsT[hc]
                        P.op("tensor", lambda e, hc=hc, c0=c0, t0=t0, bank=bank, ap_=ap_: e.matmul(
                            bank[:, :], ap_, hid_ap(hc, c0 - t0, 512),
                            start=(hc == 0), stop=(hc == 23)),
                            r=kk + hid_k(hc, c0 - t0, 512), w=(bk,), tag=f"d_mm{half}_{dc}_{hc}")
                    P.op("vector", lambda e, dc=dc, c0=c0, bank=bank: e.scalar_tensor_tensor(
                        out=self.S0.ap(F32, 512, byteoff=dc * 8192 + c0 * 4),
                        in0=bank[:, :], scalar=1.0 / 16129.0,
                        in1=self.S0.ap(F32, 512, byteoff=dc * 8192 + c0 * 4),
                        op0=Alu.mult, op1=Alu.add),
                        r=(bk,) + self.S0.K(dc * 8192 + c0 * 4, 2048),
                        w=self.S0.K(dc * 8192 + c0 * 4, 2048), tag=f"d_ev{half}_{dc}_{s}")

    def stage_l1_attn(self, nlayers=4, upto='all'):
        P = self.P

        for c in range(NCH):
            P.op("sync", lambda e, c=c: e.dma_start(
                out=self.xspill_h.ap()[c], in_=self.S0.ap(F32, T, byteoff=c * 8192)),
                r=self.S0.K(c * 8192, 8192), dma=True, tag=f"xsp{c}")

        h_ap = lambda kc, c0, cols: self.S1.ap(F32, cols, byteoff=kc * 8192 + c0 * 4)
        h_k = lambda kc, c0, cols: self.S1.K(kc * 8192 + c0 * 4, cols * 4)
        hbf = lambda kc, c0, cols: self.S0.ap(BF16, cols, byteoff=kc * 4096 + c0 * 2)
        hbf_k = lambda kc, c0, cols: self.S0.K(kc * 4096 + c0 * 2, cols * 2)
        r_ap = hbf
        r_k = hbf_k
        kfm = lambda kc, c0, cols: self.PA.ap(BF16, cols, byteoff=kc * 4096 + c0 * 2)
        kfm_k = lambda kc, c0, cols: self.PA.K(kc * 4096 + c0 * 2, cols * 2)
        vtok = lambda tt, c0, cols: self.PB.ap(BF16, cols, byteoff=tt * 2048 + c0 * 2)
        vtok_k = lambda tt, c0, cols: self.PB.K(tt * 2048 + c0 * 2, cols * 2)
        wvst = lambda kc, c0, cols: self.S0.ap(BF16, cols, byteoff=32768 + kc * 2048 + c0 * 2)
        wvst_k = lambda kc, c0, cols: self.S0.K(32768 + kc * 2048 + c0 * 2, cols * 2)
        qsl = lambda b, kc, c0, cols: self.S0.ap(BF16, cols, byteoff=49152 + b * 4096 + kc * 512 + c0 * 2)
        qsl_k = lambda b, kc, c0, cols: self.S0.K(49152 + b * 4096 + kc * 512 + c0 * 2, cols * 2)
        scB = self.S0.ap(F32, 256, byteoff=57344)
        scB_k = self.S0.K(57344, 1024)
        psx_bf = lambda: self.ps_x[:, 0:512].bitcast(BF16)

        def stream_cols(src_ap2d, oc, slot, tag, n_in=D):
            nch = n_in // 128
            wf32 = self.WP.ap(F32, nch * 128, byteoff=0)
            kf = self.WP.K(0, nch * 512)
            wb_off = 4096 + 2048 * (slot % 2)
            wbf = self.WP.ap(BF16, nch * 128, byteoff=wb_off)
            kb = self.WP.K(wb_off, nch * 256)
            src = src_ap2d[:, oc * 128:(oc + 1) * 128].rearrange("(kc p) m -> p kc m", p=128)
            P.op("sync", lambda e, src=src, wf32=wf32, nch=nch: e.dma_start(
                out=wf32.rearrange("p (kc m) -> p kc m", kc=nch), in_=src),
                w=kf, dma=True, tag=f"{tag}dma{oc}")
            P.op("vector", lambda e, wf32=wf32, wbf=wbf: e.tensor_copy(out=wbf, in_=wf32),
                 r=kf, w=kb, tag=f"{tag}cvt{oc}")
            return (lambda kc, wb_off=wb_off: self.WP.ap(BF16, 128, byteoff=wb_off + kc * 256)), kb

        def proj_fm(src_ap2d, oc_list, dst_ap, dst_k, tag, mov_ap, mov_k, nstr=4,
                    evac="copy", bias_col=None, bias_keys=(), extra=None, scale=None):
            for i, oc in enumerate(oc_list):
                wbf, wkeys = stream_cols(src_ap2d, oc, i, tag)
                for s in range(nstr):
                    bank = self.psb[(i * nstr + s) % 4]
                    bk = f"psb/{(i * nstr + s) % 4}"
                    for kc in range(NCH):
                        P.op("tensor", lambda e, oc=oc, s=s, kc=kc, bank=bank, wbf=wbf: e.matmul(
                            bank[:, :], wbf(kc), mov_ap(kc, s * 512, 512),
                            start=(kc == 0), stop=(kc == NCH - 1)),
                            r=wkeys + mov_k(kc, s * 512, 512), w=(bk,), tag=f"{tag}mm{oc}_{s}")
                    if evac == "copy":
                        P.op("vector", lambda e, oc=oc, s=s, bank=bank: e.tensor_copy(
                            out=dst_ap(oc, s * 512, 512), in_=bank[:, :]),
                            r=(bk,), w=dst_k(oc, s * 512, 512), tag=f"{tag}ev{oc}_{s}")
                    elif evac == "scale":
                        P.op("vector", lambda e, oc=oc, s=s, bank=bank: e.tensor_scalar(
                            out=dst_ap(oc, s * 512, 512), in0=bank[:, :],
                            scalar1=scale, scalar2=None, op0=Alu.mult),
                            r=(bk,), w=dst_k(oc, s * 512, 512), tag=f"{tag}ev{oc}_{s}")
                    elif evac == "custom":
                        extra(oc, s, bank, bk)

        for l in range(nlayers):
            wkqv2d = self.inp["l1_wkqv"].ap()[l]
            wm1_2d = self.inp["l1_wm1"].ap()[l]
            wm2_2d = self.inp["l1_wm2"].ap()[l]
            wf1_2d = self.inp["l1_wf1"].ap()[l]
            wf2_2d = self.inp["l1_wf2"].ap()[l]

            def load_bias(hbm_row, n, col0, tag):
                nch = n // 128
                P.op("sync", lambda e, hbm_row=hbm_row, nch=nch: e.dma_start(
                    out=self.smv("bias_nat", F32, 128, part=nch),
                    in_=hbm_row.rearrange("(c p) -> c p", p=128)),
                    w=("bias_nat",), dma=True, tag=f"b_{tag}dma")
                P.op("tensor", lambda e, nch=nch: e.transpose(
                    self.ps_x[0:128, 0:nch], self.smv("bias_nat", F32, 128, part=nch),
                    self.smv("id16f", F32, nch, part=nch)),
                    r=("bias_nat", "id16f"), w=("ps_x",), tag=f"b_{tag}tr")
                P.op("vector", lambda e, nch=nch, col0=col0: e.tensor_copy(
                    out=self.smv("biasT", F32, nch, byteoff=col0 * 4),
                    in_=self.ps_x[0:128, 0:nch]),
                    r=("ps_x",), w=("biasT",), tag=f"b_{tag}ev")

            load_bias(self.inp["l1_bm1"].ap()[l], 2048, 0, f"bm1_{l}")
            load_bias(self.inp["l1_bm2"].ap()[l], D, 16, f"bm2_{l}")
            load_bias(self.inp["l1_gng"].ap()[l], D, 24, f"gng_{l}")
            load_bias(self.inp["l1_gnb"].ap()[l], D, 32, f"gnb_{l}")
            load_bias(self.inp["l1_bf1"].ap()[l], 2048, 40, f"bf1_{l}")
            load_bias(self.inp["l1_bf2"].ap()[l], D, 56, f"bf2_{l}")
            load_bias(self.inp["l1_lng"].ap()[l], D, 64, f"lng_{l}")
            load_bias(self.inp["l1_lnb"].ap()[l], D, 72, f"lnb_{l}")
            bcol = lambda c: self.smv("biasT", F32, 1, byteoff=c * 4)

            for kc in range(NCH):
                for s in range(2):
                    P.op("vector", lambda e, kc=kc, s=s: e.tensor_copy(
                        out=hbf(kc, s * 1024, 1024), in_=h_ap(kc, s * 1024, 1024)),
                        r=h_k(kc, s * 1024, 1024), w=hbf_k(kc, s * 1024, 1024),
                        tag=f"hbf{l}_{kc}_{s}")

            for kc in range(NCH):
                wf32 = self.WP.ap(F32, D, byteoff=(kc % 2) * 4096)
                kf = self.WP.K((kc % 2) * 4096, 4096)
                P.op("sync", lambda e, kc=kc, wf32=wf32, wkqv2d=wkqv2d: e.dma_start(
                    out=wf32, in_=wkqv2d[kc * 128:(kc + 1) * 128, 2048:3072]),
                    w=kf, dma=True, tag=f"wv{l}dma{kc}")
                P.op("vector", lambda e, kc=kc, wf32=wf32: e.tensor_copy(
                    out=wvst(kc, 0, D), in_=wf32),
                    r=kf, w=wvst_k(kc, 0, D), tag=f"wv{l}cvt{kc}")
            for tt in range(NTT):
                for half in range(2):
                    bank = self.ps_ab[half]
                    bkk = f"ps_ab{half}"
                    for kc in range(NCH):
                        P.op("tensor", lambda e, tt=tt, half=half, kc=kc, bank=bank: e.matmul(
                            bank[:, :], hbf(kc, tt * 128, 128), wvst(kc, half * 512, 512),
                            start=(kc == 0), stop=(kc == NCH - 1)),
                            r=hbf_k(kc, tt * 128, 128) + wvst_k(kc, half * 512, 512),
                            w=(bkk,), tag=f"v{l}mm{tt}_{half}")
                    P.op("vector", lambda e, tt=tt, half=half, bank=bank: e.tensor_copy(
                        out=vtok(tt, half * 512, 512), in_=bank[:, :]),
                        r=(bkk,), w=vtok_k(tt, half * 512, 512), tag=f"v{l}ev{tt}_{half}")
            qwst_k = lambda oc: self.S0.K(32768 + oc * 2048, 2048)
            for oc in range(NCH):
                qstg = self.S0.ap(F32, 1024, byteoff=49152 + (oc % 2) * 4096)
                qsk = self.S0.K(49152 + (oc % 2) * 4096, 4096)
                src = wkqv2d[:, 1024 + oc * 128:1024 + (oc + 1) * 128].rearrange(
                    "(kc p) m -> p kc m", p=128)
                P.op("sync", lambda e, src=src, qstg=qstg: e.dma_start(
                    out=qstg.rearrange("p (kc m) -> p kc m", kc=NCH), in_=src),
                    w=qsk, dma=True, tag=f"qw{l}dma{oc}")
                P.op("vector", lambda e, oc=oc, qstg=qstg: e.tensor_copy(
                    out=self.S0.ap(BF16, 1024, byteoff=32768 + oc * 2048), in_=qstg),
                    r=qsk, w=qwst_k(oc), tag=f"qw{l}cvt{oc}")
            proj_fm(wkqv2d, list(range(8)), kfm, kfm_k, f"k{l}_",
                    hbf, hbf_k, evac="copy")

            for g in range(8):
                qb = g % 2
                for oc in range(NCH):
                    bank = self.psb[oc % 2]
                    bk = f"psb/{oc % 2}"
                    for kc in range(NCH):
                        P.op("tensor", lambda e, g=g, oc=oc, kc=kc, bank=bank: e.matmul(
                            bank[:, 0:256],
                            self.S0.ap(BF16, 128, byteoff=32768 + oc * 2048 + kc * 256),
                            hbf(kc, g * 256, 256),
                            start=(kc == 0), stop=(kc == NCH - 1)),
                            r=qwst_k(oc) + hbf_k(kc, g * 256, 256), w=(bk,), tag=f"q{l}mm{g}_{oc}")
                    P.op("vector", lambda e, g=g, oc=oc, bank=bank, qb=qb: e.tensor_scalar(
                        out=qsl(qb, oc, 0, 256), in0=bank[:, 0:256],
                        scalar1=1.0 / 32.0, scalar2=None, op0=Alu.mult),
                        r=(bk,), w=qsl_k(qb, oc, 0, 256), tag=f"q{l}ev{g}_{oc}")
                nkt = 2 * g + 2
                for kt in range(nkt):
                    bank = self.psb[2 + kt % 2]
                    bk = f"psb/{2 + kt % 2}"
                    for kc in range(NCH):
                        P.op("tensor", lambda e, g=g, kt=kt, kc=kc, bank=bank, qb=qb: e.matmul(
                            bank[:, 0:256], kfm(kc, kt * 128, 128), qsl(qb, kc, 0, 256),
                            start=(kc == 0), stop=(kc == NCH - 1)),
                            r=kfm_k(kc, kt * 128, 128) + qsl_k(qb, kc, 0, 256),
                            w=(bk,), tag=f"s{l}mm{g}_{kt}")
                    if kt >= 2 * g:
                        dk = kt - 2 * g
                        P.op("vector", lambda e, kt=kt, dk=dk, bank=bank: e.tensor_tensor(
                            out=self.SC.ap(BF16, 256, byteoff=kt * 512),
                            in0=bank[:, 0:256], in1=self.smv(f"trimask{dk}", BF16, 256),
                            op=Alu.mult),
                            r=(bk, f"trimask{dk}"), w=self.SC.K(kt * 512, 512),
                            tag=f"s{l}ev{g}_{kt}")
                    else:
                        P.op("vector", lambda e, kt=kt, bank=bank: e.tensor_copy(
                            out=self.SC.ap(BF16, 256, byteoff=kt * 512), in_=bank[:, 0:256]),
                            r=(bk,), w=self.SC.K(kt * 512, 512), tag=f"s{l}ev{g}_{kt}")
                for kt in range(nkt):
                    P.op("tensor", lambda e, kt=kt, nkt=nkt: e.matmul(
                        self.ps_st[0:1, 0:256], self.smv("onescol_b", BF16, 1),
                        self.SC.ap(BF16, 256, byteoff=kt * 512),
                        start=(kt == 0), stop=(kt == nkt - 1)),
                        r=self.SC.K(kt * 512, 512) + ("onescol_b",), w=("ps_st",),
                        tag=f"d{l}mm{g}_{kt}")
                P.op("sync", lambda e, g=g: e.dma_start(
                    out=self.smv("rnrow", F32, 256, part=1),
                    in_=self.const_h["rn"].ap()[0:1, g * 256:(g + 1) * 256]),
                    w=("rnrow",), dma=True, tag=f"rn{l}{g}")
                lr = lambda: self.smv("lrow2", F32, 256, part=1)
                P.op("scalar", lambda e: e.activation(
                    out=lr(), in_=self.ps_st[0:1, 0:256], func=Act.Abs),
                    r=("ps_st",), w=("lrow2/0",), tag=f"sc{l}abs{g}")
                P.op("vector", lambda e: e.tensor_tensor(
                    out=lr(), in0=lr(), in1=self.smv("rnrow", F32, 256, part=1),
                    op=Alu.max),
                    r=("lrow2/0", "rnrow"), w=("lrow2/0",), tag=f"sc{l}max{g}")
                P.op("vector", lambda e: e.reciprocal(out=lr(), in_=lr()),
                     r=("lrow2/0",), w=("lrow2/0",), tag=f"sc{l}rc{g}")
                P.op("tensor", lambda e: e.matmul(
                    self.ps_ab[0][:, 0:256], self.smv("ones_1x128f", F32, 128, part=1),
                    lr(), start=True, stop=True),
                    r=("lrow2/0", "ones_1x128f"), w=("ps_ab0",), tag=f"sc{l}bc{g}")
                P.op("vector", lambda e: e.tensor_copy(out=scB, in_=self.ps_ab[0][:, 0:256]),
                     r=("ps_ab0",), w=scB_k, tag=f"sc{l}ev{g}")
                for vc in range(NCH):
                    bank = self.psb[vc % 2]
                    bk = f"psb/{vc % 2}"
                    for kt in range(nkt):
                        P.op("tensor", lambda e, g=g, vc=vc, kt=kt, nkt=nkt, bank=bank: e.matmul(
                            bank[:, 0:256], vtok(kt, vc * 128, 128),
                            self.SC.ap(BF16, 256, byteoff=kt * 512),
                            start=(kt == 0), stop=(kt == nkt - 1)),
                            r=vtok_k(kt, vc * 128, 128) + self.SC.K(kt * 512, 512),
                            w=(bk,), tag=f"r{l}mm{g}_{vc}_{kt}")
                    P.op("vector", lambda e, g=g, vc=vc, bank=bank: e.tensor_tensor(
                        out=r_ap(vc, g * 256, 256), in0=bank[:, 0:256], in1=scB,
                        op=Alu.mult),
                        r=(bk,) + scB_k, w=r_k(vc, g * 256, 256), tag=f"r{l}ev{g}_{vc}")

            for oc in range(NCH):
                aT, akeys = stream_cols(wm1_2d, oc, 0, f"a{l}_")
                gT, gkeys = stream_cols(wm1_2d, oc + 8, 1, f"g{l}_")
                for s in range(4):
                    ba_ = self.psb[s % 2]
                    bak = f"psb/{s % 2}"
                    bg_ = self.psb[2 + s % 2]
                    bgk = f"psb/{2 + s % 2}"
                    for kc in range(NCH):
                        P.op("tensor", lambda e, oc=oc, s=s, kc=kc, ba_=ba_, aT=aT: e.matmul(
                            ba_[:, :], aT(kc), r_ap(kc, s * 512, 512),
                            start=(kc == 0), stop=(kc == NCH - 1)),
                            r=akeys + r_k(kc, s * 512, 512), w=(bak,), tag=f"a{l}mm{oc}_{s}")
                        P.op("tensor", lambda e, oc=oc, s=s, kc=kc, bg_=bg_, gT=gT: e.matmul(
                            bg_[:, :], gT(kc), r_ap(kc, s * 512, 512),
                            start=(kc == 0), stop=(kc == NCH - 1)),
                            r=gkeys + r_k(kc, s * 512, 512), w=(bgk,), tag=f"g{l}mm{oc}_{s}")
                    sg = self.S0.ap(F32, 512, byteoff=59392 + (s % 2) * 2048)
                    sgk = self.S0.K(59392 + (s % 2) * 2048, 2048)
                    P.op("scalar", lambda e, oc=oc, bg_=bg_, sg=sg: e.activation(
                        out=sg, in_=bg_[:, :], func=Act.Silu, bias=bcol(8 + oc), scale=1.0),
                        r=(bgk, "biasT"), w=sgk, tag=f"sil{l}_{oc}_{s}")
                    P.op("vector", lambda e, oc=oc, s=s, ba_=ba_, sg=sg: e.scalar_tensor_tensor(
                        out=self.PA.ap(BF16, 512, byteoff=oc * 4096 + s * 1024),
                        in0=ba_[:, :], scalar=bcol(oc), in1=sg,
                        op0=Alu.add, op1=Alu.mult),
                        r=(bak, "biasT") + sgk,
                        w=self.PA.K(oc * 4096 + s * 1024, 1024), tag=f"pr{l}_{oc}_{s}")

            prod_ap = lambda kc, c0, cols: self.PA.ap(BF16, cols, byteoff=kc * 4096 + c0 * 2)
            prod_k = lambda kc, c0, cols: self.PA.K(kc * 4096 + c0 * 2, cols * 2)
            m2_ap = lambda kc, c0, cols: self.PB.ap(BF16, cols, byteoff=kc * 4096 + c0 * 2)
            m2_k = lambda kc, c0, cols: self.PB.K(kc * 4096 + c0 * 2, cols * 2)
            nmm = [0]
            for oc in range(NCH):
                wT, wkeys = stream_cols(wm2_2d, oc, oc, f"m2{l}_")
                for s in range(4):
                    bank = self.psb[(oc * 4 + s) % 4]
                    bk = f"psb/{(oc * 4 + s) % 4}"
                    for kc in range(NCH):
                        P.op("tensor", lambda e, oc=oc, s=s, kc=kc, bank=bank, wT=wT: e.matmul(
                            bank[:, :], wT(kc), prod_ap(kc, s * 512, 512),
                            start=(kc == 0), stop=(kc == NCH - 1)),
                            r=wkeys + prod_k(kc, s * 512, 512), w=(bk,), tag=f"m2{l}mm{oc}_{s}")
                    P.op("vector", lambda e, oc=oc, s=s, bank=bank: e.tensor_scalar(
                        out=m2_ap(oc, s * 512, 512), in0=bank[:, :],
                        scalar1=bcol(16 + oc), scalar2=None, op0=Alu.add),
                        r=(bk, "biasT"), w=m2_k(oc, s * 512, 512), tag=f"m2{l}ev{oc}_{s}")
            for oc in range(NCH):
                for s in range(4):
                    i = oc * 4 + s
                    P.op("scalar", lambda e, oc=oc, s=s: e.activation(
                        out=self.SC.ap(BF16, 512, byteoff=0), in_=m2_ap(oc, s * 512, 512),
                        func=Act.Square),
                        r=m2_k(oc, s * 512, 512), w=self.SC.K(0, 1024), tag=f"gnsq{l}_{oc}_{s}")
                    P.op("tensor", lambda e, i=i, oc=oc, s=s: e.matmul(
                        self.ps_st[0:1, 0:512], self.smv("onescol_b", BF16, 1),
                        m2_ap(oc, s * 512, 512), start=(i == 0), stop=(i == 31)),
                        r=m2_k(oc, s * 512, 512) + ("onescol_b",), w=("ps_st",),
                        tag=f"gns{l}_{oc}_{s}")
                    P.op("tensor", lambda e, i=i: e.matmul(
                        self.ps_x[0:1, 0:512], self.smv("onescol_b", BF16, 1),
                        self.SC.ap(BF16, 512, byteoff=0), start=(i == 0), stop=(i == 31)),
                        r=self.SC.K(0, 1024) + ("onescol_b",), w=("ps_x",),
                        tag=f"gnq{l}_{oc}_{s}")
            gs = lambda i: self.smv("srows", F32, 1, part=1, byteoff=i * 4)
            P.op("vector", lambda e: e.reduce_sum(
                out=gs(0), in_=self.ps_st[0:1, 0:512], axis=AX.X, op=Alu.add)
                if False else e.tensor_reduce(
                out=gs(0), in_=self.ps_st[0:1, 0:512], axis=AX.X, op=Alu.add),
                r=("ps_st",), w=("srows",), tag=f"gnr1{l}")
            P.op("vector", lambda e: e.tensor_reduce(
                out=gs(1), in_=self.ps_x[0:1, 0:512], axis=AX.X, op=Alu.add),
                r=("ps_x",), w=("srows",), tag=f"gnr2{l}")
            NT2 = float(T * D)
            P.op("vector", lambda e: e.tensor_scalar(
                out=gs(0), in0=gs(0), scalar1=1.0 / NT2, scalar2=None, op0=Alu.mult),
                r=("srows",), w=("srows",), tag=f"gnmu{l}")
            P.op("vector", lambda e: e.tensor_tensor(
                out=gs(2), in0=gs(0), in1=gs(0), op=Alu.mult),
                r=("srows",), w=("srows",), tag=f"gnmsq{l}")
            P.op("vector", lambda e: e.scalar_tensor_tensor(
                out=gs(1), in0=gs(1), scalar=1.0 / NT2, in1=gs(2),
                op0=Alu.mult, op1=Alu.subtract),
                r=("srows",), w=("srows",), tag=f"gnvar{l}")
            epsap, epsk = self.cc(1e-5)
            P.op("scalar", lambda e, epsap=epsap: e.activation(
                out=gs(1), in_=gs(1), func=Act.Ln, bias=epsap[0:1, 0:1], scale=1.0),
                r=("srows", epsk), w=("srows",), tag=f"gnln{l}")
            P.op("scalar", lambda e: e.activation(
                out=gs(1), in_=gs(1), func=Act.Exp, scale=-0.5),
                r=("srows",), w=("srows",), tag=f"gnrs{l}")
            P.op("tensor", lambda e: e.matmul(
                self.ps_x[:, 0:2], self.smv("ones_1x128f", F32, 128, part=1),
                self.smv("srows", F32, 2, part=1), start=True, stop=True),
                r=("srows", "ones_1x128f"), w=("ps_x",), tag=f"gnbc{l}")
            P.op("vector", lambda e: e.tensor_copy(
                out=self.smv("ecol", F32, 2), in_=self.ps_x[:, 0:2]),
                r=("ps_x",), w=("ecol",), tag=f"gnbe{l}")
            mu_c = lambda: self.smv("ecol", F32, 1)
            rs_c = lambda: self.smv("ecol", F32, 1, byteoff=4)
            P.op("vector", lambda e: e.tensor_scalar(
                out=self.smv("ecol", F32, 8, byteoff=8),
                in0=self.smv("biasT", F32, 8, byteoff=24 * 4),
                scalar1=rs_c(), scalar2=None, op0=Alu.mult),
                r=("biasT", "ecol"), w=("ecol",), tag=f"gnA{l}")
            P.op("vector", lambda e: e.scalar_tensor_tensor(
                out=self.smv("ecol", F32, 8, byteoff=40),
                in0=self.smv("ecol", F32, 8, byteoff=8), scalar=mu_c(),
                in1=self.smv("biasT", F32, 8, byteoff=32 * 4),
                op0=Alu.mult, op1=Alu.subtract),
                r=("ecol", "biasT"), w=("ecol",), tag=f"gnB{l}")
            Acol = lambda c: self.smv("ecol", F32, 1, byteoff=8 + c * 4)
            Bcol = lambda c: self.smv("ecol", F32, 1, byteoff=40 + c * 4)
            y_ap = lambda kc, c0, cols: self.S0.ap(F32, cols, byteoff=kc * 8192 + c0 * 4)
            y_k = lambda kc, c0, cols: self.S0.K(kc * 8192 + c0 * 4, cols * 4)
            for kc in range(NCH):
                for s in range(4):
                    P.op("vector", lambda e, kc=kc, s=s: e.tensor_scalar(
                        out=self.SC.ap(F32, 512, byteoff=2048),
                        in0=m2_ap(kc, s * 512, 512), scalar1=Acol(kc), scalar2=Bcol(kc),
                        op0=Alu.mult, op1=Alu.subtract),
                        r=m2_k(kc, s * 512, 512) + ("ecol",), w=self.SC.K(2048, 2048),
                        tag=f"y{l}a{kc}_{s}")
                    P.op("vector", lambda e, kc=kc, s=s: e.tensor_tensor(
                        out=y_ap(kc, s * 512, 512), in0=self.SC.ap(F32, 512, byteoff=2048),
                        in1=h_ap(kc, s * 512, 512), op=Alu.add),
                        r=self.SC.K(2048, 2048) + h_k(kc, s * 512, 512),
                        w=y_k(kc, s * 512, 512), tag=f"y{l}b{kc}_{s}")

            ybf = lambda kc, c0, cols: self.PB.ap(BF16, cols, byteoff=kc * 4096 + c0 * 2)
            ybf_k = lambda kc, c0, cols: self.PB.K(kc * 4096 + c0 * 2, cols * 2)
            for sp in range(2):
                for sl in range(2):
                    s = sp * 2 + sl
                    for kc in range(NCH):
                        P.op("vector", lambda e, kc=kc, s=s: e.tensor_copy(
                            out=ybf(kc, s * 512, 512), in_=y_ap(kc, s * 512, 512)),
                            r=y_k(kc, s * 512, 512), w=ybf_k(kc, s * 512, 512),
                            tag=f"ybf{l}_{kc}_{s}")
                for fo in range(16):
                    wT, wkeys = stream_cols(wf1_2d, fo, fo, f"f1{l}p{sp}_")
                    for sl in range(2):
                        s = sp * 2 + sl
                        bank = self.psb[2 * (fo % 2) + sl]
                        bk = f"psb/{2 * (fo % 2) + sl}"
                        for kc in range(NCH):
                            P.op("tensor", lambda e, fo=fo, s=s, kc=kc, bank=bank, wT=wT: e.matmul(
                                bank[:, :], wT(kc), ybf(kc, s * 512, 512),
                                start=(kc == 0), stop=(kc == NCH - 1)),
                                r=wkeys + ybf_k(kc, s * 512, 512), w=(bk,), tag=f"f1{l}mm{s}_{fo}")
                        P.op("scalar", lambda e, fo=fo, sl=sl, bank=bank: e.activation(
                            out=self.PA.ap(BF16, 512, byteoff=fo * 2048 + sl * 1024),
                            in_=bank[:, :],
                            func=Act.Gelu, bias=bcol(40 + fo), scale=1.0),
                            r=(bk, "biasT"), w=self.PA.K(fo * 2048 + sl * 1024, 1024),
                            tag=f"f1{l}ev{s}_{fo}")
                for oc in range(NCH):
                    banks = (self.ps_ab[0], self.ps_ab[1])
                    bks = ("ps_ab0", "ps_ab1")
                    for hhalf in range(2):
                        wT, wkeys = stream_cols(
                            wf2_2d[hhalf * 1024:(hhalf + 1) * 1024, :], oc,
                            2 * oc + hhalf, f"f2{l}p{sp}h{hhalf}_")
                        for kc in range(NCH):
                            kci = hhalf * 8 + kc
                            for sl in range(2):
                                P.op("tensor", lambda e, kci=kci, kc=kc, sl=sl, bank=banks[sl],
                                     wT=wT: e.matmul(
                                    bank[:, :], wT(kc),
                                    self.PA.ap(BF16, 512, byteoff=kci * 2048 + sl * 1024),
                                    start=(kci == 0), stop=(kci == 15)),
                                    r=wkeys + self.PA.K(kci * 2048 + sl * 1024, 1024),
                                    w=(bks[sl],), tag=f"f2{l}mm{sp}_{oc}_{kci}_{sl}")
                    for sl in range(2):
                        s = sp * 2 + sl
                        P.op("vector", lambda e, oc=oc, s=s, bank=banks[sl]: e.scalar_tensor_tensor(
                            out=y_ap(oc, s * 512, 512), in0=bank[:, :], scalar=bcol(56 + oc),
                            in1=y_ap(oc, s * 512, 512), op0=Alu.add, op1=Alu.add),
                            r=(bks[sl], "biasT") + y_k(oc, s * 512, 512),
                            w=y_k(oc, s * 512, 512), tag=f"f2{l}ev{s}_{oc}")
            self.ln_fm(f"iln{l}", self.S0, lambda c: c * 8192,
                       self.S1, lambda c: c * 8192, F32,
                       scratch=(self.SC, 0),
                       gamma=lambda c: self.smv("biasT", F32, 1, byteoff=(64 + c) * 4),
                       beta=lambda c: self.smv("biasT", F32, 1, byteoff=(72 + c) * 4),
                       gb_keys=("biasT",))

        if upto == "h":
            return
        for c in range(NCH):
            P.op("sync", lambda e, c=c: e.dma_start(
                out=self.S0.ap(F32, T, byteoff=c * 8192), in_=self.xspill_h.ap()[c]),
                w=self.S0.K(c * 8192, 8192), dma=True, tag=f"xrs{c}")
            for s in range(4):
                P.op("vector", lambda e, c=c, s=s: e.tensor_tensor(
                    out=self.S0.ap(F32, 512, byteoff=c * 8192 + s * 2048),
                    in0=self.S0.ap(F32, 512, byteoff=c * 8192 + s * 2048),
                    in1=self.S1.ap(F32, 512, byteoff=c * 8192 + s * 2048), op=Alu.add),
                    r=self.S0.K(c * 8192 + s * 2048, 2048) + self.S1.K(c * 8192 + s * 2048, 2048),
                    w=self.S0.K(c * 8192 + s * 2048, 2048), tag=f"xadd{c}_{s}")

    def quant_chain(self, x_ap, x_keys, rdq2_col, dqh_col, col_keys,
                    out_ap, out_keys, scr, scr_off, tag, ncols=1024):
        P = self.P
        nb = ncols // 16
        CM = float(3 * 2 ** 22)
        buf = lambda i: scr.ap(F32, ncols, byteoff=scr_off + i * 4096)
        bk = lambda i: scr.K(scr_off + i * 4096, 4096)
        kbuf = lambda i: scr.ap(BF16, ncols, byteoff=scr_off + 16384 + i * 2048)
        kk = lambda i: scr.K(scr_off + 16384 + i * 2048, 2048)
        g3 = lambda ap_: ap_.rearrange("p (g s) -> p g s", s=16)

        def bc3(col_ap):
            import concourse.bass as _b
            pd = list(col_ap.ap[0])
            return _b.AP(col_ap.tensor, col_ap.offset, [pd, [1, nb], [0, 16]])

        P.op("vector", lambda e: e.tensor_tensor(
            out=g3(buf(0)), in0=g3(x_ap), in1=bc3(rdq2_col), op=Alu.mult),
            r=x_keys + col_keys, w=bk(0), tag=f"{tag}u")
        P.op("scalar", lambda e: e.activation(out=buf(1), in_=buf(0), func=Act.Abs),
             r=bk(0), w=bk(1), tag=f"{tag}abs")
        P.op("vector", lambda e: e.tensor_scalar(
            out=buf(2), in0=buf(1), scalar1=4.0, scalar2=0.0, op0=Alu.subtract, op1=Alu.max),
            r=bk(1), w=bk(2), tag=f"{tag}r1")
        P.op("vector", lambda e: e.tensor_scalar(
            out=buf(3), in0=buf(1), scalar1=8.0, scalar2=0.0, op0=Alu.subtract, op1=Alu.max),
            r=bk(1), w=bk(3), tag=f"{tag}r2")
        P.op("vector", lambda e: e.scalar_tensor_tensor(
            out=buf(2), in0=buf(3), scalar=0.5, in1=buf(2), op0=Alu.mult, op1=Alu.add),
            r=bk(2) + bk(3), w=bk(2), tag=f"{tag}w2")
        P.op("vector", lambda e: e.scalar_tensor_tensor(
            out=buf(0), in0=buf(2), scalar=-0.5, in1=buf(1), op0=Alu.mult, op1=Alu.add),
            r=bk(1) + bk(2), w=bk(0), tag=f"{tag}g")
        P.op("vector", lambda e: e.tensor_scalar(
            out=buf(1), in0=buf(0), scalar1=CM, scalar2=CM, op0=Alu.add, op1=Alu.subtract),
            r=bk(0), w=bk(1), tag=f"{tag}k")
        P.op("vector", lambda e: e.tensor_scalar(
            out=kbuf(0), in0=buf(1), scalar1=7.0, scalar2=0.0, op0=Alu.min, op1=Alu.max),
            r=bk(1), w=kk(0), tag=f"{tag}kc")
        P.op("vector", lambda e: e.tensor_scalar(
            out=kbuf(1), in0=kbuf(0), scalar1=4.0, scalar2=0.0, op0=Alu.subtract, op1=Alu.max),
            r=kk(0), w=kk(1), tag=f"{tag}s1")
        P.op("vector", lambda e: e.tensor_scalar(
            out=kbuf(2), in0=kbuf(0), scalar1=6.0, scalar2=0.0, op0=Alu.subtract, op1=Alu.max),
            r=kk(0), w=kk(2), tag=f"{tag}s2")
        P.op("vector", lambda e: e.scalar_tensor_tensor(
            out=kbuf(1), in0=kbuf(2), scalar=2.0, in1=kbuf(1), op0=Alu.mult, op1=Alu.add),
            r=kk(1) + kk(2), w=kk(1), tag=f"{tag}t")
        P.op("gpsimd", lambda e: e.tensor_tensor(
            out=kbuf(0), in0=kbuf(1), in1=kbuf(0), op=Alu.add),
            r=kk(0) + kk(1), w=kk(0), tag=f"{tag}q2")
        P.op("scalar", lambda e: e.activation(out=kbuf(2), in_=x_ap, func=Act.Sign),
             r=x_keys, w=kk(2), tag=f"{tag}sg")
        P.op("vector", lambda e: e.tensor_tensor(
            out=g3(buf(2)), in0=g3(kbuf(0)), in1=bc3(dqh_col), op=Alu.mult),
            r=kk(0) + col_keys, w=bk(2), tag=f"{tag}vd")
        P.op("vector", lambda e: e.tensor_tensor(
            out=out_ap, in0=buf(2), in1=kbuf(2), op=Alu.mult),
            r=bk(2) + kk(2), w=out_keys, tag=f"{tag}out")

    def amax16(self, src_ap, src_keys, am_ap, am_keys, tag, ncols=1024):
        g3 = lambda ap_: ap_.rearrange("p (g s) -> p g s", s=16)
        self.P.op("vector", lambda e: e.tensor_reduce(
            out=am_ap.unsqueeze(2), in_=g3(src_ap), axis=AX.X, op=Alu.max,
            apply_absolute_value=True),
            r=src_keys, w=am_keys, tag=f"{tag}am")

    def scales_from_amax(self, am_ap, am_keys, gs_off, rdq2_col, dqh_col, col_keys,
                         scr, scr_off, tag, nb=64):
        P = self.P
        CM = float(3 * 2 ** 22)
        sc = scr.ap(F32, nb, byteoff=scr_off)
        sck = scr.K(scr_off, nb * 4)
        Sb = scr.ap(F32, nb, byteoff=scr_off + 256)
        Sbk = scr.K(scr_off + 256, nb * 4)
        mkb = scr.ap(U8, nb, byteoff=scr_off + 512)
        mkk = scr.K(scr_off + 512, nb)
        hb = scr.ap(F32, nb, byteoff=scr_off + 576)
        hbk = scr.K(scr_off + 576, nb * 4)
        f8 = scr.ap(FP8, nb, byteoff=scr_off + 832)
        f8k = scr.K(scr_off + 832, nb)
        s8 = scr.ap(F32, nb, byteoff=scr_off + 896)
        s8k = scr.K(scr_off + 896, nb * 4)
        gcol = lambda i: self.smv("ecol", F32, 1, byteoff=gs_off + i * 4)
        P.op("vector", lambda e: e.tensor_scalar(
            out=sc, in0=am_ap, scalar1=gcol(0), scalar2=448.0, op0=Alu.mult, op1=Alu.min),
            r=am_keys + ("ecol",), w=sck, tag=f"{tag}sc")
        P.op("vector", lambda e: e.tensor_single_scalar(
            out=mkb, in_=sc, scalar=0.03125, op=Alu.is_ge),
            r=sck, w=mkk, tag=f"{tag}mk")
        P.op("vector", lambda e: e.tensor_scalar(
            out=Sb, in0=sc, scalar1=512.0, scalar2=CM, op0=Alu.mult, op1=Alu.add),
            r=sck, w=Sbk, tag=f"{tag}S1")
        P.op("vector", lambda e: e.tensor_scalar(
            out=Sb, in0=Sb, scalar1=CM, scalar2=1.0 / 512.0, op0=Alu.subtract, op1=Alu.mult),
            r=Sbk, w=Sbk, tag=f"{tag}S2")
        P.op("vector", lambda e: e.tensor_scalar(
            out=hb, in0=sc, scalar1=0.5, scalar2=None, op0=Alu.mult),
            r=sck, w=hbk, tag=f"{tag}h")
        P.op("vector", lambda e: e.tensor_copy(out=f8, in_=hb), r=hbk, w=f8k, tag=f"{tag}f8")
        P.op("vector", lambda e: e.tensor_copy(out=s8, in_=f8), r=f8k, w=s8k, tag=f"{tag}f8b")
        P.op("vector", lambda e: e.tensor_scalar(
            out=s8, in0=s8, scalar1=2.0, scalar2=None, op0=Alu.mult),
            r=s8k, w=s8k, tag=f"{tag}N")
        P.op("vector", lambda e: e.select(out=hb, mask=mkb, on_true=s8, on_false=Sb, add_drain=True),
             r=s8k + mkk + Sbk, w=hbk, tag=f"{tag}sel")
        P.op("vector", lambda e: e.tensor_scalar(
            out=sc, in0=hb, scalar1=1e-30, scalar2=None, op0=Alu.max),
            r=hbk, w=sck, tag=f"{tag}mx")
        P.op("vector", lambda e: e.reciprocal(out=sc, in_=sc), r=sck, w=sck, tag=f"{tag}rc")
        P.op("vector", lambda e, rdq2_col=rdq2_col: e.tensor_scalar(
            out=rdq2_col, in0=sc, scalar1=gcol(1), scalar2=None, op0=Alu.mult),
            r=sck + ("ecol",), w=col_keys, tag=f"{tag}rdq")
        P.op("vector", lambda e, dqh_col=dqh_col: e.tensor_scalar(
            out=dqh_col, in0=hb, scalar1=gcol(2), scalar2=None, op0=Alu.mult),
            r=hbk + ("ecol",), w=col_keys, tag=f"{tag}dqh")

    def make_gs_cols(self, gs_src_ap, gs_src_keys, gs_off, tag):
        P = self.P
        t = lambda i: self.smv("srows", F32, 1, part=1, byteoff=32 + i * 4)
        P.op("vector", lambda e: e.tensor_scalar(
            out=t(0), in0=gs_src_ap, scalar1=1.0 / 6.0, scalar2=None, op0=Alu.mult),
            r=gs_src_keys, w=("srows",), tag=f"{tag}a")
        P.op("vector", lambda e: e.tensor_scalar(
            out=t(1), in0=gs_src_ap, scalar1=2.0, scalar2=None, op0=Alu.mult),
            r=gs_src_keys, w=("srows",), tag=f"{tag}b")
        P.op("vector", lambda e: e.reciprocal(out=t(2), in_=t(1)),
             r=("srows",), w=("srows",), tag=f"{tag}c")
        P.op("tensor", lambda e: e.matmul(
            self.ps_x[:, 0:3], self.smv("ones_1x128f", F32, 128, part=1),
            self.smv("srows", F32, 3, part=1, byteoff=32), start=True, stop=True),
            r=("srows", "ones_1x128f"), w=("ps_x",), tag=f"{tag}d")
        P.op("vector", lambda e: e.tensor_copy(
            out=self.smv("ecol", F32, 3, byteoff=gs_off), in_=self.ps_x[:, 0:3]),
            r=("ps_x",), w=("ecol",), tag=f"{tag}e")

    def stage_l1_ffn(self, upto='all'):
        P = self.P
        for nm, off in (("f1_sin", 64), ("f1_sind", 76)):
            P.op("sync", lambda e, nm=nm: e.dma_start(
                out=self.smv("srows", F32, 1, part=1, byteoff=28), in_=self.inp[nm].ap()),
                w=("srows",), dma=True, tag=f"ld{nm}")
            self.make_gs_cols(self.smv("srows", F32, 1, part=1, byteoff=28),
                              ("srows",), off, f"gs{nm}")

        def quant_weight(hbm, n_o, n_i, outT_h, wtag):
            nsl = n_o // 128
            nsub = n_i // 1024
            gmax = self.smv("ecol", F32, 1, byteoff=108)
            smax = self.smv("ecol", F32, 1, byteoff=104)
            for sl in range(nsl):
                wf = self.WP.ap(F32, n_i, byteoff=(sl % 2) * 4096) if n_i <= 1024 else None
                if n_i <= 1024:
                    kf = self.WP.K((sl % 2) * 4096, n_i * 4)
                    P.op("sync", lambda e, sl=sl, wf=wf, hbm=hbm: e.dma_start(
                        out=wf, in_=hbm.ap()[sl * 128:(sl + 1) * 128, :]),
                        w=kf, dma=True, tag=f"{wtag}Adma{sl}")
                    P.op("vector", lambda e, wf=wf: e.tensor_reduce(
                        out=smax, in_=wf, axis=AX.X, op=Alu.max,
                        apply_absolute_value=True),
                        r=kf, w=("ecol",), tag=f"{wtag}Ared{sl}")
                    ops = [0]
                else:
                    for sub in range(nsub):
                        wfs = self.WP.ap(F32, 1024, byteoff=(sub % 2) * 4096)
                        kfs = self.WP.K((sub % 2) * 4096, 4096)
                        P.op("sync", lambda e, sl=sl, sub=sub, wfs=wfs, hbm=hbm: e.dma_start(
                            out=wfs, in_=hbm.ap()[sl * 128:(sl + 1) * 128,
                                                  sub * 1024:(sub + 1) * 1024]),
                            w=kfs, dma=True, tag=f"{wtag}Adma{sl}_{sub}")
                        if sub == 0:
                            P.op("vector", lambda e, wfs=wfs: e.tensor_reduce(
                                out=smax, in_=wfs, axis=AX.X, op=Alu.max,
                                apply_absolute_value=True),
                                r=kfs, w=("ecol",), tag=f"{wtag}Ared{sl}_{sub}")
                        else:
                            P.op("vector", lambda e, wfs=wfs: e.tensor_reduce(
                                out=self.smv("ecol", F32, 1, byteoff=100), in_=wfs,
                                axis=AX.X, op=Alu.max, apply_absolute_value=True),
                                r=kfs, w=("ecol",), tag=f"{wtag}Ared{sl}_{sub}")
                            P.op("vector", lambda e: e.tensor_tensor(
                                out=smax, in0=smax,
                                in1=self.smv("ecol", F32, 1, byteoff=100), op=Alu.max),
                                r=("ecol",), w=("ecol",), tag=f"{wtag}Am{sl}_{sub}")
                if sl == 0:
                    P.op("vector", lambda e: e.tensor_copy(out=gmax, in_=smax),
                         r=("ecol",), w=("ecol",), tag=f"{wtag}Ainit")
                else:
                    P.op("vector", lambda e: e.tensor_tensor(
                        out=gmax, in0=gmax, in1=smax, op=Alu.max),
                        r=("ecol",), w=("ecol",), tag=f"{wtag}Aacc{sl}")
            P.op("tensor", lambda e: e.transpose(
                self.ps_x[0:1, 0:128], gmax, self.smv("id128f", F32, 128)),
                r=("ecol", "id128f"), w=("ps_x",), tag=f"{wtag}Atr")
            gsr = self.smv("srows", F32, 1, part=1, byteoff=28)
            P.op("vector", lambda e, gsr=gsr: e.tensor_reduce(
                out=gsr, in_=self.ps_x[0:1, 0:128], axis=AX.X, op=Alu.max),
                r=("ps_x",), w=("srows",), tag=f"{wtag}Ag")
            P.op("vector", lambda e, gsr=gsr: e.reciprocal(out=gsr, in_=gsr),
                 r=("srows",), w=("srows",), tag=f"{wtag}Arc")
            P.op("vector", lambda e, gsr=gsr: e.tensor_scalar(
                out=gsr, in0=gsr, scalar1=2688.0, scalar2=None, op0=Alu.mult),
                r=("srows",), w=("srows",), tag=f"{wtag}Ascale")
            self.make_gs_cols(gsr, ("srows",), 88, f"{wtag}gsc")
            for sl in range(nsl):
                for sub in range(nsub):
                    wfs = self.WP.ap(F32, 1024, byteoff=(sub % 2) * 4096)
                    kfs = self.WP.K((sub % 2) * 4096, 4096)
                    P.op("sync", lambda e, sl=sl, sub=sub, wfs=wfs, hbm=hbm: e.dma_start(
                        out=wfs, in_=hbm.ap()[sl * 128:(sl + 1) * 128,
                                              sub * 1024:(sub + 1) * 1024]),
                        w=kfs, dma=True, tag=f"{wtag}Bdma{sl}_{sub}")
                    am = self.S1.ap(F32, 64, byteoff=28672)
                    amk = self.S1.K(28672, 256)
                    self.amax16(wfs, kfs, am, amk, f"{wtag}B{sl}_{sub}")
                    rq = self.S1.ap(F32, 64, byteoff=29696)
                    dq = self.S1.ap(F32, 64, byteoff=29952)
                    ck = self.S1.K(29696, 512)
                    self.scales_from_amax(am, amk, 88, rq, dq, ck, self.S1, 24576,
                                          f"{wtag}S{sl}_{sub}")
                    qout = self.S1.ap(BF16, 1024, byteoff=45056 + sub * 2048)
                    qok = self.S1.K(45056 + sub * 2048, 2048)
                    self.quant_chain(wfs, kfs, rq, dq, ck, qout, qok,
                                     self.S1, 0, f"{wtag}Q{sl}_{sub}")
                for ic in range(n_i // 128):
                    trb, trk = self.trbanks[ic % 4]
                    P.op("tensor", lambda e, ic=ic, trb=trb: e.transpose(
                        trb[:, 0:512].bitcast(BF16)[:, 0:128],
                        self.S1.ap(BF16, 128, byteoff=45056 + ic * 256),
                        self.smv("id128b", BF16, 128)),
                        r=self.S1.K(45056 + ic * 256, 256) + ("id128b",),
                        w=(trk,), tag=f"{wtag}tr{sl}_{ic}")
                    P.op("vector", lambda e, ic=ic, trb=trb: e.tensor_copy(
                        out=self.PA.ap(BF16, 128, byteoff=ic * 256),
                        in_=trb[:, 0:512].bitcast(BF16)[:, 0:128]),
                        r=(trk,), w=self.PA.K(ic * 256, 256), tag=f"{wtag}te{sl}_{ic}")
                P.op("sync", lambda e, sl=sl, outT_h=outT_h, n_i=n_i: e.dma_start(
                    out=outT_h.ap()[:, sl * 128:(sl + 1) * 128].rearrange(
                        "(ic p) m -> p ic m", p=128),
                    in_=self.PA.ap(BF16, n_i, byteoff=0).rearrange(
                        "p (ic m) -> p ic m", m=128)),
                    r=self.PA.K(0, n_i * 2), dma=True, tag=f"{wtag}st{sl}")

        if upto != "nowq":
            quant_weight(self.inp["f1_gw"], 4096, D, self.gwT_h, "qg")
            quant_weight(self.inp["f1_uw"], 4096, D, self.uwT_h, "qu")
            quant_weight(self.inp["f1_dw"], D, 4096, self.dwT_h, "qd")
        if upto == "wq":
            return

        self.ln_fm("ln3", self.S0, lambda c: c * 8192, self.S1, lambda c: c * 8192,
                   F32, scratch=(self.PB, 0))
        l3t = lambda tt, c0, cols: (self.PA if tt < 8 else self.PB).ap(
            F32, cols, byteoff=(tt % 8) * 4096 + c0 * 4)
        l3t_k = lambda tt, c0, cols: (self.PA if tt < 8 else self.PB).K(
            (tt % 8) * 4096 + c0 * 4, cols * 4)
        for tt in range(NTT):
            for c in range(NCH):
                trb, trk = self.trbanks[(tt * NCH + c) % 4]
                P.op("tensor", lambda e, tt=tt, c=c, trb=trb: e.transpose(
                    trb[:, 0:128], self.S1.ap(F32, 128, byteoff=c * 8192 + tt * 512),
                    self.smv("id128f", F32, 128)),
                    r=self.S1.K(c * 8192 + tt * 512, 512) + ("id128f",),
                    w=(trk,), tag=f"l3t{tt}_{c}")
                P.op("vector", lambda e, tt=tt, c=c, trb=trb: e.tensor_copy(
                    out=l3t(tt, c * 128, 128), in_=trb[:, 0:128]),
                    r=(trk,), w=l3t_k(tt, c * 128, 128), tag=f"l3e{tt}_{c}")
        xqfm = lambda kc, c0, cols: self.S1.ap(BF16, cols, byteoff=kc * 4096 + c0 * 2)
        xqfm_k = lambda kc, c0, cols: self.S1.K(kc * 4096 + c0 * 2, cols * 2)
        for tt in range(NTT):
            am = self.S1.ap(F32, 64, byteoff=55296)
            amk = self.S1.K(55296, 256)
            self.amax16(l3t(tt, 0, 1024), l3t_k(tt, 0, 1024), am, amk, f"xq{tt}")
            rq = self.S1.ap(F32, 64, byteoff=63488)
            dq = self.S1.ap(F32, 64, byteoff=63744)
            ck = self.S1.K(63488, 512)
            self.scales_from_amax(am, amk, 64, rq, dq, ck, self.S1, 56320, f"xqs{tt}")
            xqt = self.S1.ap(BF16, 1024, byteoff=59392 + (tt % 2) * 2048)
            xqt_k = self.S1.K(59392 + (tt % 2) * 2048, 2048)
            self.quant_chain(l3t(tt, 0, 1024), l3t_k(tt, 0, 1024), rq, dq, ck,
                             xqt, xqt_k, self.S1, 32768, f"xqc{tt}")
            for c in range(NCH):
                trb, trk = self.trbanks[c % 4]
                P.op("tensor", lambda e, tt=tt, c=c, xqt=xqt, trb=trb: e.transpose(
                    trb[:, 0:512].bitcast(BF16)[:, 0:128],
                    xqt[:, c * 128:(c + 1) * 128], self.smv("id128b", BF16, 128)),
                    r=xqt_k + ("id128b",), w=(trk,), tag=f"xqt{tt}_{c}")
                P.op("vector", lambda e, tt=tt, c=c, trb=trb: e.tensor_copy(
                    out=xqfm(c, tt * 128, 128), in_=trb[:, 0:512].bitcast(BF16)[:, 0:128]),
                    r=(trk,), w=xqfm_k(c, tt * 128, 128), tag=f"xqe{tt}_{c}")

        if upto in ("xq", "nowq_xq"):
            return
        hid_bf = lambda tl, c0, cols: self.PB.ap(BF16, cols, byteoff=tl * 8192 + c0 * 2)
        hid_bf_k = lambda tl, c0, cols: self.PB.K(tl * 8192 + c0 * 2, cols * 2)
        hqfm = lambda hc, tl, c0, cols: self.S1.ap(BF16, cols, byteoff=32768 + hc * 1024 + tl * 256 + c0 * 2)
        hqfm_k = lambda hc, tl, c0, cols: self.S1.K(32768 + hc * 1024 + tl * 256 + c0 * 2, cols * 2)
        for st in range(1 if upto == "strip0" else 4):
            for hs in range(8):
                for w_i, (wh, wnm) in enumerate(((self.gwT_h, "g"), (self.uwT_h, "u"))):
                    wbuf = self.WP if (hs * 2 + w_i) % 2 == 0 else self.SC
                    for hf in range(2):
                        P.op("sync", lambda e, hs=hs, hf=hf, wh=wh, wbuf=wbuf: e.dma_start(
                            out=wbuf.ap(BF16, 2048, byteoff=hf * 4096).rearrange(
                                "p (ic m) -> p ic m", ic=4),
                            in_=wh.ap()[hf * 512:(hf + 1) * 512,
                                        hs * 512:(hs + 1) * 512].rearrange(
                                "(ic p) m -> p ic m", p=128)),
                            w=wbuf.K(hf * 4096, 4096), dma=True,
                            tag=f"w{wnm}{st}_{hs}_{hf}")
                    for tl in range(4):
                        tt = st * 4 + tl
                        bank = self.psb[tl]
                        bk = f"psb/{tl}"
                        for ic in range(NCH):
                            hf = ic // 4
                            icl = ic % 4
                            P.op("tensor", lambda e, tt=tt, hs=hs, ic=ic, hf=hf, icl=icl,
                                 bank=bank, wbuf=wbuf: e.matmul(
                                 bank[:, :], xqfm(ic, tt * 128, 128),
                                 wbuf.ap(BF16, 512, byteoff=hf * 4096 + icl * 1024),
                                 start=(ic == 0), stop=(ic == NCH - 1)),
                                 r=xqfm_k(ic, tt * 128, 128) + wbuf.K(hf * 4096 + icl * 1024, 1024),
                                 w=(bk,), tag=f"{wnm}mm{st}_{hs}_{tl}")
                        if w_i == 0:
                            P.op("scalar", lambda e, tl=tl, bank=bank: e.activation(
                                out=self.PA.ap(F32, 512, byteoff=tl * 2048),
                                in_=bank[:, :], func=Act.Silu),
                                r=(bk,), w=self.PA.K(tl * 2048, 2048),
                                tag=f"sil{st}_{hs}_{tl}")
                        else:
                            P.op("vector", lambda e, tl=tl, bank=bank: e.tensor_tensor(
                                out=self.PA.ap(F32, 512, byteoff=8192),
                                in0=bank[:, :], in1=self.PA.ap(F32, 512, byteoff=tl * 2048),
                                op=Alu.mult),
                                r=(bk,) + self.PA.K(tl * 2048, 2048),
                                w=self.PA.K(8192, 2048), tag=f"hid{st}_{hs}_{tl}")
                            P.op("vector", lambda e, tl=tl, hs=hs: e.tensor_reduce(
                                out=self.PA.ap(F32, 32, byteoff=24576 + tl * 1024 + hs * 128).unsqueeze(2),
                                in_=self.PA.ap(F32, 512, byteoff=8192).rearrange(
                                    "p (g s) -> p g s", s=16),
                                axis=AX.X, op=Alu.max, apply_absolute_value=True),
                                r=self.PA.K(8192, 2048),
                                w=self.PA.K(24576 + tl * 1024 + hs * 128, 128),
                                tag=f"ham{st}_{hs}_{tl}")
                            P.op("vector", lambda e, tl=tl, hs=hs: e.tensor_copy(
                                out=hid_bf(tl, hs * 512, 512),
                                in_=self.PA.ap(F32, 512, byteoff=8192)),
                                r=self.PA.K(8192, 2048), w=hid_bf_k(tl, hs * 512, 512),
                                tag=f"hbf{st}_{hs}_{tl}")
            for tl in range(4):
                for sub in range(4):
                    am = self.PA.ap(F32, 64, byteoff=24576 + tl * 1024 + sub * 256)
                    amk = self.PA.K(24576 + tl * 1024 + sub * 256, 256)
                    rq = self.PA.ap(F32, 64, byteoff=28928)
                    dq = self.PA.ap(F32, 64, byteoff=29184)
                    ck = self.PA.K(28928, 512)
                    self.scales_from_amax(am, amk, 76, rq, dq, ck, self.PA, 22528,
                                          f"hs{st}_{tl}_{sub}")
                    self.quant_chain(hid_bf(tl, sub * 1024, 1024),
                                     hid_bf_k(tl, sub * 1024, 1024), rq, dq, ck,
                                     hid_bf(tl, sub * 1024, 1024),
                                     hid_bf_k(tl, sub * 1024, 1024),
                                     self.PA, 0, f"hq{st}_{tl}_{sub}")
                for hc in range(32):
                    trb, trk = self.trbanks[hc % 4]
                    P.op("tensor", lambda e, tl=tl, hc=hc, trb=trb: e.transpose(
                        trb[:, 0:512].bitcast(BF16)[:, 0:128],
                        hid_bf(tl, hc * 128, 128), self.smv("id128b", BF16, 128)),
                        r=hid_bf_k(tl, hc * 128, 128) + ("id128b",), w=(trk,),
                        tag=f"hqt{st}_{tl}_{hc}")
                    P.op("vector", lambda e, tl=tl, hc=hc, trb=trb: e.tensor_copy(
                        out=hqfm(hc, tl, 0, 128),
                        in_=trb[:, 0:512].bitcast(BF16)[:, 0:128]),
                        r=(trk,), w=hqfm_k(hc, tl, 0, 128), tag=f"hqe{st}_{tl}_{hc}")
            if upto == "strip0":
                return
            for dc in range(NCH):
                wbuf = self.SC if dc % 2 == 0 else self.WP
                for hf in range(2):
                    P.op("sync", lambda e, dc=dc, hf=hf, wbuf=wbuf: e.dma_start(
                        out=wbuf.ap(BF16, 2048, byteoff=hf * 4096).rearrange(
                            "p (ic m) -> p ic m", ic=16),
                        in_=self.dwT_h.ap()[hf * 2048:(hf + 1) * 2048,
                                            dc * 128:(dc + 1) * 128].rearrange(
                            "(ic p) m -> p ic m", p=128)),
                        w=wbuf.K(hf * 4096, 4096), dma=True, tag=f"dw{st}_{dc}_{hf}")
                bank = self.ps_ab[dc % 2]
                bk = f"ps_ab{dc % 2}"
                for hc in range(32):
                    hf = hc // 16
                    hcl = hc % 16
                    P.op("tensor", lambda e, st=st, dc=dc, hc=hc, hf=hf, hcl=hcl,
                         bank=bank, wbuf=wbuf: e.matmul(
                         bank[:, :], wbuf.ap(BF16, 128, byteoff=hf * 4096 + hcl * 256),
                         hqfm(hc, 0, 0, 512), start=(hc == 0), stop=(hc == 31)),
                         r=wbuf.K(hf * 4096 + hcl * 256, 256) + self.S1.K(32768 + hc * 1024, 1024),
                         w=(bk,), tag=f"dmm{st}_{dc}_{hc}")
                P.op("vector", lambda e, st=st, dc=dc, bank=bank: e.tensor_tensor(
                    out=self.S0.ap(F32, 512, byteoff=dc * 8192 + st * 2048),
                    in0=bank[:, :], in1=self.S0.ap(F32, 512, byteoff=dc * 8192 + st * 2048),
                    op=Alu.add),
                    r=(bk,) + self.S0.K(dc * 8192 + st * 2048, 2048),
                    w=self.S0.K(dc * 8192 + st * 2048, 2048), tag=f"dev{st}_{dc}")

    def debug_out_fm(self, region, chunk_off, dtype=F32, scale=1.0):
        P = self.P
        dsz = {F32: 4, BF16: 2}[dtype]
        for tt in range(NTT):
            for c in range(NCH):
                rk = region.K(chunk_off(c) + tt * 128 * dsz, 128 * dsz)
                if dtype == F32:
                    P.op("tensor", lambda e, tt=tt, c=c: e.transpose(
                        self.ps_x[:, 0:128],
                        region.ap(F32, 128, byteoff=chunk_off(c) + tt * 512),
                        self.smv("id128f", F32, 128)),
                        r=rk + ("id128f",), w=("ps_x",), tag=f"dbgt{tt}_{c}")
                    src2 = self.ps_x[:, 0:128]
                else:
                    P.op("tensor", lambda e, tt=tt, c=c: e.transpose(
                        self.ps_x[:, 0:512].bitcast(BF16)[:, 0:128],
                        region.ap(BF16, 128, byteoff=chunk_off(c) + tt * 256),
                        self.smv("id128b", BF16, 128)),
                        r=rk + ("id128b",), w=("ps_x",), tag=f"dbgt{tt}_{c}")
                    src2 = self.ps_x[:, 0:512].bitcast(BF16)[:, 0:128]
                P.op("vector", lambda e, tt=tt, c=c, src2=src2: e.tensor_scalar(
                    out=self.WP.ap(F32, 128, byteoff=c * 512), in0=src2,
                    scalar1=float(scale), scalar2=None, op0=Alu.mult),
                    r=("ps_x",), w=self.WP.K(c * 512, 512), tag=f"dbge{tt}_{c}")
            tok = P.op("sync", lambda e, tt=tt: e.dma_start(
                out=self.y_h.ap()[tt * 128:(tt + 1) * 128, :],
                in_=self.WP.ap(F32, D, byteoff=0)),
                r=self.WP.K(0, 4096), dma=True, tag=f"dbgs{tt}")
            P.final_waits.add(tok)

    def debug_out_tok(self, region, tile_off, dtype=BF16):
        P = self.P
        dsz = {F32: 4, BF16: 2}[dtype]
        for tt in range(NTT):
            P.op("vector", lambda e, tt=tt: e.tensor_copy(
                out=self.WP.ap(F32, D, byteoff=0),
                in_=region.ap(dtype, D, byteoff=tile_off(tt))),
                r=region.K(tile_off(tt), D * dsz), w=self.WP.K(0, 4096), tag=f"dtk{tt}")
            tok = P.op("sync", lambda e, tt=tt: e.dma_start(
                out=self.y_h.ap()[tt * 128:(tt + 1) * 128, :],
                in_=self.WP.ap(F32, D, byteoff=0)),
                r=self.WP.K(0, 4096), dma=True, tag=f"dtks{tt}")
            P.final_waits.add(tok)

    def finalize(self, listing=None):
        return self.P.finalize(listing=listing)


def in_map_for_core(ins, core):
    return {
        "x": np.ascontiguousarray(ins["L0_attn_x"][core]),
        "l0_wq": ins["L0_attn_Wq"], "l0_wk": ins["L0_attn_Wk"],
        "l0_wv": ins["L0_attn_Wv"], "l0_wo": ins["L0_attn_Wo"],
        "l0_wa": ins["L0_attn_Wa"], "l0_ba": np.ascontiguousarray(ins["L0_attn_ba"].reshape(1, 16)),
        "l0f_wg": ins["L0_ffn_w_gate"], "l0f_wu": ins["L0_ffn_w_up"],
        "l0f_wd": ins["L0_ffn_w_down"],
        "l1_wkqv": ins["L1_attn_Wkqv"], "l1_wm1": ins["L1_attn_Wm1"],
        "l1_bm1": ins["L1_attn_bm1"], "l1_wm2": ins["L1_attn_Wm2"],
        "l1_bm2": ins["L1_attn_bm2"], "l1_gng": ins["L1_attn_gn_g"],
        "l1_gnb": ins["L1_attn_gn_b"], "l1_wf1": ins["L1_attn_Wf1"],
        "l1_bf1": ins["L1_attn_bf1"], "l1_wf2": ins["L1_attn_Wf2"],
        "l1_bf2": ins["L1_attn_bf2"], "l1_lng": ins["L1_attn_ln_g"],
        "l1_lnb": ins["L1_attn_ln_b"],
        "f1_gw": ins["L1_ffn_gate_w"], "f1_uw": ins["L1_ffn_up_w"],
        "f1_dw": ins["L1_ffn_down_w"],
        "f1_sin": np.ascontiguousarray(ins["L1_ffn_s_in"].reshape(1, 1)),
        "f1_sind": np.ascontiguousarray(ins["L1_ffn_s_in_down"].reshape(1, 1)),
    }


def _build():
    m = MK()
    m.stage_load_x()
    m.ln_fm("ln0", m.S0, lambda c: c * 8192, m.S1, lambda c: c * 4096, BF16,
            scratch=(m.S1, 32768 + 8192))
    m.stage_l0_attn()
    m.ln_fm("ln1", m.S0, lambda c: c * 8192, m.S1, lambda c: c * 4096, BF16,
            scratch=(m.S1, 32768 + 8192))
    m.stage_l0_ffn()
    m.ln_fm("ln2", m.S0, lambda c: c * 8192, m.S1, lambda c: c * 8192, F32,
            scratch=(m.PB, 0))
    m.stage_l1_attn()
    m.stage_l1_ffn()
    m.debug_out_fm(m.S0, lambda c: c * 8192, dtype=F32)
    m.finalize()
    return m


def kernel(**inputs):
    from concourse.bass_utils import run_bass_kernel_spmd
    m = _build()
    in_maps = [in_map_for_core(inputs, core) for core in range(8)]
    res = run_bass_kernel_spmd(m.nc, in_maps, core_ids=list(range(8)))
    out = np.stack([np.asarray(res.results[c]["y"]) for c in range(8)], axis=0)
    return out.astype(np.float32)
